# version 03
# baseline (speedup 1.0000x reference)
import math
from contextlib import ExitStack

import numpy as np
import ml_dtypes

import concourse.bass as bass
import concourse.mybir as mybir
from concourse.bass_utils import run_bass_kernel_spmd

F32 = mybir.dt.float32
BF16 = mybir.dt.bfloat16
AF = mybir.ActivationFunctionType
ALU = mybir.AluOpType
AX = mybir.AxisListType

D = 4096
S = 2048
B = 4
NCORE = 8
TOWN = 1024
NH = 32
DH = 128
HR = 128
NEXP = 8
VIS_LO, VIS_HI = 1, 513
TEXT_LO, TEXT_HI = 513, 1992
NTEXT = TEXT_HI - TEXT_LO
NVIS = VIS_HI - VIS_LO
KEEP = 256
EPS = 1e-5
ROPE_THETA = 10000.0
GRAM_SCALE = 1.0 / math.sqrt(2048.0)
ATT_SCALE = 1.0 / math.sqrt(128.0)
FFN_FACTOR = 16.0
NEG = -1.0e30
DCH = D // 128
GRAM_LBS = DCH
GRAM_SUB = 3



def _pi_blocks(h):
    return [2 * i + h for i in range(8)] + [2 * i + (1 - h) for i in range(8)]


def _pi_perm(h):
    return np.concatenate([np.arange(128) + 128 * g for g in _pi_blocks(h)])


def _rope_cos_sin(positions):
    inv = 1.0 / (ROPE_THETA ** (np.arange(0, DH, 2, dtype=np.float64) / DH))
    ang = inv[:, None] * np.asarray(positions, dtype=np.float64)[None, :]
    ang = np.concatenate([ang, ang], axis=0)
    return (np.cos(ang).astype(np.float32), np.sin(ang).astype(np.float32))


def _prep_wT(w):
    a = np.ascontiguousarray(np.asarray(w, dtype=np.float32).T)
    a = a.reshape(DCH, 128, DCH, 128)
    a = np.ascontiguousarray(a.transpose(2, 1, 0, 3))
    return np.ascontiguousarray(a.reshape(DCH, 128, D))



class MB:

    def __init__(self, p):
        self.p = p

    def __enter__(self):
        self.blk = self.p.nc.Block()
        self.b = self.blk.__enter__()
        return self.b

    def __exit__(self, *a):
        r = self.blk.__exit__(*a)
        if a[0] is None:
            self.p.nc.all_engine_barrier()
        return r


class Prog:
    def __init__(self, debug=()):
        self.nc = bass.Bass("TRN2", target_bir_lowering=False)
        self.debug = set(debug)
        self.st = ExitStack()
        self.sem = {}
        self.cnt = {}
        self.c = {}

    def semaphore(self, name):
        self.sem[name] = self.st.enter_context(self.nc.semaphore(name))
        self.cnt[name] = 0

    def sbuf(self, name, shape, dtype=F32):
        t = self.st.enter_context(self.nc.sbuf_tensor(name, shape, dtype))
        self.c[name] = t
        return t

    def dbg(self, name, shape, dtype=F32):
        if name in self.debug:
            return self.nc.dram_tensor(name, shape, dtype, kind="ExternalOutput")
        return None

    def edge(self, eng, instr, sem):
        self.cnt[sem] += 1
        instr.then_inc(self.sem[sem], 1)
        eng.wait_ge(self.sem[sem], self.cnt[sem])


def _store(p, sy_or_g, dst_ap, src_ap):
    return sy_or_g.dma_start(out=dst_ap, in_=src_ap)


def _copy_dram(p, dst_ap, src_ap):
    nc = p.nc
    with MB(p) as b:
        @b.sync
        def _(sy):
            sy.dma_start(out=dst_ap, in_=src_ap).then_inc(p.sem["stq"], 16)
            p.cnt["stq"] += 16
            sy.wait_ge(p.sem["stq"], p.cnt["stq"])



def _p1_lnnat(p, xnat5, lnnat_s):
    nc = p.nc
    xa = xnat5.ap()
    la = lnnat_s.ap()
    NT = 5 * (S // 128)
    with ExitStack() as st:
        xin = [st.enter_context(nc.sbuf_tensor(f"p1xin{i}", [128, D], F32)) for i in range(2)]
        lno = [st.enter_context(nc.sbuf_tensor(f"p1lno{i}", [128, D], BF16)) for i in range(2)]
        junkv = st.enter_context(nc.sbuf_tensor("p1junkv", [128, D], BF16))
        junks = st.enter_context(nc.sbuf_tensor("p1junks", [128, D], BF16))
        s1 = st.enter_context(nc.sbuf_tensor("p1s1", [128, NT], F32))
        s2 = st.enter_context(nc.sbuf_tensor("p1s2", [128, NT], F32))
        negm = st.enter_context(nc.sbuf_tensor("p1negm", [128, NT], F32))
        rstd = st.enter_context(nc.sbuf_tensor("p1rstd", [128, NT], F32))
        var = st.enter_context(nc.sbuf_tensor("p1var", [128, NT], F32))
        m2 = st.enter_context(nc.sbuf_tensor("p1m2", [128, NT], F32))

        L = [p.sem["L0"], p.sem["L1"]]
        Sm = [p.sem["S0"], p.sem["S1"]]
        l0 = [p.cnt["L0"], p.cnt["L1"]]
        s0 = [p.cnt["S0"], p.cnt["S1"]]
        ve0 = p.cnt["ve"]
        se0 = p.cnt["se"]

        with MB(p) as b:
            @b.sync
            def _(sy):
                for i in range(NT):
                    s, t = divmod(i, S // 128)
                    if i >= 2:
                        sy.wait_ge(p.sem["ve"], ve0 + 3 * (i - 2) + 3)
                        sy.wait_ge(p.sem["se"], se0 + 2 * (i - 2) + 1)
                    sy.dma_start(out=xin[i % 2][:], in_=xa[s, 128 * t:128 * (t + 1), :]) \
                        .then_inc(L[i % 2], 16)
                    p.cnt[f"L{i % 2}"] += 16

            @b.vector
            def _(v):
                for i in range(NT):
                    x = xin[i % 2]
                    ci = slice(i, i + 1)
                    v.wait_ge(L[i % 2], l0[i % 2] + 16 * (i // 2 + 1))
                    v.tensor_scalar(out=junkv[:], in0=x[:], scalar1=1.0, scalar2=0.0,
                                    op0=ALU.mult, op1=ALU.add, accum_out=s1[:, ci]) \
                        .then_inc(p.sem["ve"], 1)
                    v.wait_ge(p.sem["se"], se0 + 2 * i + 1)
                    v.tensor_sub(var[:, ci], var[:, ci], m2[:, ci]).then_inc(p.sem["ve"], 1)
                    v.wait_ge(p.sem["se"], se0 + 2 * i + 2)
                    if i >= 2:
                        v.wait_ge(Sm[i % 2], s0[i % 2] + 16 * ((i - 2) // 2 + 1))
                    v.tensor_scalar(out=lno[i % 2][:], in0=x[:], scalar1=negm[:, ci],
                                    scalar2=rstd[:, ci], op0=ALU.add, op1=ALU.mult) \
                        .then_inc(p.sem["ve"], 1)
                p.cnt["ve"] += 3 * NT

            @b.scalar
            def _(sc):
                for i in range(NT):
                    x = xin[i % 2]
                    ci = slice(i, i + 1)
                    sc.wait_ge(L[i % 2], l0[i % 2] + 16 * (i // 2 + 1))
                    sc.activation(out=junks[:], in_=x[:], func=AF.Square, accum_out=s2[:, ci])
                    sc.wait_ge(p.sem["ve"], ve0 + 3 * i + 1)
                    p.edge(sc, sc.mul(negm[:, ci], s1[:, ci], -1.0 / D), "sx")
                    sc.activation(out=m2[:, ci], in_=negm[:, ci], func=AF.Square)
                    sc.mul(var[:, ci], s2[:, ci], 1.0 / D).then_inc(p.sem["se"], 1)
                    sc.wait_ge(p.sem["ve"], ve0 + 3 * i + 2)
                    p.edge(sc, sc.activation(out=var[:, ci], in_=var[:, ci], func=AF.Ln,
                                             bias=p.c["epscol"][:, 0:1]), "sx")
                    sc.activation(out=rstd[:, ci], in_=var[:, ci], func=AF.Exp, scale=-0.5) \
                        .then_inc(p.sem["se"], 1)
                p.cnt["se"] += 2 * NT

            @b.gpsimd
            def _(g):
                for i in range(NT):
                    s, t = divmod(i, S // 128)
                    g.wait_ge(p.sem["ve"], ve0 + 3 * i + 3)
                    g.dma_start(out=la[s, 128 * t:128 * (t + 1), :], in_=lno[i % 2][:]) \
                        .then_inc(Sm[i % 2], 16)
                    p.cnt[f"S{i % 2}"] += 16
                g.wait_ge(Sm[0], p.cnt["S0"])
                g.wait_ge(Sm[1], p.cnt["S1"])



def _p1_gram_msg(p, lnnat_s, xcv2, x1_s, dbg_prob):
    nc = p.nc
    la = lnnat_s.ap()
    xca = xcv2.ap()
    x1a = x1_s.ap()
    NTC = S // 128
    PAIRS = [(0, 1), (0, 2), (0, 3), (0, 4), (1, 1), (1, 2), (1, 3), (1, 4)]
    id_b, id_f = p.c["id_b"], p.c["id_f"]

    with ExitStack() as st:
        gsl = [st.enter_context(nc.sbuf_tensor(f"g{s}", [128, S], BF16)) for s in range(5)]
        lnc = [st.enter_context(nc.sbuf_tensor(f"lc{m}", [128, S], BF16)) for m in range(4)]
        lnc0 = st.enter_context(nc.sbuf_tensor("lc_own", [128, S], BF16))
        dall = st.enter_context(nc.sbuf_tensor("dall", [128, 8], F32))
        prob = st.enter_context(nc.sbuf_tensor("prob", [128, 8], F32))
        e4t = st.enter_context(nc.sbuf_tensor("e4t", [128, 8], F32))
        nmx = st.enter_context(nc.sbuf_tensor("nmx", [128, 2], F32))
        sme = st.enter_context(nc.sbuf_tensor("sme", [128, 2], F32))
        rcp = st.enter_context(nc.sbuf_tensor("rcp", [128, 2], F32))
        dmu = [[st.enter_context(nc.sbuf_tensor(f"dm{u}{m}", [128, 128], BF16))
                for m in range(4)] for u in range(2)]
        jnk = st.enter_context(nc.sbuf_tensor("jnkg", [128, 128], F32))
        xcv = [st.enter_context(nc.sbuf_tensor(f"xc{i}", [128, 512], F32)) for i in range(2)]
        x1o = [st.enter_context(nc.sbuf_tensor(f"x1o{i}", [128, 512], F32)) for i in range(2)]
        pg = [st.enter_context(nc.psum_tensor(f"pg{i}", [128, 128], F32)) for i in range(4)]
        pt = [st.enter_context(nc.psum_tensor(f"pt{i}", [128, 128], BF16)) for i in range(2)]
        pm = [st.enter_context(nc.psum_tensor(f"pm{i}", [128, 512], F32)) for i in range(2)]
        dbga = dbg_prob.ap() if dbg_prob is not None else None

        for lb in range(GRAM_LBS):
            te0 = p.cnt["te"]
            ve0 = p.cnt["ve"]
            se0 = p.cnt["se"]
            with MB(p) as b:
                @b.sync
                def _(sy):
                    for s in range(5):
                        sy.dma_start(
                            out=gsl[s][:],
                            in_=bass.AP(la.tensor, la.offset + s * S * D + lb * 128,
                                        [[D, 128], [128 * D, NTC], [1, 128]])) \
                            .then_inc(p.sem[f"L{s}"], 16)
                        p.cnt[f"L{s}"] += 16

                @b.tensor
                def _(te):
                    for s in range(5):
                        te.wait_ge(p.sem[f"L{s}"], p.cnt[f"L{s}"])
                    for j, (a, bb) in enumerate(PAIRS):
                        if j >= 4:
                            te.wait_ge(p.sem["ve"], ve0 + (j - 4) + 1)
                        for tc in range(NTC):
                            mm = te.matmul(pg[j % 4][:], lhsT=gsl[a][:, 128 * tc:128 * (tc + 1)],
                                           rhs=gsl[bb][:, 128 * tc:128 * (tc + 1)],
                                           start=(tc == 0), stop=(tc == NTC - 1))
                        mm.then_inc(p.sem["te"], 1)
                    p.cnt["te"] += 8

                @b.vector
                def _(v):
                    for j in range(8):
                        v.wait_ge(p.sem["te"], te0 + (j + 1))
                        p.edge(v, v.scalar_tensor_tensor(out=jnk[:], in0=pg[j % 4][:], scalar=1.0,
                                                         in1=id_f[:], op0=ALU.mult, op1=ALU.mult,
                                                         accum_out=dall[:, j:j + 1]), "vx")
                        v.nop().then_inc(p.sem["ve"], 1)
                    for u in range(2):
                        cu = slice(4 * u, 4 * u + 4)
                        p.edge(v, v.reduce_max(nmx[:, u:u + 1], dall[:, cu], axis=AX.X), "vx")
                        v.tensor_scalar(out=nmx[:, u:u + 1], in0=nmx[:, u:u + 1],
                                        scalar1=-GRAM_SCALE, scalar2=None, op0=ALU.mult) \
                            .then_inc(p.sem["ve"], 1)
                        v.wait_ge(p.sem["se"], se0 + u + 1)
                        v.reciprocal(rcp[:, u:u + 1], sme[:, u:u + 1]).then_inc(p.sem["ve"], 1)
                    for u in range(2):
                        cu = slice(4 * u, 4 * u + 4)
                        p.edge(v, v.tensor_scalar(out=prob[:, cu], in0=e4t[:, cu],
                                                  scalar1=rcp[:, u:u + 1],
                                                  scalar2=None, op0=ALU.mult), "vx")
                        v.nop().then_inc(p.sem["ve"], 1)
                        for m in range(4):
                            v.tensor_scalar(out=dmu[u][m][:], in0=id_b[:],
                                            scalar1=prob[:, 4 * u + m:4 * u + m + 1],
                                            scalar2=None, op0=ALU.mult).then_inc(p.sem["ve"], 1)
                    p.cnt["ve"] += 22

                @b.scalar
                def _(sc):
                    for u in range(2):
                        cu = slice(4 * u, 4 * u + 4)
                        sc.wait_ge(p.sem["ve"], ve0 + 8 + 2 * u + 1)
                        sc.activation(out=e4t[:, cu], in_=dall[:, cu], func=AF.Exp,
                                      scale=GRAM_SCALE, bias=nmx[:, u:u + 1],
                                      accum_out=sme[:, u:u + 1]).then_inc(p.sem["se"], 1)
                    p.cnt["se"] += 2

                @b.gpsimd
                def _(g):
                    if dbga is not None:
                        g.wait_ge(p.sem["ve"], ve0 + 22)
                        g.dma_start(out=dbga[0, :, 4 * lb:4 * lb + 4], in_=prob[:, 0:4]) \
                            .then_inc(p.sem["S2"], 16)
                        p.cnt["S2"] += 16
                        g.wait_ge(p.sem["S2"], p.cnt["S2"])
                        g.dma_start(out=dbga[1, :, 4 * lb:4 * lb + 4], in_=prob[:, 4:8]) \
                            .then_inc(p.sem["S2"], 16)
                        p.cnt["S2"] += 16
                        g.wait_ge(p.sem["S2"], p.cnt["S2"])

            if GRAM_SUB < 2:
                continue

            te0 = p.cnt["te"]
            ve0 = p.cnt["ve"]
            se0 = p.cnt["se"]
            lx0 = [p.cnt["L4"], p.cnt["L5"]]
            sx0 = [p.cnt["S0"], p.cnt["S1"]]
            with MB(p) as b:
                @b.sync
                def _(sy):
                    for j in range(8):
                        u, tc = divmod(j, 4)
                        if j >= 2:
                            sy.wait_ge(p.sem["ve"], ve0 + (j - 2) + 1)
                        sy.dma_start(out=xcv[j % 2][:],
                                     in_=xca[u, 128 * lb:128 * (lb + 1), 512 * tc:512 * (tc + 1)]) \
                            .then_inc(p.sem[f"L{4 + j % 2}"], 16)
                        p.cnt[f"L{4 + j % 2}"] += 16

                @b.tensor
                def _(te):
                    for m in range(5):
                        for tc in range(NTC):
                            j = m * NTC + tc
                            if j >= 2:
                                te.wait_ge(p.sem["se"], se0 + (j - 1))
                            srcsl = gsl[1 + m] if m < 4 else gsl[0]
                            te.transpose(pt[j % 2][:], srcsl[:, 128 * tc:128 * (tc + 1)],
                                         id_b[:]).then_inc(p.sem["te"], 1)
                    te.wait_ge(p.sem["se"], se0 + 64)
                    for j in range(8):
                        u, tc = divmod(j, 4)
                        if j >= 2:
                            te.wait_ge(p.sem["ve"], ve0 + (j - 2) + 1)
                        for m in range(4):
                            mm = te.matmul(pm[j % 2][:], lhsT=dmu[u][m][:],
                                           rhs=lnc[m][:, 512 * tc:512 * (tc + 1)],
                                           start=(m == 0), stop=(m == 3))
                        mm.then_inc(p.sem["te"], 1)
                    p.cnt["te"] += 88

                @b.scalar
                def _(sc):
                    for m in range(5):
                        for tc in range(NTC):
                            j = m * NTC + tc
                            sc.wait_ge(p.sem["te"], te0 + j + 1)
                            dstsl = lnc[m] if m < 4 else lnc0
                            sc.copy(dstsl[:, 128 * tc:128 * (tc + 1)], pt[j % 2][:]) \
                                .then_inc(p.sem["se"], 1)
                    p.cnt["se"] += 80

                if GRAM_SUB >= 3:
                    @b.vector
                    def _(v):
                        for j in range(8):
                            u, tc = divmod(j, 4)
                            v.wait_ge(p.sem["te"], te0 + 80 + j + 1)
                            v.wait_ge(p.sem["se"], se0 + 80)
                            v.wait_ge(p.sem[f"L{4 + j % 2}"], lx0[j % 2] + 16 * (j // 2 + 1))
                            if j >= 2:
                                v.wait_ge(p.sem[f"S{j % 2}"], sx0[j % 2] + 16 * ((j - 2) // 2 + 1))
                            p.edge(v, v.scalar_tensor_tensor(out=x1o[j % 2][:], in0=pm[j % 2][:],
                                                             scalar=1.0, in1=xcv[j % 2][:],
                                                             op0=ALU.mult, op1=ALU.add), "vx")
                            lnsel = lnc0 if u == 0 else lnc[0]
                            v.tensor_add(x1o[j % 2][:], x1o[j % 2][:],
                                         lnsel[:, 512 * tc:512 * (tc + 1)]) \
                                .then_inc(p.sem["ve"], 1)
                        p.cnt["ve"] += 8

                    @b.gpsimd
                    def _(g):
                        for j in range(8):
                            u, tc = divmod(j, 4)
                            g.wait_ge(p.sem["ve"], ve0 + j + 1)
                            g.dma_start(out=x1a[u, 128 * lb:128 * (lb + 1), 512 * tc:512 * (tc + 1)],
                                        in_=x1o[j % 2][:]).then_inc(p.sem[f"S{j % 2}"], 16)
                            p.cnt[f"S{j % 2}"] += 16
                        g.wait_ge(p.sem["S0"], p.cnt["S0"])
                        g.wait_ge(p.sem["S1"], p.cnt["S1"])
                else:
                    @b.vector
                    def _(v):
                        for j in range(8):
                            v.wait_ge(p.sem["te"], te0 + 80 + j + 1)
                        p.cnt["ve"] += 0



def _ffn(p, src_ap, dst_ap, hs_ap, T, tag, dbg_hs=None):
    nc = p.nc
    NTC = T // 512
    NTT = T // 128
    wa_sb, wb_sb, wr_sb, br_sb = p.c["wa_sb"], p.c["wb_sb"], p.c["wr_sb"], p.c["br_sb"]
    with ExitStack() as st:
        xb = [st.enter_context(nc.sbuf_tensor(f"f{tag}xb{i}", [128, T], BF16)) for i in range(2)]
        sq = [st.enter_context(nc.sbuf_tensor(f"f{tag}sq{i}", [128, T], BF16)) for i in range(2)]
        xin = [st.enter_context(nc.sbuf_tensor(f"f{tag}xi{i}", [128, T], F32)) for i in range(2)]
        s1r = st.enter_context(nc.sbuf_tensor(f"f{tag}s1", [1, T], F32))
        s2r = st.enter_context(nc.sbuf_tensor(f"f{tag}s2", [1, T], F32))
        nmr = st.enter_context(nc.sbuf_tensor(f"f{tag}nm", [1, T], F32))
        rr = st.enter_context(nc.sbuf_tensor(f"f{tag}rr", [1, T], F32))
        m2r = st.enter_context(nc.sbuf_tensor(f"f{tag}m2", [1, T], F32))
        nmb = st.enter_context(nc.sbuf_tensor(f"f{tag}nmb", [128, T], F32))
        hso = [st.enter_context(nc.sbuf_tensor(f"f{tag}hs{i}", [128, T], BF16)) for i in range(2)]
        lowT = st.enter_context(nc.sbuf_tensor(f"f{tag}low", [128, T], BF16))
        rtraw = st.enter_context(nc.sbuf_tensor(f"f{tag}rt", [8, T], F32))
        lgN = st.enter_context(nc.sbuf_tensor(f"f{tag}lgN", [128, 8], F32))
        nm8 = st.enter_context(nc.sbuf_tensor(f"f{tag}nm8", [128, 1], F32))
        sm8 = st.enter_context(nc.sbuf_tensor(f"f{tag}sm8", [128, 1], F32))
        rc8 = st.enter_context(nc.sbuf_tensor(f"f{tag}rc8", [128, 1], F32))
        rcol = st.enter_context(nc.sbuf_tensor(f"f{tag}rcol", [128, 1], F32))
        fac = st.enter_context(nc.sbuf_tensor(f"f{tag}fac", [128, 1], F32))
        e8n = st.enter_context(nc.sbuf_tensor(f"f{tag}e8n", [128, 8], BF16))
        e8T = st.enter_context(nc.sbuf_tensor(f"f{tag}e8T", [8, T], BF16))
        mid = st.enter_context(nc.sbuf_tensor(f"f{tag}mid", [128, T], BF16))
        x2o = [st.enter_context(nc.sbuf_tensor(f"f{tag}xo{i}", [128, 512], F32)) for i in range(2)]
        onef = st.enter_context(nc.sbuf_tensor(f"f{tag}onef", [1, 128], F32))

        l0 = [p.cnt["L0"], p.cnt["L1"]]
        ve0, se0, te0 = p.cnt["ve"], p.cnt["se"], p.cnt["te"]
        with ExitStack() as pst, MB(p) as b:
            ps1 = [pst.enter_context(nc.psum_tensor(f"f{tag}p1{i}", [1, 512], F32)) for i in range(NTC)]
            ps2 = [pst.enter_context(nc.psum_tensor(f"f{tag}p2{i}", [1, 512], F32)) for i in range(NTC)]

            @b.sync
            def _(sy):
                for dc in range(DCH):
                    if dc >= 2:
                        sy.wait_ge(p.sem["te"], te0 + (dc - 2) + 1)
                    sy.dma_start(out=xin[dc % 2][:], in_=src_ap[128 * dc:128 * (dc + 1), :]) \
                        .then_inc(p.sem[f"L{dc % 2}"], 16)
                    p.cnt[f"L{dc % 2}"] += 16

            @b.vector
            def _(v):
                v.memset(onef[:], 1.0).then_inc(p.sem["ve"], 1)
                for dc in range(DCH):
                    v.wait_ge(p.sem[f"L{dc % 2}"], l0[dc % 2] + 16 * (dc // 2 + 1))
                    if dc >= 2:
                        v.wait_ge(p.sem["te"], te0 + (dc - 2) + 1)
                    v.tensor_copy(xb[dc % 2][:], xin[dc % 2][:]).then_inc(p.sem["ve"], 1)
                p.cnt["ve"] += 1 + DCH

            @b.scalar
            def _(sc):
                for dc in range(DCH):
                    sc.wait_ge(p.sem[f"L{dc % 2}"], l0[dc % 2] + 16 * (dc // 2 + 1))
                    if dc >= 2:
                        sc.wait_ge(p.sem["te"], te0 + (dc - 2) + 1)
                    sc.activation(out=sq[dc % 2][:], in_=xin[dc % 2][:], func=AF.Square) \
                        .then_inc(p.sem["se"], 1)
                p.cnt["se"] += DCH

            @b.tensor
            def _(te):
                for dc in range(DCH):
                    te.wait_ge(p.sem["ve"], ve0 + 1 + dc + 1)
                    te.wait_ge(p.sem["se"], se0 + dc + 1)
                    for tc in range(NTC):
                        te.matmul(ps1[tc][:], lhsT=p.c["ones_col"][:],
                                  rhs=xb[dc % 2][:, 512 * tc:512 * (tc + 1)],
                                  start=(dc == 0), stop=(dc == DCH - 1))
                    for tc in range(NTC):
                        mm = te.matmul(ps2[tc][:], lhsT=p.c["ones_col"][:],
                                       rhs=sq[dc % 2][:, 512 * tc:512 * (tc + 1)],
                                       start=(dc == 0), stop=(dc == DCH - 1))
                    mm.then_inc(p.sem["te"], 1)
                p.cnt["te"] += DCH

            @b.scalar
            def _(sc2):
                sc2.wait_ge(p.sem["te"], te0 + DCH)
                for tc in range(NTC):
                    cs = slice(512 * tc, 512 * (tc + 1))
                    p.edge(sc2, sc2.copy(s1r[:, cs], ps1[tc][:]), "sx")
                    p.edge(sc2, sc2.copy(s2r[:, cs], ps2[tc][:]), "sx")
                p.edge(sc2, sc2.mul(nmr[:], s1r[:], -1.0 / D), "sx")
                p.edge(sc2, sc2.activation(out=m2r[:], in_=nmr[:], func=AF.Square), "sx")
                p.edge(sc2, sc2.mul(rr[:], s2r[:], 1.0 / D), "sx")
                sc2.nop().then_inc(p.sem["se"], 1)
                p.cnt["se"] += 1
                sc2.wait_ge(p.sem["ve"], ve0 + 1 + DCH + 1)
                p.edge(sc2, sc2.activation(out=rr[:], in_=rr[:], func=AF.Ln,
                                           bias=p.c["epscol"][0:1, 0:1]), "sx")
                p.edge(sc2, sc2.activation(out=rr[:], in_=rr[:], func=AF.Exp, scale=-0.5), "sx")
                sc2.nop().then_inc(p.sem["se"], 1)
                p.cnt["se"] += 1

            @b.vector
            def _(v2):
                v2.wait_ge(p.sem["se"], se0 + DCH + 1)
                p.edge(v2, v2.tensor_sub(rr[:], rr[:], m2r[:]), "vx")
                v2.nop().then_inc(p.sem["ve"], 1)
                p.cnt["ve"] += 1

        l0 = [p.cnt["L0"], p.cnt["L1"]]
        s0 = [p.cnt["S0"], p.cnt["S1"]]
        ve0, se0, te0 = p.cnt["ve"], p.cnt["se"], p.cnt["te"]
        with ExitStack() as pst, MB(p) as b:
            pb = [pst.enter_context(nc.psum_tensor(f"f{tag}pb{i}", [128, 512], F32)) for i in range(NTC)]

            @b.tensor
            def _(te):
                for tc in range(NTC):
                    mm = te.matmul(pb[tc][:], lhsT=onef[:], rhs=nmr[0:1, 512 * tc:512 * (tc + 1)],
                                   start=True, stop=True)
                mm.then_inc(p.sem["te"], 1)
                p.cnt["te"] += 1

            @b.scalar
            def _(sc):
                sc.wait_ge(p.sem["te"], te0 + 1)
                for tc in range(NTC):
                    p.edge(sc, sc.copy(nmb[:, 512 * tc:512 * (tc + 1)], pb[tc][:]), "sx")
                sc.nop().then_inc(p.sem["se"], 1)
                p.cnt["se"] += 1

            @b.sync
            def _(sy):
                for dc in range(DCH):
                    if dc >= 2:
                        sy.wait_ge(p.sem["ve"], ve0 + (dc - 2) + 1)
                    sy.dma_start(out=xin[dc % 2][:], in_=src_ap[128 * dc:128 * (dc + 1), :]) \
                        .then_inc(p.sem[f"L{dc % 2}"], 16)
                    p.cnt[f"L{dc % 2}"] += 16

            @b.vector
            def _(v):
                v.wait_ge(p.sem["se"], se0 + 1)
                for dc in range(DCH):
                    v.wait_ge(p.sem[f"L{dc % 2}"], l0[dc % 2] + 16 * (dc // 2 + 1))
                    if dc >= 2:
                        v.wait_ge(p.sem[f"S{dc % 2}"], s0[dc % 2] + 16 * ((dc - 2) // 2 + 1))
                    v.tensor_add(hso[dc % 2][:], xin[dc % 2][:], nmb[:]).then_inc(p.sem["ve"], 1)
                p.cnt["ve"] += DCH

            @b.gpsimd
            def _(g):
                for dc in range(DCH):
                    g.wait_ge(p.sem["ve"], ve0 + dc + 1)
                    g.dma_start(out=hs_ap[128 * dc:128 * (dc + 1), :], in_=hso[dc % 2][:]) \
                        .then_inc(p.sem[f"S{dc % 2}"], 16)
                    p.cnt[f"S{dc % 2}"] += 16
                g.wait_ge(p.sem["S0"], p.cnt["S0"])
                g.wait_ge(p.sem["S1"], p.cnt["S1"])

        l0 = [p.cnt[f"L{i}"] for i in range(4)]
        ve0, se0, te0 = p.cnt["ve"], p.cnt["se"], p.cnt["te"]
        with ExitStack() as pst, MB(p) as b:
            pLow = pst.enter_context(nc.psum_tensor(f"f{tag}pL", [128, 512], F32))
            pRt = pst.enter_context(nc.psum_tensor(f"f{tag}pR", [8, 512], F32))
            pr8 = pst.enter_context(nc.psum_tensor(f"f{tag}pr8", [8, 512], F32))

            @b.sync
            def _(sy):
                hbuf = [xb[0], xb[1], sq[0], sq[1]]
                for tc in range(NTC):
                    for dc in range(DCH):
                        j = tc * DCH + dc
                        if j >= 4:
                            tcp, dcp = divmod(j - 4, DCH)
                            sy.wait_ge(p.sem["te"], te0 + tcp * (DCH + 1) + dcp + 1)
                        sy.dma_start(out=hbuf[j % 4][:, 0:512],
                                     in_=hs_ap[128 * dc:128 * (dc + 1), 512 * tc:512 * (tc + 1)]) \
                            .then_inc(p.sem[f"L{j % 4}"], 16)
                        p.cnt[f"L{j % 4}"] += 16

            @b.tensor
            def _(te):
                hbuf = [xb[0], xb[1], sq[0], sq[1]]
                for tc in range(NTC):
                    if tc >= 1:
                        te.wait_ge(p.sem["se"], se0 + 2 * (tc - 1) + 2)
                        te.wait_ge(p.sem["ve"], ve0 + 2 * (tc - 1) + 1)
                    for dc in range(DCH):
                        j = tc * DCH + dc
                        te.wait_ge(p.sem[f"L{j % 4}"], l0[j % 4] + 16 * (j // 4 + 1))
                        rhs = hbuf[j % 4][:, 0:512]
                        te.matmul(pLow[:], lhsT=wa_sb[:, 128 * dc:128 * (dc + 1)], rhs=rhs,
                                  start=(dc == 0), stop=(dc == DCH - 1))
                        mm = te.matmul(pRt[:], lhsT=wr_sb[:, 8 * dc:8 * (dc + 1)], rhs=rhs,
                                       start=(dc == 0), stop=(dc == DCH - 1))
                        mm.then_inc(p.sem["te"], 1)
                    mm = te.matmul(pr8[:], lhsT=onef[0:1, 0:8], rhs=rr[0:1, 512 * tc:512 * (tc + 1)],
                                   start=True, stop=True)
                    mm.then_inc(p.sem["te"], 1)
                p.cnt["te"] += NTC * (DCH + 1)

            @b.scalar
            def _(sc):
                for tc in range(NTC):
                    cs = slice(512 * tc, 512 * (tc + 1))
                    sc.wait_ge(p.sem["te"], te0 + tc * (DCH + 1) + DCH)
                    sc.copy(lowT[:, cs], pLow[:]).then_inc(p.sem["se"], 1)
                    sc.copy(rtraw[:, cs], pRt[:]).then_inc(p.sem["se"], 1)
                p.cnt["se"] += 2 * NTC

            @b.vector
            def _(v):
                for tc in range(NTC):
                    cs = slice(512 * tc, 512 * (tc + 1))
                    v.wait_ge(p.sem["te"], te0 + (tc + 1) * (DCH + 1))
                    v.wait_ge(p.sem["se"], se0 + 2 * tc + 2)
                    p.edge(v, v.tensor_mul(rtraw[:, cs], rtraw[:, cs], pr8[:]), "vx")
                    v.nop().then_inc(p.sem["ve"], 1)
                    v.tensor_scalar(out=rtraw[:, cs], in0=rtraw[:, cs], scalar1=br_sb[:, 0:1],
                                    scalar2=None, op0=ALU.add).then_inc(p.sem["ve"], 1)
                p.cnt["ve"] += 2 * NTC

        ve0, se0, te0 = p.cnt["ve"], p.cnt["se"], p.cnt["te"]
        with ExitStack() as pst, MB(p) as b:
            pLgN = pst.enter_context(nc.psum_tensor(f"f{tag}pN", [128, 8], F32))
            pRc = pst.enter_context(nc.psum_tensor(f"f{tag}pC", [128, 1], F32))
            p8 = pst.enter_context(nc.psum_tensor(f"f{tag}p8", [8, 128], F32))

            @b.tensor
            def _(te):
                for tt in range(NTT):
                    if tt >= 1:
                        te.wait_ge(p.sem["se"], se0 + 4 * tt)
                    mm = te.matmul(pLgN[:], lhsT=rtraw[:, 128 * tt:128 * (tt + 1)], rhs=p.c["i8_sb"][:],
                                   start=True, stop=True)
                    mm.then_inc(p.sem["te"], 1)
                    mm = te.matmul(pRc[:], lhsT=rr[0:1, 128 * tt:128 * (tt + 1)], rhs=onef[0:1, 0:1],
                                   start=True, stop=True)
                    mm.then_inc(p.sem["te"], 1)
                    te.wait_ge(p.sem["ve"], ve0 + 5 * tt + 5)
                    mm = te.matmul(p8[:], lhsT=e8n[:], rhs=p.c["id_b"][:], start=True, stop=True)
                    mm.then_inc(p.sem["te"], 1)
                p.cnt["te"] += 3 * NTT

            @b.scalar
            def _(sc):
                for tt in range(NTT):
                    sc.wait_ge(p.sem["te"], te0 + 3 * tt + 1)
                    p.edge(sc, sc.copy(lgN[:], pLgN[:]), "sx")
                    sc.nop().then_inc(p.sem["se"], 1)
                    sc.wait_ge(p.sem["te"], te0 + 3 * tt + 2)
                    p.edge(sc, sc.copy(rcol[:], pRc[:]), "sx")
                    sc.nop().then_inc(p.sem["se"], 1)
                    sc.wait_ge(p.sem["ve"], ve0 + 5 * tt + 2)
                    sc.wait_ge(p.sem["te"], te0 + 3 * tt)
                    p.edge(sc, sc.activation(out=e8n[:], in_=lgN[:], func=AF.Exp,
                                             bias=nm8[:, 0:1], accum_out=sm8[:, 0:1]), "sx")
                    sc.nop().then_inc(p.sem["se"], 1)
                    sc.wait_ge(p.sem["te"], te0 + 3 * tt + 3)
                    p.edge(sc, sc.copy(e8T[:, 128 * tt:128 * (tt + 1)], p8[:]), "sx")
                    sc.nop().then_inc(p.sem["se"], 1)
                p.cnt["se"] += 4 * NTT

            @b.vector
            def _(v):
                for tt in range(NTT):
                    v.wait_ge(p.sem["se"], se0 + 4 * tt + 1)
                    p.edge(v, v.reduce_max(nm8[:, 0:1], lgN[:], axis=AX.X), "vx")
                    p.edge(v, v.tensor_scalar(out=nm8[:, 0:1], in0=nm8[:, 0:1], scalar1=-1.0,
                                              scalar2=None, op0=ALU.mult), "vx")
                    v.nop().then_inc(p.sem["ve"], 1)
                    v.nop().then_inc(p.sem["ve"], 1)
                    v.wait_ge(p.sem["se"], se0 + 4 * tt + 3)
                    p.edge(v, v.reciprocal(rc8[:, 0:1], sm8[:, 0:1]), "vx")
                    v.nop().then_inc(p.sem["ve"], 1)
                    p.edge(v, v.tensor_scalar(out=fac[:, 0:1], in0=rc8[:, 0:1], scalar1=rcol[:, 0:1],
                                              scalar2=FFN_FACTOR, op0=ALU.mult, op1=ALU.mult), "vx")
                    v.nop().then_inc(p.sem["ve"], 1)
                    p.edge(v, v.tensor_scalar(out=e8n[:], in0=e8n[:], scalar1=fac[:, 0:1],
                                              scalar2=None, op0=ALU.mult), "vx")
                    v.nop().then_inc(p.sem["ve"], 1)
                p.cnt["ve"] += 5 * NTT

        ve0, se0, te0 = p.cnt["ve"], p.cnt["se"], p.cnt["te"]
        with ExitStack() as pst, MB(p) as b:
            pRep = [pst.enter_context(nc.psum_tensor(f"f{tag}pE{i}", [128, 512], F32)) for i in range(2)]

            @b.tensor
            def _(te):
                for tc in range(NTC):
                    if tc >= 2:
                        te.wait_ge(p.sem["ve"], ve0 + (tc - 2) + 1)
                    mm = te.matmul(pRep[tc % 2][:], lhsT=p.c["e8_sb"][:],
                                   rhs=e8T[:, 512 * tc:512 * (tc + 1)], start=True, stop=True)
                    mm.then_inc(p.sem["te"], 1)
                p.cnt["te"] += NTC

            @b.vector
            def _(v):
                for tc in range(NTC):
                    cs = slice(512 * tc, 512 * (tc + 1))
                    v.wait_ge(p.sem["te"], te0 + tc + 1)
                    v.tensor_mul(mid[:, cs], lowT[:, cs], pRep[tc % 2][:]).then_inc(p.sem["ve"], 1)
                p.cnt["ve"] += NTC

        l0 = [p.cnt["L0"], p.cnt["L1"]]
        s0 = [p.cnt["S0"], p.cnt["S1"]]
        ve0, se0, te0 = p.cnt["ve"], p.cnt["se"], p.cnt["te"]
        with ExitStack() as pst, MB(p) as b:
            pOut = [pst.enter_context(nc.psum_tensor(f"f{tag}pO{i}", [128, 512], F32)) for i in range(2)]

            @b.sync
            def _(sy):
                for j in range(DCH * NTC):
                    oblk, tc = divmod(j, NTC)
                    if j >= 2:
                        sy.wait_ge(p.sem["ve"], ve0 + (j - 2) + 1)
                    sy.dma_start(out=xin[j % 2][:, 0:512],
                                 in_=src_ap[128 * oblk:128 * (oblk + 1), 512 * tc:512 * (tc + 1)]) \
                        .then_inc(p.sem[f"L{j % 2}"], 16)
                    p.cnt[f"L{j % 2}"] += 16

            @b.tensor
            def _(te):
                for j in range(DCH * NTC):
                    oblk, tc = divmod(j, NTC)
                    if j >= 2:
                        te.wait_ge(p.sem["ve"], ve0 + (j - 2) + 1)
                    mm = te.matmul(pOut[j % 2][:], lhsT=wb_sb[:, 128 * oblk:128 * (oblk + 1)],
                                   rhs=mid[:, 512 * tc:512 * (tc + 1)], start=True, stop=True)
                    mm.then_inc(p.sem["te"], 1)
                p.cnt["te"] += DCH * NTC

            @b.vector
            def _(v):
                for j in range(DCH * NTC):
                    v.wait_ge(p.sem["te"], te0 + j + 1)
                    v.wait_ge(p.sem[f"L{j % 2}"], l0[j % 2] + 16 * (j // 2 + 1))
                    if j >= 2:
                        v.wait_ge(p.sem[f"S{j % 2}"], s0[j % 2] + 16 * ((j - 2) // 2 + 1))
                    v.scalar_tensor_tensor(out=x2o[j % 2][:], in0=pOut[j % 2][:], scalar=1.0,
                                           in1=xin[j % 2][:, 0:512], op0=ALU.mult, op1=ALU.add) \
                        .then_inc(p.sem["ve"], 1)
                p.cnt["ve"] += DCH * NTC

            @b.gpsimd
            def _(g):
                for j in range(DCH * NTC):
                    oblk, tc = divmod(j, NTC)
                    g.wait_ge(p.sem["ve"], ve0 + j + 1)
                    g.dma_start(out=dst_ap[128 * oblk:128 * (oblk + 1), 512 * tc:512 * (tc + 1)],
                                in_=x2o[j % 2][:]).then_inc(p.sem[f"S{j % 2}"], 16)
                    p.cnt[f"S{j % 2}"] += 16
                g.wait_ge(p.sem["S0"], p.cnt["S0"])
                g.wait_ge(p.sem["S1"], p.cnt["S1"])



def _ln_stats_rows(p, src_ap, T, nmr, rr, m2r, onef, tag):
    nc = p.nc
    NTC = T // 512
    with ExitStack() as st:
        xin = [st.enter_context(nc.sbuf_tensor(f"s{tag}xi{i}", [128, T], F32)) for i in range(2)]
        xb = [st.enter_context(nc.sbuf_tensor(f"s{tag}xb{i}", [128, T], BF16)) for i in range(2)]
        sq = [st.enter_context(nc.sbuf_tensor(f"s{tag}sq{i}", [128, T], BF16)) for i in range(2)]
        s1r = st.enter_context(nc.sbuf_tensor(f"s{tag}s1", [1, T], F32))
        s2r = st.enter_context(nc.sbuf_tensor(f"s{tag}s2", [1, T], F32))
        l0 = [p.cnt["L0"], p.cnt["L1"]]
        ve0, se0, te0 = p.cnt["ve"], p.cnt["se"], p.cnt["te"]
        with ExitStack() as pst, MB(p) as b:
            ps1 = [pst.enter_context(nc.psum_tensor(f"s{tag}p1{i}", [1, 512], F32)) for i in range(NTC)]
            ps2 = [pst.enter_context(nc.psum_tensor(f"s{tag}p2{i}", [1, 512], F32)) for i in range(NTC)]

            @b.sync
            def _(sy):
                for dc in range(DCH):
                    if dc >= 2:
                        sy.wait_ge(p.sem["te"], te0 + (dc - 2) + 1)
                    sy.dma_start(out=xin[dc % 2][:], in_=src_ap[128 * dc:128 * (dc + 1), :]) \
                        .then_inc(p.sem[f"L{dc % 2}"], 16)
                    p.cnt[f"L{dc % 2}"] += 16

            @b.vector
            def _(v):
                for dc in range(DCH):
                    v.wait_ge(p.sem[f"L{dc % 2}"], l0[dc % 2] + 16 * (dc // 2 + 1))
                    if dc >= 2:
                        v.wait_ge(p.sem["te"], te0 + (dc - 2) + 1)
                    v.tensor_copy(xb[dc % 2][:], xin[dc % 2][:]).then_inc(p.sem["ve"], 1)
                p.cnt["ve"] += DCH
                v.wait_ge(p.sem["se"], se0 + DCH + 1)
                p.edge(v, v.tensor_sub(rr[:], rr[:], m2r[:]), "vx")
                v.nop().then_inc(p.sem["ve"], 1)
                p.cnt["ve"] += 1

            @b.scalar
            def _(sc):
                for dc in range(DCH):
                    sc.wait_ge(p.sem[f"L{dc % 2}"], l0[dc % 2] + 16 * (dc // 2 + 1))
                    if dc >= 2:
                        sc.wait_ge(p.sem["te"], te0 + (dc - 2) + 1)
                    sc.activation(out=sq[dc % 2][:], in_=xin[dc % 2][:], func=AF.Square) \
                        .then_inc(p.sem["se"], 1)
                p.cnt["se"] += DCH
                sc.wait_ge(p.sem["te"], te0 + DCH)
                for tc in range(NTC):
                    cs = slice(512 * tc, 512 * (tc + 1))
                    p.edge(sc, sc.copy(s1r[:, cs], ps1[tc][:]), "sx")
                    p.edge(sc, sc.copy(s2r[:, cs], ps2[tc][:]), "sx")
                p.edge(sc, sc.mul(nmr[:], s1r[:], -1.0 / D), "sx")
                p.edge(sc, sc.activation(out=m2r[:], in_=nmr[:], func=AF.Square), "sx")
                p.edge(sc, sc.mul(rr[:], s2r[:], 1.0 / D), "sx")
                sc.nop().then_inc(p.sem["se"], 1)
                p.cnt["se"] += 1
                sc.wait_ge(p.sem["ve"], ve0 + DCH + 1)
                p.edge(sc, sc.activation(out=rr[:], in_=rr[:], func=AF.Ln,
                                         bias=p.c["epscol"][0:1, 0:1]), "sx")
                p.edge(sc, sc.activation(out=rr[:], in_=rr[:], func=AF.Exp, scale=-0.5), "sx")
                sc.nop().then_inc(p.sem["se"], 1)
                p.cnt["se"] += 1

            @b.tensor
            def _(te):
                for dc in range(DCH):
                    te.wait_ge(p.sem["ve"], ve0 + dc + 1)
                    te.wait_ge(p.sem["se"], se0 + dc + 1)
                    for tc in range(NTC):
                        te.matmul(ps1[tc][:], lhsT=p.c["ones_col"][:],
                                  rhs=xb[dc % 2][:, 512 * tc:512 * (tc + 1)],
                                  start=(dc == 0), stop=(dc == DCH - 1))
                    for tc in range(NTC):
                        mm = te.matmul(ps2[tc][:], lhsT=p.c["ones_col"][:],
                                       rhs=sq[dc % 2][:, 512 * tc:512 * (tc + 1)],
                                       start=(dc == 0), stop=(dc == DCH - 1))
                    mm.then_inc(p.sem["te"], 1)
                p.cnt["te"] += DCH


def _bcast_row(p, b_te, b_sc, row_ap, out_sb, T, onef, pbank):
    raise NotImplementedError


def _p3(p, x2_s, x2pi_s, costabs, wq, wk, wv, q_sc, k_sc, vt_sc, hsel, dbg):
    nc = p.nc
    onef = p.c["onef3"]
    with ExitStack() as st0:
        nmz = st0.enter_context(nc.sbuf_tensor("nmz", [1, S], F32))
        rz = st0.enter_context(nc.sbuf_tensor("rz", [1, S], F32))

        with ExitStack() as st1:
            nm3 = st1.enter_context(nc.sbuf_tensor("nm3", [1, S], F32))
            r3 = st1.enter_context(nc.sbuf_tensor("r3", [1, S], F32))
            m2t = st1.enter_context(nc.sbuf_tensor("m2t", [1, S], F32))

            _ln_stats_rows(p, x2_s.ap()[0][0:D, :], S, nm3, r3, m2t, onef, "3p")
            _ln_stats_rows(p, x2_s.ap()[1][0:D, :], S, nmz, rz, m2t, onef, "3z")

            with MB(p) as b:
                @b.gpsimd
                def _(g):
                    x2a = x2_s.ap()
                    g.dma_start(out=x2a[0][D:D + 1, :], in_=nm3[:]).then_inc(p.sem["S2"], 16)
                    p.cnt["S2"] += 16
                    g.wait_ge(p.sem["S2"], p.cnt["S2"])
                    g.dma_start(out=x2a[0][D + 1:D + 2, :], in_=r3[:]).then_inc(p.sem["S2"], 16)
                    p.cnt["S2"] += 16
                    g.wait_ge(p.sem["S2"], p.cnt["S2"])
                    rh = g.alloc_register(name="c_rh")
                    p.roffs = []
                    for half in range(2):
                        g.reg_load(rh, p.c["hsel_sb"][0:1, half:half + 1])
                        g.reg_alu(rh, rh, 128, ALU.mult)
                        for i in range(8):
                            r = g.alloc_register(name=f"roff{half}_{i}")
                            g.reg_alu(r, rh, 256 * i, ALU.add)
                            p.roffs.append(g.snap(r, donate=True, min_val=0, max_val=S - 128))
                    g.free_register(rh)
                    for j in range(16):
                        sv = p.roffs[j]
                        src128 = bass.AP(x2a.tensor, sv, [[S, D + 2], [1, 128]])
                        g.dma_start(out=x2pi_s.ap()[:, 128 * j:128 * (j + 1)], in_=src128) \
                            .then_inc(p.sem["S3"], 16)
                        p.cnt["S3"] += 16
                    p.gather_done = p.cnt["S3"]

        _p3_zside(p, x2_s.ap()[1][0:D, :], nmz, rz, wq, wk, costabs, dbg)

    _p3_pside(p, x2pi_s, wq, wk, wv, q_sc, k_sc, vt_sc, costabs, dbg)


def _p3_zside(p, x2z_ap, nmz, rz, wq, wk, costabs, dbg):
    nc = p.nc
    onef = p.c["onef3"]
    cq_t, ck_t, ct_t, cv_t = costabs
    TW = [512, 512, NTEXT - 1024]
    TOFF = [0, 512, 1024]
    with ExitStack() as st:
        nmzb = st.enter_context(nc.sbuf_tensor("nmzb", [128, S], F32))
        rztb = st.enter_context(nc.sbuf_tensor("rztb", [128, NTEXT], BF16))
        cvr = st.enter_context(nc.sbuf_tensor("cvr", [128, NVIS], BF16))
        svr = st.enter_context(nc.sbuf_tensor("svr", [128, NVIS], BF16))
        qsum = st.enter_context(nc.sbuf_tensor("qsum", [128, DCH], F32))
        qsb = st.enter_context(nc.sbuf_tensor("qsb", [128, DCH], BF16))
        imp = st.enter_context(nc.sbuf_tensor("impr", [1, NVIS], F32))

        te0, se0, ve0 = p.cnt["te"], p.cnt["se"], p.cnt["ve"]
        gl0 = p.cnt["gl"]
        with ExitStack() as st2, MB(p) as b:
            rzvb = st2.enter_context(nc.sbuf_tensor("rzvb", [128, NVIS], F32))
            cv_sb = st2.enter_context(nc.sbuf_tensor("cv_sb", [128, 2 * NVIS], BF16))
            pbn = [st2.enter_context(nc.psum_tensor(f"zpb{i}", [128, 512], F32)) for i in range(4)]
            pbt = [st2.enter_context(nc.psum_tensor(f"zpt{i}", [128, 512], F32)) for i in range(3)]
            pbv = st2.enter_context(nc.psum_tensor("zpv", [128, 512], F32))

            @b.gpsimd
            def _(g):
                def gchain(dst_ap, src_ap):
                    g.wait_ge(p.sem["gl"], p.cnt["gl"])
                    g.dma_start(out=dst_ap, in_=src_ap).then_inc(p.sem["gl"], 16)
                    p.cnt["gl"] += 16
                gchain(cv_sb[:, 0:NVIS], cv_t.ap()[0])
                gchain(cv_sb[:, NVIS:2 * NVIS], cv_t.ap()[1])
                g.wait_ge(p.sem["gl"], p.cnt["gl"])

            @b.tensor
            def _(te):
                for tc in range(4):
                    te.matmul(pbn[tc][:], lhsT=onef[:], rhs=nmz[0:1, 512 * tc:512 * (tc + 1)],
                              start=True, stop=True).then_inc(p.sem["te"], 1)
                for tz in range(3):
                    te.matmul(pbt[tz][:, 0:TW[tz]], lhsT=onef[:],
                              rhs=rz[0:1, TEXT_LO + 512 * tz:TEXT_LO + 512 * tz + TW[tz]],
                              start=True, stop=True).then_inc(p.sem["te"], 1)
                te.matmul(pbv[:], lhsT=onef[:], rhs=rz[0:1, VIS_LO:VIS_HI],
                          start=True, stop=True).then_inc(p.sem["te"], 1)
                p.cnt["te"] += 8

            @b.scalar
            def _(sc):
                for tc in range(4):
                    sc.wait_ge(p.sem["te"], te0 + tc + 1)
                    sc.copy(nmzb[:, 512 * tc:512 * (tc + 1)], pbn[tc][:]).then_inc(p.sem["se"], 1)
                for tz in range(3):
                    sc.wait_ge(p.sem["te"], te0 + 4 + tz + 1)
                    sc.copy(rztb[:, 512 * tz:512 * tz + TW[tz]], pbt[tz][:, 0:TW[tz]]) \
                        .then_inc(p.sem["se"], 1)
                sc.wait_ge(p.sem["te"], te0 + 8)
                sc.copy(rzvb[:], pbv[:]).then_inc(p.sem["se"], 1)
                p.cnt["se"] += 8

            @b.vector
            def _(v):
                v.wait_ge(p.sem["se"], se0 + 8)
                v.wait_ge(p.sem["gl"], gl0 + 16 * 2)
                p.edge(v, v.tensor_mul(cvr[:], cv_sb[:, 0:NVIS], rzvb[:]), "vx")
                p.edge(v, v.tensor_mul(svr[:], cv_sb[:, NVIS:2 * NVIS], rzvb[:]), "vx")
                v.nop().then_inc(p.sem["ve"], 1)
                p.cnt["ve"] += 1

        with ExitStack() as st3:
            hsZt = st3.enter_context(nc.sbuf_tensor("hsZt", [128, DCH * NTEXT], BF16))
            l0 = [p.cnt["L0"], p.cnt["L1"]]
            ve0 = p.cnt["ve"]
            with ExitStack() as stx, MB(p) as b:
                xin = [stx.enter_context(nc.sbuf_tensor(f"zxi{i}", [128, NTEXT], F32))
                       for i in range(2)]

                @b.sync
                def _(sy):
                    for dc in range(DCH):
                        if dc >= 2:
                            sy.wait_ge(p.sem["ve"], ve0 + (dc - 2) + 1)
                        sy.dma_start(out=xin[dc % 2][:],
                                     in_=x2z_ap[128 * dc:128 * (dc + 1), TEXT_LO:TEXT_HI]) \
                            .then_inc(p.sem[f"L{dc % 2}"], 16)
                        p.cnt[f"L{dc % 2}"] += 16

                @b.vector
                def _(v):
                    for dc in range(DCH):
                        v.wait_ge(p.sem[f"L{dc % 2}"], l0[dc % 2] + 16 * (dc // 2 + 1))
                        v.tensor_add(hsZt[:, NTEXT * dc:NTEXT * (dc + 1)],
                                     xin[dc % 2][:], nmzb[:, TEXT_LO:TEXT_HI]) \
                            .then_inc(p.sem["ve"], 1)
                    p.cnt["ve"] += DCH

            gl0 = [p.cnt["L6"], p.cnt["L7"]]
            te0, se0, ve0 = p.cnt["te"], p.cnt["se"], p.cnt["ve"]
            with ExitStack() as pst, MB(p) as b:
                ct_sb = pst.enter_context(nc.sbuf_tensor("ct_sb", [128, 2 * NTEXT], BF16))
                wqb = [pst.enter_context(nc.sbuf_tensor(f"zwq{i}", [128, D], BF16))
                       for i in range(2)]
                qtx = [pst.enter_context(nc.sbuf_tensor(f"qtx{i}", [128, 512], BF16))
                       for i in range(2)]
                q1 = pst.enter_context(nc.sbuf_tensor("q1z", [128, 512], BF16))
                q2 = pst.enter_context(nc.sbuf_tensor("q2z", [128, 512], BF16))
                jk = pst.enter_context(nc.sbuf_tensor("jkz", [128, 512], BF16))
                qs3 = pst.enter_context(nc.sbuf_tensor("qs3", [128, 4], F32))
                pQ = [pst.enter_context(nc.psum_tensor(f"zpQ{i}", [128, 512], F32)) for i in range(2)]
                pR = [pst.enter_context(nc.psum_tensor(f"zpR{i}", [128, 512], F32)) for i in range(2)]

                @b.gpsimd
                def _(g):
                    def gchain(dst_ap, src_ap):
                        g.wait_ge(p.sem["gl"], p.cnt["gl"])
                        g.dma_start(out=dst_ap, in_=src_ap).then_inc(p.sem["gl"], 16)
                        p.cnt["gl"] += 16
                    gchain(ct_sb[:, 0:NTEXT], ct_t.ap()[0])
                    gchain(ct_sb[:, NTEXT:2 * NTEXT], ct_t.ap()[1])
                    for ob in range(DCH):
                        if ob >= 2:
                            g.wait_ge(p.sem["te"], te0 + 6 * ob - 7)
                        g.dma_start(out=wqb[ob % 2][:], in_=wq.ap()[ob]) \
                            .then_inc(p.sem[f"L{6 + ob % 2}"], 16)
                        p.cnt[f"L{6 + ob % 2}"] += 16
                    g.wait_ge(p.sem["gl"], p.cnt["gl"])

                @b.tensor
                def _(te):
                    for ob in range(DCH):
                        te.wait_ge(p.sem[f"L{6 + ob % 2}"], gl0[ob % 2] + 16 * (ob // 2 + 1))
                        for tz in range(3):
                            j = 3 * ob + tz
                            if j >= 2:
                                te.wait_ge(p.sem["se"], se0 + (j - 1))
                            for dc in range(DCH):
                                mm = te.matmul(pQ[j % 2][:, 0:TW[tz]],
                                               lhsT=wqb[ob % 2][:, 128 * dc:128 * (dc + 1)],
                                               rhs=hsZt[:, NTEXT * dc + TOFF[tz]:
                                                        NTEXT * dc + TOFF[tz] + TW[tz]],
                                               start=(dc == 0), stop=(dc == DCH - 1))
                            mm.then_inc(p.sem["te"], 1)
                            te.wait_ge(p.sem["se"], se0 + j + 1)
                            if j >= 2:
                                te.wait_ge(p.sem["ve"], ve0 + 4 * ((j - 2) // 3) + ((j - 2) % 3) + 1)
                            mm = te.matmul(pR[j % 2][:, 0:TW[tz]], lhsT=p.c["rmat_sb"][:],
                                           rhs=qtx[j % 2][:, 0:TW[tz]], start=True, stop=True)
                            mm.then_inc(p.sem["te"], 1)
                    p.cnt["te"] += 6 * DCH

                @b.scalar
                def _(sc):
                    for ob in range(DCH):
                        for tz in range(3):
                            j = 3 * ob + tz
                            sc.wait_ge(p.sem["te"], te0 + 2 * j + 1)
                            if j >= 2:
                                sc.wait_ge(p.sem["te"], te0 + 2 * (j - 2) + 2)
                                sc.wait_ge(p.sem["ve"],
                                           ve0 + 4 * ((j - 2) // 3) + ((j - 2) % 3) + 1)
                            sc.copy(qtx[j % 2][:, 0:TW[tz]], pQ[j % 2][:, 0:TW[tz]]) \
                                .then_inc(p.sem["se"], 1)
                    p.cnt["se"] += 3 * DCH

                @b.vector
                def _(v):
                    gwait = p.cnt["gl"]
                    for ob in range(DCH):
                        for tz in range(3):
                            j = 3 * ob + tz
                            w = TW[tz]
                            cs = slice(TOFF[tz], TOFF[tz] + w)
                            v.wait_ge(p.sem["se"], se0 + j + 1)
                            if j == 0:
                                v.wait_ge(p.sem["gl"], gwait)
                            p.edge(v, v.tensor_mul(q1[:, 0:w], qtx[j % 2][:, 0:w],
                                                   ct_sb[:, cs]), "vx")
                            v.wait_ge(p.sem["te"], te0 + 2 * j + 2)
                            p.edge(v, v.tensor_mul(q2[:, 0:w], pR[j % 2][:, 0:w],
                                                   ct_sb[:, NTEXT + TOFF[tz]:NTEXT + TOFF[tz] + w]),
                                   "vx")
                            p.edge(v, v.tensor_add(q1[:, 0:w], q1[:, 0:w], q2[:, 0:w]), "vx")
                            p.edge(v, v.scalar_tensor_tensor(out=jk[:, 0:w], in0=q1[:, 0:w],
                                                             scalar=1.0, in1=rztb[:, cs],
                                                             op0=ALU.mult, op1=ALU.mult,
                                                             accum_out=qs3[:, tz:tz + 1]), "vx")
                            v.nop().then_inc(p.sem["ve"], 1)
                        p.edge(v, v.tensor_add(qs3[:, 3:4], qs3[:, 0:1], qs3[:, 1:2]), "vx")
                        p.edge(v, v.tensor_add(qsum[:, ob:ob + 1], qs3[:, 3:4], qs3[:, 2:3]), "vx")
                        v.nop().then_inc(p.sem["ve"], 1)
                    p.cnt["ve"] += 4 * DCH

        with ExitStack() as st4:
            hsZv = st4.enter_context(nc.sbuf_tensor("hsZv", [128, DCH * NVIS], BF16))
            l0 = [p.cnt["L0"], p.cnt["L1"]]
            ve0 = p.cnt["ve"]
            with ExitStack() as stx, MB(p) as b:
                xin = [stx.enter_context(nc.sbuf_tensor(f"zxv{i}", [128, NVIS], F32))
                       for i in range(2)]

                @b.sync
                def _(sy):
                    for dc in range(DCH):
                        if dc >= 2:
                            sy.wait_ge(p.sem["ve"], ve0 + (dc - 2) + 1)
                        sy.dma_start(out=xin[dc % 2][:],
                                     in_=x2z_ap[128 * dc:128 * (dc + 1), VIS_LO:VIS_HI]) \
                            .then_inc(p.sem[f"L{dc % 2}"], 16)
                        p.cnt[f"L{dc % 2}"] += 16

                @b.vector
                def _(v):
                    for dc in range(DCH):
                        v.wait_ge(p.sem[f"L{dc % 2}"], l0[dc % 2] + 16 * (dc // 2 + 1))
                        v.tensor_add(hsZv[:, NVIS * dc:NVIS * (dc + 1)],
                                     xin[dc % 2][:], nmzb[:, VIS_LO:VIS_HI]) \
                            .then_inc(p.sem["ve"], 1)
                    p.cnt["ve"] += DCH

            gl0 = [p.cnt["L6"], p.cnt["L7"]]
            te0, se0, ve0 = p.cnt["te"], p.cnt["se"], p.cnt["ve"]
            with ExitStack() as pst, MB(p) as b:
                wqb = [pst.enter_context(nc.sbuf_tensor(f"zwk{i}", [128, D], BF16))
                       for i in range(2)]
                kvr = pst.enter_context(nc.sbuf_tensor("kvr", [128, NVIS], BF16))
                kvo = [pst.enter_context(nc.sbuf_tensor(f"kvo{i}", [128, NVIS], BF16))
                       for i in range(2)]
                q2v = pst.enter_context(nc.sbuf_tensor("q2v", [128, NVIS], BF16))
                pK = [pst.enter_context(nc.psum_tensor(f"zpK{i}", [128, 512], F32)) for i in range(2)]
                pR = [pst.enter_context(nc.psum_tensor(f"zpR2{i}", [128, 512], F32)) for i in range(2)]
                pI = pst.enter_context(nc.psum_tensor("zpI", [1, 512], F32))

                @b.gpsimd
                def _(g):
                    for ob in range(DCH):
                        if ob >= 2:
                            g.wait_ge(p.sem["te"], te0 + 3 * (ob - 2) + 1)
                        g.dma_start(out=wqb[ob % 2][:], in_=wk.ap()[ob]) \
                            .then_inc(p.sem[f"L{6 + ob % 2}"], 16)
                        p.cnt[f"L{6 + ob % 2}"] += 16

                @b.tensor
                def _(te):
                    for ob in range(DCH):
                        te.wait_ge(p.sem[f"L{6 + ob % 2}"], gl0[ob % 2] + 16 * (ob // 2 + 1))
                        if ob >= 2:
                            te.wait_ge(p.sem["se"], se0 + (ob - 2) + 1)
                        for dc in range(DCH):
                            mm = te.matmul(pK[ob % 2][:], lhsT=wqb[ob % 2][:, 128 * dc:128 * (dc + 1)],
                                           rhs=hsZv[:, NVIS * dc:NVIS * (dc + 1)],
                                           start=(dc == 0), stop=(dc == DCH - 1))
                        mm.then_inc(p.sem["te"], 1)
                        te.wait_ge(p.sem["se"], se0 + ob + 1)
                        if ob >= 2:
                            te.wait_ge(p.sem["ve"], ve0 + 1 + (ob - 2) + 1)
                        mm = te.matmul(pR[ob % 2][:], lhsT=p.c["rmat_sb"][:], rhs=kvr[:],
                                       start=True, stop=True)
                        mm.then_inc(p.sem["te"], 1)
                        te.wait_ge(p.sem["ve"], ve0 + 1 + ob + 1)
                        mm = te.matmul(pI[:], lhsT=qsb[:, ob:ob + 1], rhs=kvo[ob % 2][:],
                                       start=(ob == 0), stop=(ob == DCH - 1))
                        mm.then_inc(p.sem["te"], 1)
                    p.cnt["te"] += 3 * DCH

                @b.scalar
                def _(sc):
                    for ob in range(DCH):
                        sc.wait_ge(p.sem["te"], te0 + 3 * ob + 1)
                        if ob >= 1:
                            sc.wait_ge(p.sem["ve"], ve0 + 1 + (ob - 1) + 1)
                        sc.copy(kvr[:], pK[ob % 2][:]).then_inc(p.sem["se"], 1)
                    sc.wait_ge(p.sem["te"], te0 + 3 * DCH)
                    sc.copy(imp[:], pI[:]).then_inc(p.sem["se"], 1)
                    p.cnt["se"] += DCH + 1

                @b.vector
                def _(v):
                    v.tensor_copy(qsb[:], qsum[:]).then_inc(p.sem["ve"], 1)
                    for ob in range(DCH):
                        v.wait_ge(p.sem["se"], se0 + ob + 1)
                        if ob >= 2:
                            v.wait_ge(p.sem["te"], te0 + 3 * (ob - 2) + 3)
                        p.edge(v, v.tensor_mul(kvo[ob % 2][:], kvr[:], cvr[:]), "vx")
                        v.wait_ge(p.sem["te"], te0 + 3 * ob + 2)
                        p.edge(v, v.tensor_mul(q2v[:], pR[ob % 2][:], svr[:]), "vx")
                        p.edge(v, v.tensor_add(kvo[ob % 2][:], kvo[ob % 2][:], q2v[:]), "vx")
                        v.nop().then_inc(p.sem["ve"], 1)
                    p.cnt["ve"] += DCH + 1

        te0, se0, ve0 = p.cnt["te"], p.cnt["se"], p.cnt["ve"]
        gq0 = p.cnt["gq"]
        with ExitStack() as pst, MB(p) as b:
            impb = pst.enter_context(nc.sbuf_tensor("impb", [128, NVIS], F32))
            impc = pst.enter_context(nc.sbuf_tensor("impc", [128, 4], F32))
            jkf = pst.enter_context(nc.sbuf_tensor("jkf", [128, NVIS], F32))
            cg = pst.enter_context(nc.sbuf_tensor("cgz", [128, 4], F32))
            ce = pst.enter_context(nc.sbuf_tensor("cez", [128, 4], F32))
            kbcol = pst.enter_context(nc.sbuf_tensor("kbcolz", [128, 4], F32))
            kbg = pst.enter_context(nc.sbuf_tensor("kbgz", [1, NVIS], F32))
            kbrow16 = pst.enter_context(nc.sbuf_tensor("kbrow16", [16, 128], BF16))
            pB = pst.enter_context(nc.psum_tensor("zpB", [128, 512], F32))
            pC = pst.enter_context(nc.psum_tensor("zpC", [128, 4], F32))
            pRo = pst.enter_context(nc.psum_tensor("zpRo", [1, 512], F32))
            pkc = pst.enter_context(nc.psum_tensor("zpkc", [128, 16], F32))

            @b.tensor
            def _(te):
                te.matmul(pB[:], lhsT=onef[:], rhs=imp[:], start=True, stop=True) \
                    .then_inc(p.sem["te"], 1)
                for c4 in range(4):
                    mm = te.matmul(pC[:, c4:c4 + 1], lhsT=imp[0:1, 128 * c4:128 * (c4 + 1)],
                                   rhs=onef[0:1, 0:1], start=True, stop=True)
                mm.then_inc(p.sem["te"], 1)
                te.wait_ge(p.sem["ve"], ve0 + 1)
                for c4 in range(4):
                    mm = te.matmul(pRo[:, 128 * c4:128 * (c4 + 1)], lhsT=kbcol[:, c4:c4 + 1],
                                   rhs=p.c["id_f"][:], start=True, stop=True)
                mm.then_inc(p.sem["te"], 1)
                te.wait_ge(p.sem["gq"], gq0 + 2)
                te.matmul(pkc[:], lhsT=kbrow16[:], rhs=p.c["pperm_sb"][:],
                          start=True, stop=True).then_inc(p.sem["te"], 1)
                p.cnt["te"] += 4

            @b.scalar
            def _(sc):
                sc.wait_ge(p.sem["te"], te0 + 1)
                sc.copy(impb[:], pB[:]).then_inc(p.sem["se"], 1)
                sc.wait_ge(p.sem["te"], te0 + 2)
                sc.copy(impc[:], pC[:]).then_inc(p.sem["se"], 1)
                sc.wait_ge(p.sem["te"], te0 + 3)
                sc.copy(kbg[:], pRo[:]).then_inc(p.sem["se"], 1)
                sc.wait_ge(p.sem["te"], te0 + 4)
                sc.copy(p.c["kbc_sb"][:], pkc[:]).then_inc(p.sem["se"], 1)
                p.cnt["se"] += 4

            @b.vector
            def _(v):
                v.wait_ge(p.sem["se"], se0 + 2)
                for c4 in range(4):
                    p.edge(v, v.tensor_scalar(out=jkf[:], in0=impb[:],
                                              scalar1=impc[:, c4:c4 + 1], scalar2=0.0,
                                              op0=ALU.is_gt, op1=ALU.add,
                                              accum_out=cg[:, c4:c4 + 1]), "vx")
                    p.edge(v, v.scalar_tensor_tensor(out=jkf[:], in0=impb[:],
                                                     scalar=impc[:, c4:c4 + 1],
                                                     in1=p.c["ltm_sb"][:, NVIS * c4:NVIS * (c4 + 1)],
                                                     op0=ALU.is_equal, op1=ALU.mult,
                                                     accum_out=ce[:, c4:c4 + 1]), "vx")
                p.edge(v, v.tensor_add(cg[:], cg[:], ce[:]), "vx")
                p.edge(v, v.tensor_scalar(out=cg[:], in0=cg[:], scalar1=float(KEEP) - 0.5,
                                          scalar2=None, op0=ALU.is_lt), "vx")
                p.edge(v, v.tensor_scalar(out=kbcol[:], in0=cg[:], scalar1=1.0, scalar2=-NEG,
                                          op0=ALU.subtract, op1=ALU.mult), "vx")
                v.nop().then_inc(p.sem["ve"], 1)
                p.cnt["ve"] += 1

            @b.gpsimd
            def _(g):
                g.memset(kbrow16[:], 0.0).then_inc(p.sem["gq"], 1)
                p.cnt["gq"] += 1
                g.wait_ge(p.sem["gq"], p.cnt["gq"])
                g.wait_ge(p.sem["se"], se0 + 3)
                for (dst, srcs) in (
                        (kbrow16[0:1, 1:128], kbg[0:1, 0:127]),
                        (kbrow16[1:2, 0:128], kbg[0:1, 127:255]),
                        (kbrow16[2:3, 0:128], kbg[0:1, 255:383]),
                        (kbrow16[3:4, 0:128], kbg[0:1, 383:511]),
                        (kbrow16[4:5, 0:1], kbg[0:1, 511:512])):
                    g.dma_start(out=dst, in_=srcs).then_inc(p.sem["S2"], 16)
                    p.cnt["S2"] += 16
                    g.wait_ge(p.sem["S2"], p.cnt["S2"])
                g.nop().then_inc(p.sem["gq"], 1)
                p.cnt["gq"] += 1
                if dbg.get("dbg_imp") is not None:
                    g.dma_start(out=dbg["dbg_imp"].ap(), in_=imp[:]).then_inc(p.sem["S2"], 16)
                    p.cnt["S2"] += 16
                    g.wait_ge(p.sem["S2"], p.cnt["S2"])
                if dbg.get("dbg_kb") is not None:
                    g.wait_ge(p.sem["se"], se0 + 4)
                    g.dma_start(out=dbg["dbg_kb"].ap(), in_=p.c["kbc_sb"][:]) \
                        .then_inc(p.sem["S2"], 16)
                    p.cnt["S2"] += 16
                    g.wait_ge(p.sem["S2"], p.cnt["S2"])
                if dbg.get("dbg_qt") is not None:
                    g.dma_start(out=dbg["dbg_qt"].ap(), in_=qsum[:]).then_inc(p.sem["S2"], 16)
                    p.cnt["S2"] += 16
                    g.wait_ge(p.sem["S2"], p.cnt["S2"])


def _p3_pside(p, x2pi_s, wq, wk, wv, q_sc, k_sc, vt_sc, costabs, dbg):
    nc = p.nc
    onef = p.c["onef3"]
    cq_t, ck_t, ct_t, cv_t = costabs
    x2pi_ap = x2pi_s.ap()
    with ExitStack() as st:
        rpb = st.enter_context(nc.sbuf_tensor("rpb", [128, S], F32))
        ck_sb = st.enter_context(nc.sbuf_tensor("ck_sb", [128, 2 * S], BF16))
        cq_sb = st.enter_context(nc.sbuf_tensor("cq_sb", [128, 2 * TOWN], BF16))
        hs3 = st.enter_context(nc.sbuf_tensor("hs3", [128, DCH * S], BF16))

        with ExitStack() as stn:
            nmpb = stn.enter_context(nc.sbuf_tensor("nmpb", [128, S], F32))

            te0, se0 = p.cnt["te"], p.cnt["se"]
            gl0 = p.cnt["gl"]
            with ExitStack() as st1, MB(p) as b:
                nm3pi = st1.enter_context(nc.sbuf_tensor("nm3pi", [1, S], F32))
                r3pi = st1.enter_context(nc.sbuf_tensor("r3pi", [1, S], F32))
                pbn = [st1.enter_context(nc.psum_tensor(f"ppb{i}", [128, 512], F32))
                       for i in range(4)]
                pbr = [st1.enter_context(nc.psum_tensor(f"ppr{i}", [128, 512], F32))
                       for i in range(4)]

                @b.gpsimd
                def _(g):
                    g.wait_ge(p.sem["S3"], p.gather_done)
                    def gchain(dst_ap, src_ap):
                        g.wait_ge(p.sem["gl"], p.cnt["gl"])
                        g.dma_start(out=dst_ap, in_=src_ap).then_inc(p.sem["gl"], 16)
                        p.cnt["gl"] += 16
                    gchain(nm3pi[:], x2pi_ap[D:D + 1, :])
                    gchain(r3pi[:], x2pi_ap[D + 1:D + 2, :])
                    gchain(ck_sb[:, 0:S], ck_t.ap()[0])
                    gchain(ck_sb[:, S:2 * S], ck_t.ap()[1])
                    gchain(cq_sb[:, 0:TOWN], cq_t.ap()[0])
                    gchain(cq_sb[:, TOWN:2 * TOWN], cq_t.ap()[1])
                    g.wait_ge(p.sem["gl"], p.cnt["gl"])

                @b.tensor
                def _(te):
                    te.wait_ge(p.sem["gl"], gl0 + 32)
                    for tc in range(4):
                        te.matmul(pbn[tc][:], lhsT=onef[:], rhs=nm3pi[0:1, 512 * tc:512 * (tc + 1)],
                                  start=True, stop=True).then_inc(p.sem["te"], 1)
                        te.matmul(pbr[tc][:], lhsT=onef[:], rhs=r3pi[0:1, 512 * tc:512 * (tc + 1)],
                                  start=True, stop=True).then_inc(p.sem["te"], 1)
                    p.cnt["te"] += 8

                @b.scalar
                def _(sc):
                    for tc in range(4):
                        cs = slice(512 * tc, 512 * (tc + 1))
                        sc.wait_ge(p.sem["te"], te0 + 2 * tc + 1)
                        sc.copy(nmpb[:, cs], pbn[tc][:]).then_inc(p.sem["se"], 1)
                        sc.wait_ge(p.sem["te"], te0 + 2 * tc + 2)
                        sc.copy(rpb[:, cs], pbr[tc][:]).then_inc(p.sem["se"], 1)
                    p.cnt["se"] += 8

            with ExitStack() as st2, MB(p) as b:
                rpbb = st2.enter_context(nc.sbuf_tensor("rpbb", [128, S], BF16))

                @b.vector
                def _(v):
                    p.edge(v, v.tensor_copy(rpbb[:], rpb[:]), "vx")
                    for tc in range(4):
                        cs = slice(512 * tc, 512 * (tc + 1))
                        cs2 = slice(S + 512 * tc, S + 512 * (tc + 1))
                        p.edge(v, v.tensor_mul(ck_sb[:, cs], ck_sb[:, cs], rpbb[:, cs]), "vx")
                        p.edge(v, v.tensor_mul(ck_sb[:, cs2], ck_sb[:, cs2], rpbb[:, cs]), "vx")
                    for tc in range(2):
                        cs = slice(512 * tc, 512 * (tc + 1))
                        cs2 = slice(TOWN + 512 * tc, TOWN + 512 * (tc + 1))
                        p.edge(v, v.tensor_mul(cq_sb[:, cs], cq_sb[:, cs], rpbb[:, cs]), "vx")
                        p.edge(v, v.tensor_mul(cq_sb[:, cs2], cq_sb[:, cs2], rpbb[:, cs]), "vx")
                    v.nop().then_inc(p.sem["ve"], 1)
                    p.cnt["ve"] += 1

            l0 = [p.cnt["L0"], p.cnt["L1"]]
            ve0 = p.cnt["ve"]
            with ExitStack() as st3, MB(p) as b:
                xin = [st3.enter_context(nc.sbuf_tensor(f"pxi{i}", [128, S], F32))
                       for i in range(2)]

                @b.sync
                def _(sy):
                    sy.wait_ge(p.sem["S3"], p.gather_done)
                    for dc in range(DCH):
                        if dc >= 2:
                            sy.wait_ge(p.sem["ve"], ve0 + (dc - 2) + 1)
                        sy.dma_start(out=xin[dc % 2][:], in_=x2pi_ap[128 * dc:128 * (dc + 1), :]) \
                            .then_inc(p.sem[f"L{dc % 2}"], 16)
                        p.cnt[f"L{dc % 2}"] += 16

                @b.vector
                def _(v):
                    for dc in range(DCH):
                        v.wait_ge(p.sem[f"L{dc % 2}"], l0[dc % 2] + 16 * (dc // 2 + 1))
                        v.tensor_add(hs3[:, S * dc:S * (dc + 1)], xin[dc % 2][:], nmpb[:]) \
                            .then_inc(p.sem["ve"], 1)
                    p.cnt["ve"] += DCH

        with ExitStack() as st4:
            wwb = [st4.enter_context(nc.sbuf_tensor(f"pww{i}", [128, D], BF16)) for i in range(2)]
            with ExitStack() as st5:
                ktmp = [st5.enter_context(nc.sbuf_tensor(f"pkt{i}", [128, 512], BF16))
                        for i in range(2)]
                e2b = st5.enter_context(nc.sbuf_tensor("pe2", [128, 512], BF16))
                kto = [st5.enter_context(nc.sbuf_tensor(f"pko{i}", [128, 512], BF16))
                       for i in range(2)]
                _proj_rope(p, wk, hs3, ck_sb, S, k_sc, S, "k",
                           kbuf=wwb, ktmp=ktmp, e2b=e2b, kto=kto)
                _proj_rope(p, wq, hs3, cq_sb, TOWN, q_sc, TOWN, "q",
                           kbuf=wwb, ktmp=ktmp, e2b=e2b, kto=kto)
            with ExitStack() as st6:
                vto = st6.enter_context(nc.sbuf_tensor("pvto", [128, 512], BF16))
                vtt = st6.enter_context(nc.sbuf_tensor("pvtt", [128, S], BF16))
                _proj_v(p, wv, hs3, rpb, vt_sc, wwb, vto, vtt)


def _proj_rope(p, wmat, hs3, tabs, sinoff, dst_sc, T, tag, kbuf, ktmp, e2b, kto):
    nc = p.nc
    NTC = T // 512
    gl0 = [p.cnt["L6"], p.cnt["L7"]]
    te0, se0, ve0 = p.cnt["te"], p.cnt["se"], p.cnt["ve"]
    s0 = [p.cnt["S0"], p.cnt["S1"]]
    with ExitStack() as pst, MB(p) as b:
        pK = [pst.enter_context(nc.psum_tensor(f"{tag}pK{i}", [128, 512], F32)) for i in range(2)]
        pR = [pst.enter_context(nc.psum_tensor(f"{tag}pR{i}", [128, 512], F32)) for i in range(2)]

        @b.gpsimd
        def _(g):
            def stores(ob):
                for tc in range(NTC):
                    j = NTC * ob + tc
                    g.wait_ge(p.sem["ve"], ve0 + j + 1)
                    g.dma_start(out=dst_sc.ap()[128 * ob:128 * (ob + 1), 512 * tc:512 * (tc + 1)],
                                in_=kto[j % 2][:]).then_inc(p.sem[f"S{j % 2}"], 16)
                    p.cnt[f"S{j % 2}"] += 16
            for ob in range(DCH):
                if ob >= 2:
                    g.wait_ge(p.sem["te"], te0 + 2 * NTC * (ob - 1))
                g.dma_start(out=kbuf[ob % 2][:], in_=wmat.ap()[ob]) \
                    .then_inc(p.sem[f"L{6 + ob % 2}"], 16)
                p.cnt[f"L{6 + ob % 2}"] += 16
                if ob >= 1:
                    stores(ob - 1)
            stores(DCH - 1)
            g.wait_ge(p.sem["S0"], p.cnt["S0"])
            g.wait_ge(p.sem["S1"], p.cnt["S1"])

        @b.tensor
        def _(te):
            for ob in range(DCH):
                te.wait_ge(p.sem[f"L{6 + ob % 2}"], gl0[ob % 2] + 16 * (ob // 2 + 1))
                for tc in range(NTC):
                    j = NTC * ob + tc
                    if j >= 2:
                        te.wait_ge(p.sem["se"], se0 + (j - 1))
                    for dc in range(DCH):
                        mm = te.matmul(pK[j % 2][:],
                                       lhsT=kbuf[ob % 2][:, 128 * dc:128 * (dc + 1)],
                                       rhs=hs3[:, S * dc + 512 * tc:S * dc + 512 * (tc + 1)],
                                       start=(dc == 0), stop=(dc == DCH - 1))
                    mm.then_inc(p.sem["te"], 1)
                    te.wait_ge(p.sem["se"], se0 + j + 1)
                    if j >= 2:
                        te.wait_ge(p.sem["ve"], ve0 + (j - 2) + 1)
                    mm = te.matmul(pR[j % 2][:], lhsT=p.c["rmat_sb"][:],
                                   rhs=ktmp[j % 2][:], start=True, stop=True)
                    mm.then_inc(p.sem["te"], 1)
            p.cnt["te"] += 2 * NTC * DCH

        @b.scalar
        def _(sc):
            for ob in range(DCH):
                for tc in range(NTC):
                    j = NTC * ob + tc
                    sc.wait_ge(p.sem["te"], te0 + 2 * j + 1)
                    if j >= 2:
                        sc.wait_ge(p.sem["te"], te0 + 2 * j)
                        sc.wait_ge(p.sem["ve"], ve0 + (j - 2) + 1)
                    sc.copy(ktmp[j % 2][:], pK[j % 2][:]).then_inc(p.sem["se"], 1)
            p.cnt["se"] += NTC * DCH

        @b.vector
        def _(v):
            for ob in range(DCH):
                for tc in range(NTC):
                    j = NTC * ob + tc
                    cs = slice(512 * tc, 512 * (tc + 1))
                    cs2 = slice(sinoff + 512 * tc, sinoff + 512 * (tc + 1))
                    v.wait_ge(p.sem["se"], se0 + j + 1)
                    if j >= 2:
                        v.wait_ge(p.sem[f"S{j % 2}"], s0[j % 2] + 16 * ((j - 2) // 2 + 1))
                    p.edge(v, v.tensor_mul(kto[j % 2][:], ktmp[j % 2][:], tabs[:, cs]), "vx")
                    v.wait_ge(p.sem["te"], te0 + 2 * j + 2)
                    p.edge(v, v.tensor_mul(e2b[:], pR[j % 2][:], tabs[:, cs2]), "vx")
                    p.edge(v, v.tensor_add(kto[j % 2][:], kto[j % 2][:], e2b[:]), "vx")
                    v.nop().then_inc(p.sem["ve"], 1)
            p.cnt["ve"] += NTC * DCH


def _proj_v(p, wv, hs3, rpb, vt_sc, kbuf, vto, vtt):
    nc = p.nc
    NTC = S // 512
    gl0 = [p.cnt["L6"], p.cnt["L7"]]
    te0, se0, ve0 = p.cnt["te"], p.cnt["se"], p.cnt["ve"]
    s0 = p.cnt["S0"]
    with ExitStack() as pst, MB(p) as b:
        pV = [pst.enter_context(nc.psum_tensor(f"vpV{i}", [128, 512], F32)) for i in range(2)]
        pT = [pst.enter_context(nc.psum_tensor(f"vpT{i}", [128, 128], BF16)) for i in range(2)]

        @b.gpsimd
        def _(g):
            def store(ob):
                g.wait_ge(p.sem["se"], se0 + 16 * (ob + 1))
                g.dma_start(out=vt_sc.ap()[ob], in_=vtt[:]).then_inc(p.sem["S0"], 16)
                p.cnt["S0"] += 16
                g.wait_ge(p.sem["S0"], p.cnt["S0"])
            for ob in range(DCH):
                if ob >= 2:
                    g.wait_ge(p.sem["te"], te0 + 20 * (ob - 1))
                g.dma_start(out=kbuf[ob % 2][:], in_=wv.ap()[ob]) \
                    .then_inc(p.sem[f"L{6 + ob % 2}"], 16)
                p.cnt[f"L{6 + ob % 2}"] += 16
                if ob >= 1:
                    store(ob - 1)
            store(DCH - 1)

        @b.tensor
        def _(te):
            for ob in range(DCH):
                te.wait_ge(p.sem[f"L{6 + ob % 2}"], gl0[ob % 2] + 16 * (ob // 2 + 1))
                for tc in range(NTC):
                    j = NTC * ob + tc
                    if j >= 2:
                        te.wait_ge(p.sem["ve"], ve0 + (j - 2) + 1)
                    for dc in range(DCH):
                        mm = te.matmul(pV[j % 2][:],
                                       lhsT=kbuf[ob % 2][:, 128 * dc:128 * (dc + 1)],
                                       rhs=hs3[:, S * dc + 512 * tc:S * dc + 512 * (tc + 1)],
                                       start=(dc == 0), stop=(dc == DCH - 1))
                    mm.then_inc(p.sem["te"], 1)
                    te.wait_ge(p.sem["ve"], ve0 + j + 1)
                    for ts4 in range(4):
                        jj = 4 * j + ts4
                        if jj >= 2:
                            te.wait_ge(p.sem["se"], se0 + (jj - 1))
                        te.transpose(pT[jj % 2][:], vto[:, 128 * ts4:128 * (ts4 + 1)], p.c["id_b"][:]) \
                            .then_inc(p.sem["te"], 1)
            p.cnt["te"] += (NTC * 5) * DCH

        @b.vector
        def _(v):
            for ob in range(DCH):
                for tc in range(NTC):
                    j = NTC * ob + tc
                    v.wait_ge(p.sem["te"], te0 + 20 * ob + 5 * tc + 1)
                    if j >= 1:
                        v.wait_ge(p.sem["se"], se0 + 16 * ob + 4 * tc)
                    v.tensor_mul(vto[:], pV[j % 2][:], rpb[:, 512 * tc:512 * (tc + 1)]) \
                        .then_inc(p.sem["ve"], 1)
            p.cnt["ve"] += NTC * DCH

        @b.scalar
        def _(sc):
            for ob in range(DCH):
                if ob >= 1:
                    sc.wait_ge(p.sem["S0"], s0 + 16 * ob)
                for tc in range(NTC):
                    j = NTC * ob + tc
                    for ts4 in range(4):
                        jj = 4 * j + ts4
                        sc.wait_ge(p.sem["te"], te0 + 20 * ob + 5 * tc + 1 + ts4 + 1)
                        sc.copy(vtt[:, 128 * (4 * tc + ts4):128 * (4 * tc + ts4 + 1)], pT[jj % 2][:]) \
                            .then_inc(p.sem["se"], 1)
            p.cnt["se"] += 4 * NTC * DCH




def _p4_attn(p, q_sc, k_sc, vt_sc, attn_sb):
    nc = p.nc
    chunks = []
    for hd in range(NH):
        for qc in range(4):
            jl = list(range(0, 2 * qc + 2)) + list(range(8, 8 + 2 * qc + 2))
            chunks.append({"hd": hd, "qc": qc, "jl": jl, "n": len(jl)})

    te0, se0, ve0 = p.cnt["te"], p.cnt["se"], p.cnt["ve"]
    lk = [p.cnt["L0"], p.cnt["L1"]]
    lq = p.cnt["L4"]
    lv = [p.cnt["S0"], p.cnt["S1"]]
    te, se, ve = te0, se0, ve0
    gk = 0
    gv = 0
    gs = 0
    seE_g = {}
    tePV_g = {}
    for t, r in enumerate(chunks):
        n = r["n"]
        r["t"] = t
        lq += 16
        r["lq"] = lq
        r["lk"] = []
        r["lv"] = []
        for i in range(n // 2):
            lk[gk % 2] += 16
            r["lk"].append((gk % 2, lk[gk % 2]))
            gk += 1
        for i in range(n // 2):
            lv[gv % 2] += 16
            r["lv"].append((gv % 2, lv[gv % 2]))
            gv += 1
        r["gs0"] = gs
        r["teS"] = []
        for i in range(n):
            te += 1
            r["teS"].append(te)
        r["seE"] = []
        for i in range(n):
            se += 1
            r["seE"].append(se)
            seE_g[gs + i] = se
        r["veM"] = {}
        for j in r["jl"]:
            if j in (2 * r["qc"], 2 * r["qc"] + 1, 8 + 2 * r["qc"], 9 + 2 * r["qc"]):
                ve += 1
                r["veM"][j] = ve
        r["tePV"] = []
        for i in range(n):
            te += 1
            r["tePV"].append(te)
            tePV_g[gs + i] = te
        gs += n
        se += 1
        r["seD"] = se
        te += 1
        r["teB"] = te
        se += 1
        r["seB"] = se
        ve += 1
        r["veF"] = ve
    p.cnt["te"] = te
    p.cnt["se"] = se
    p.cnt["ve"] = ve
    for s, v in (("L0", lk[0]), ("L1", lk[1]), ("L4", lq), ("S0", lv[0]), ("S1", lv[1])):
        p.cnt[s] = v

    qa, ka, va = q_sc.ap(), k_sc.ap(), vt_sc.ap()
    with ExitStack() as st, MB(p) as b:
        qtl = st.enter_context(nc.sbuf_tensor("a4q", [128, 256], BF16))
        ktl = [st.enter_context(nc.sbuf_tensor(f"a4k{i}", [128, 256], BF16)) for i in range(2)]
        vtl = [st.enter_context(nc.sbuf_tensor(f"a4v{i}", [128, 256], BF16)) for i in range(2)]
        ptl = st.enter_context(nc.sbuf_tensor("a4p", [128, 16 * 256], BF16))
        dnr = st.enter_context(nc.sbuf_tensor("a4d", [1, 256], F32))
        denb = [st.enter_context(nc.sbuf_tensor(f"a4db{i}", [128, 256], F32)) for i in range(2)]
        pS = [st.enter_context(nc.psum_tensor(f"a4pS{i}", [128, 512], F32)) for i in range(2)]
        pD = [st.enter_context(nc.psum_tensor(f"a4pD{i}", [1, 512], F32)) for i in range(2)]
        pO = [st.enter_context(nc.psum_tensor(f"a4pO{i}", [128, 512], F32)) for i in range(2)]
        pB = [st.enter_context(nc.psum_tensor(f"a4pB{i}", [128, 512], F32)) for i in range(2)]

        @b.sync
        def _(sy):
            for r in chunks:
                t, hd, qc, jl, n = r["t"], r["hd"], r["qc"], r["jl"], r["n"]
                if t >= 1:
                    sy.wait_ge(p.sem["te"], chunks[t - 1]["teS"][-1])
                sy.dma_start(out=qtl[:],
                             in_=qa[128 * hd:128 * (hd + 1), 256 * qc:256 * (qc + 1)]) \
                    .then_inc(p.sem["L4"], 16)
                for pl in range(n // 2):
                    bk, val = r["lk"][pl]
                    gki = r["gs0"] // 2 + pl
                    if gki >= 2:
                        sy.wait_ge(p.sem["te"], _teS_abs(chunks, 2 * (gki - 2) + 1))
                    j0 = jl[2 * pl]
                    sy.dma_start(out=ktl[bk][:],
                                 in_=ka[128 * hd:128 * (hd + 1), 128 * j0:128 * j0 + 256]) \
                        .then_inc(p.sem[f"L{bk}"], 16)

        @b.gpsimd
        def _(g):
            for r in chunks:
                t, hd, qc, jl, n = r["t"], r["hd"], r["qc"], r["jl"], r["n"]
                for pl in range(n // 2):
                    bk, val = r["lv"][pl]
                    gvi = r["gs0"] // 2 + pl
                    if gvi >= 2:
                        g.wait_ge(p.sem["te"], tePV_g[2 * (gvi - 2) + 1])
                    j0 = jl[2 * pl]
                    g.dma_start(out=vtl[bk][:],
                                in_=va[hd][:, 128 * j0:128 * j0 + 256]) \
                        .then_inc(p.sem[f"S{bk}"], 16)

        @b.tensor
        def _(te_):
            for r in chunks:
                t, hd, qc, jl, n = r["t"], r["hd"], r["qc"], r["jl"], r["n"]
                gs0 = r["gs0"]
                for i in range(n):
                    if i == 0:
                        te_.wait_ge(p.sem["L4"], r["lq"])
                    bk, val = r["lk"][i // 2]
                    te_.wait_ge(p.sem[f"L{bk}"], val)
                    gsi = gs0 + i
                    if gsi >= 2:
                        te_.wait_ge(p.sem["se"], seE_g[gsi - 2])
                    te_.matmul(pS[gsi % 2][:, 0:256],
                               lhsT=ktl[(gsi // 2) % 2][:, 128 * (i % 2):128 * (i % 2) + 128],
                               rhs=qtl[:], start=True, stop=True).then_inc(p.sem["te"], 1)
                for i in range(n):
                    j = jl[i]
                    if j in r["veM"]:
                        te_.wait_ge(p.sem["ve"], r["veM"][j])
                    else:
                        te_.wait_ge(p.sem["se"], r["seE"][i])
                    if i == 0 and t >= 2:
                        te_.wait_ge(p.sem["se"], chunks[t - 2]["seD"])
                        te_.wait_ge(p.sem["ve"], chunks[t - 2]["veF"])
                    te_.matmul(pD[t % 2][0:1, 0:256], lhsT=p.c["ones_col"][:],
                               rhs=ptl[:, 256 * i:256 * (i + 1)],
                               start=(i == 0), stop=(i == n - 1))
                    bk, val = r["lv"][i // 2]
                    te_.wait_ge(p.sem[f"S{bk}"], val)
                    te_.matmul(pO[t % 2][:, 0:256],
                               lhsT=vtl[(r["gs0"] // 2 + i // 2) % 2][:, 128 * (i % 2):128 * (i % 2) + 128],
                               rhs=ptl[:, 256 * i:256 * (i + 1)],
                               start=(i == 0), stop=(i == n - 1)).then_inc(p.sem["te"], 1)
                te_.wait_ge(p.sem["se"], r["seD"])
                if t >= 2:
                    te_.wait_ge(p.sem["se"], chunks[t - 2]["seB"])
                te_.matmul(pB[t % 2][:, 0:256], lhsT=p.c["onef3"][:], rhs=dnr[:],
                           start=True, stop=True).then_inc(p.sem["te"], 1)

        @b.scalar
        def _(sc):
            for r in chunks:
                t, qc, jl, n = r["t"], r["qc"], r["jl"], r["n"]
                for i in range(n):
                    j = jl[i]
                    sc.wait_ge(p.sem["te"], r["teS"][i])
                    sc.activation(out=ptl[:, 256 * i:256 * (i + 1)],
                                  in_=pS[(r["gs0"] + i) % 2][:, 0:256], func=AF.Exp,
                                  scale=ATT_SCALE, bias=p.c["kbc_sb"][:, j:j + 1]) \
                        .then_inc(p.sem["se"], 1)
                sc.wait_ge(p.sem["te"], r["tePV"][-1])
                if t >= 1:
                    sc.wait_ge(p.sem["te"], chunks[t - 1]["teB"])
                p.edge(sc, sc.copy(dnr[:], pD[t % 2][0:1, 0:256]), "sx")
                p.edge(sc, sc.activation(out=dnr[:], in_=dnr[:], func=AF.Ln), "sx")
                sc.activation(out=dnr[:], in_=dnr[:], func=AF.Exp, scale=-1.0) \
                    .then_inc(p.sem["se"], 1)
                sc.wait_ge(p.sem["te"], r["teB"])
                if t >= 2:
                    sc.wait_ge(p.sem["ve"], chunks[t - 2]["veF"])
                sc.copy(denb[t % 2][:], pB[t % 2][:, 0:256]).then_inc(p.sem["se"], 1)

        @b.vector
        def _(v):
            tri = p.c["tri_sb"]
            mp = p.c["mpar_sb"]
            for r in chunks:
                t, hd, qc, jl, n = r["t"], r["hd"], r["qc"], r["jl"], r["n"]
                for i in range(n):
                    j = jl[i]
                    if j not in r["veM"]:
                        continue
                    base = 256 * i
                    h1 = slice(base, base + 128)
                    h2 = slice(base + 128, base + 256)
                    v.wait_ge(p.sem["se"], r["seE"][i])
                    if j == 2 * qc:
                        p.edge(v, v.tensor_mul(ptl[:, h1], ptl[:, h1], tri[:]), "vx")
                    elif j == 2 * qc + 1:
                        p.edge(v, v.memset(ptl[:, h1], 0.0), "vx")
                        p.edge(v, v.tensor_mul(ptl[:, h2], ptl[:, h2], tri[:]), "vx")
                    elif j == 8 + 2 * qc:
                        p.edge(v, v.tensor_mul(ptl[:, h1], ptl[:, h1], mp[:, 0:128]), "vx")
                    else:
                        p.edge(v, v.memset(ptl[:, h1], 0.0), "vx")
                        p.edge(v, v.tensor_mul(ptl[:, h2], ptl[:, h2], mp[:, 0:128]), "vx")
                    v.nop().then_inc(p.sem["ve"], 1)
                v.wait_ge(p.sem["se"], r["seB"])
                v.tensor_mul(attn_sb[:, hd * TOWN + 256 * qc:hd * TOWN + 256 * (qc + 1)],
                             pO[t % 2][:, 0:256], denb[t % 2][:]).then_inc(p.sem["ve"], 1)


def _teS_abs(chunks, gsi):
    for r in chunks:
        if r["gs0"] <= gsi < r["gs0"] + r["n"]:
            return r["teS"][gsi - r["gs0"]] - 0
    raise KeyError(gsi)



def _p5_oproj(p, wo, attn_sb, x2pi_s, x4_s):
    nc = p.nc
    gl0 = [p.cnt["L6"], p.cnt["L7"]]
    l0 = [p.cnt["L0"], p.cnt["L1"]]
    s0 = [p.cnt["S0"], p.cnt["S1"]]
    te0, ve0 = p.cnt["te"], p.cnt["ve"]
    x2a = x2pi_s.ap()
    with ExitStack() as st, MB(p) as b:
        owb = [st.enter_context(nc.sbuf_tensor(f"o5w{i}", [128, D], BF16)) for i in range(2)]
        xres = [st.enter_context(nc.sbuf_tensor(f"o5x{i}", [128, 512], F32)) for i in range(2)]
        x4o = [st.enter_context(nc.sbuf_tensor(f"o5o{i}", [128, 512], F32)) for i in range(2)]
        pOut = [st.enter_context(nc.psum_tensor(f"o5p{i}", [128, 512], F32)) for i in range(2)]

        @b.gpsimd
        def _(g):
            def stores(jj):
                ob, tc = divmod(jj, 2)
                g.wait_ge(p.sem["ve"], ve0 + jj + 1)
                g.dma_start(out=x4_s.ap()[128 * ob:128 * (ob + 1), 512 * tc:512 * (tc + 1)],
                            in_=x4o[jj % 2][:]).then_inc(p.sem[f"S{jj % 2}"], 16)
                p.cnt[f"S{jj % 2}"] += 16
            for ob in range(DCH):
                if ob >= 2:
                    g.wait_ge(p.sem["te"], te0 + 2 * ob - 2)
                g.dma_start(out=owb[ob % 2][:], in_=wo.ap()[ob]) \
                    .then_inc(p.sem[f"L{6 + ob % 2}"], 16)
                p.cnt[f"L{6 + ob % 2}"] += 16
                if ob >= 1:
                    stores(2 * (ob - 1))
                    stores(2 * (ob - 1) + 1)
            stores(2 * (DCH - 1))
            stores(2 * (DCH - 1) + 1)
            g.wait_ge(p.sem["S0"], p.cnt["S0"])
            g.wait_ge(p.sem["S1"], p.cnt["S1"])

        @b.sync
        def _(sy):
            for jj in range(2 * DCH):
                ob, tc = divmod(jj, 2)
                if jj >= 2:
                    sy.wait_ge(p.sem["ve"], ve0 + (jj - 2) + 1)
                sy.dma_start(out=xres[jj % 2][:],
                             in_=x2a[128 * ob:128 * (ob + 1), 512 * tc:512 * (tc + 1)]) \
                    .then_inc(p.sem[f"L{jj % 2}"], 16)
                p.cnt[f"L{jj % 2}"] += 16

        @b.tensor
        def _(te):
            for jj in range(2 * DCH):
                ob, tc = divmod(jj, 2)
                if tc == 0:
                    te.wait_ge(p.sem[f"L{6 + ob % 2}"], gl0[ob % 2] + 16 * (ob // 2 + 1))
                if jj >= 2:
                    te.wait_ge(p.sem["ve"], ve0 + (jj - 2) + 1)
                for dc in range(DCH):
                    mm = te.matmul(pOut[jj % 2][:],
                                   lhsT=owb[ob % 2][:, 128 * dc:128 * (dc + 1)],
                                   rhs=attn_sb[:, dc * TOWN + 512 * tc:dc * TOWN + 512 * (tc + 1)],
                                   start=(dc == 0), stop=(dc == DCH - 1))
                mm.then_inc(p.sem["te"], 1)
            p.cnt["te"] += 2 * DCH

        @b.vector
        def _(v):
            for jj in range(2 * DCH):
                v.wait_ge(p.sem["te"], te0 + jj + 1)
                v.wait_ge(p.sem[f"L{jj % 2}"], l0[jj % 2] + 16 * (jj // 2 + 1))
                if jj >= 2:
                    v.wait_ge(p.sem[f"S{jj % 2}"], s0[jj % 2] + 16 * ((jj - 2) // 2 + 1))
                v.scalar_tensor_tensor(out=x4o[jj % 2][:], in0=pOut[jj % 2][:], scalar=1.0,
                                       in1=xres[jj % 2][:], op0=ALU.mult, op1=ALU.add) \
                    .then_inc(p.sem["ve"], 1)
            p.cnt["ve"] += 2 * DCH




def build(debug=(), phases=99):
    p = Prog(debug=debug)
    nc = p.nc

    xnat5 = nc.dram_tensor("xnat5", [5, S, D], F32, kind="ExternalInput")
    xcv2 = nc.dram_tensor("xcv2", [2, D, S], F32, kind="ExternalInput")
    wq = nc.dram_tensor("wq", [DCH, 128, D], F32, kind="ExternalInput")
    wk = nc.dram_tensor("wk", [DCH, 128, D], F32, kind="ExternalInput")
    wv = nc.dram_tensor("wv", [DCH, 128, D], F32, kind="ExternalInput")
    wo = nc.dram_tensor("wo", [DCH, 128, D], F32, kind="ExternalInput")
    wa = nc.dram_tensor("wa", [128, D], F32, kind="ExternalInput")
    wb = nc.dram_tensor("wb", [HR, D], F32, kind="ExternalInput")
    wr = nc.dram_tensor("wr", [128, NEXP * DCH], F32, kind="ExternalInput")
    brv = nc.dram_tensor("brv", [NEXP, 1], F32, kind="ExternalInput")
    cq = nc.dram_tensor("cq", [2, 128, TOWN], F32, kind="ExternalInput")
    ck = nc.dram_tensor("ck", [2, 128, S], F32, kind="ExternalInput")
    ct = nc.dram_tensor("ct", [2, 128, NTEXT], F32, kind="ExternalInput")
    cv = nc.dram_tensor("cv", [2, 128, NVIS], F32, kind="ExternalInput")
    rmat = nc.dram_tensor("rmat", [128, 128], F32, kind="ExternalInput")
    mpar = nc.dram_tensor("mpar", [128, 256], F32, kind="ExternalInput")
    ltm = nc.dram_tensor("ltm", [128, 4 * NVIS], F32, kind="ExternalInput")
    hsel = nc.dram_tensor("hsel", [1, 2], mybir.dt.uint32, kind="ExternalInput")
    pperm = nc.dram_tensor("pperm", [16, 16], BF16, kind="ExternalInput")

    outv = nc.dram_tensor("outv", [D, TOWN], F32, kind="ExternalOutput")

    lnnat_s = nc.dram_tensor("lnnat_s", [5, S, D], BF16)
    x1_s = nc.dram_tensor("x1_s", [2, D, S], F32)
    hs2_s = nc.dram_tensor("hs2_s", [2, D, S], BF16)
    x2_s = nc.dram_tensor("x2_s", [2, D + 2, S], F32)
    x2pi_s = nc.dram_tensor("x2pi_s", [D + 2, S], F32)
    q_sc = nc.dram_tensor("q_sc", [D, TOWN], BF16)
    k_sc = nc.dram_tensor("k_sc", [D, S], BF16)
    vt_sc = nc.dram_tensor("vt_sc", [NH, 128, (S // 128) * DH], BF16)
    x4_s = nc.dram_tensor("x4_s", [D, TOWN], F32)
    hs4_s = nc.dram_tensor("hs4_s", [D, TOWN], BF16)

    p.sbuf("wa_sb", [128, D], BF16)
    p.sbuf("wb_sb", [128, D], BF16)
    p.sbuf("wr_sb", [128, NEXP * DCH], BF16)
    p.sbuf("br_sb", [NEXP, 1], F32)
    p.sbuf("e8_sb", [NEXP, HR], BF16)
    p.sbuf("i8_sb", [NEXP, NEXP], F32)
    p.sbuf("id_f", [128, 128], F32)
    p.sbuf("id_b", [128, 128], BF16)
    p.sbuf("ones_col", [128, 1], BF16)
    p.sbuf("ones_row", [1, 128], BF16)
    p.sbuf("one1", [1, 1], BF16)
    p.sbuf("rmat_sb", [128, 128], BF16)
    p.sbuf("mpar_sb", [128, 256], BF16)
    p.sbuf("tri_sb", [128, 128], BF16)
    p.sbuf("ltm_sb", [128, 4 * NVIS], F32)
    p.sbuf("epscol", [128, 1], F32)
    p.sbuf("onef3", [1, 128], F32)
    p.sbuf("hsel_sb", [1, 2], mybir.dt.uint32)
    p.sbuf("kbc_sb", [128, 16], F32)
    p.sbuf("pperm_sb", [16, 16], BF16)

    for s in ("ld", "gl", "te", "ve", "se", "stq", "gq", "sx", "vx",
              "L0", "L1", "L2", "L3", "L4", "L5", "L6", "L7",
              "S0", "S1", "S2", "S3"):
        p.semaphore(s)

    dbg_x1 = p.dbg("dbg_x1", [2 * D, S], F32)
    dbg_x2 = p.dbg("dbg_x2", [2 * D, S], F32)
    dbg3 = {
        "dbg_qt": p.dbg("dbg_qt", [128, DCH], F32),
        "dbg_imp": p.dbg("dbg_imp", [1, NVIS], F32),
        "dbg_kb": p.dbg("dbg_kb", [128, 16], F32),
    }
    dbg_q = p.dbg("dbg_q", [D, TOWN], BF16)
    dbg_k = p.dbg("dbg_k", [D, S], BF16)
    dbg_vt = p.dbg("dbg_vt", [NH * 128, (S // 128) * DH], BF16)
    dbg_prob = p.dbg("dbg_prob", [2, 128, 128], F32)
    dbg_at = p.dbg("dbg_at", [128, NH * TOWN], BF16)
    dbg_x4 = p.dbg("dbg_x4", [D, TOWN], F32)
    dbg_ln = p.dbg("dbg_ln", [5 * S, D], BF16)

    with MB(p) as b:
        @b.sync
        def _(sy):
            def chain(dst_ap, src_ap):
                sy.wait_ge(p.sem["ld"], p.cnt["ld"])
                sy.dma_start(out=dst_ap, in_=src_ap).then_inc(p.sem["ld"], 16)
                p.cnt["ld"] += 16
            chain(p.c["br_sb"][:], brv.ap())
            chain(p.c["ltm_sb"][:], ltm.ap())
            chain(p.c["hsel_sb"][:], hsel.ap())
            sy.wait_ge(p.sem["ld"], p.cnt["ld"])

        @b.gpsimd
        def _(g):
            def gchain(dst_ap, src_ap):
                g.wait_ge(p.sem["gl"], p.cnt["gl"])
                g.dma_start(out=dst_ap, in_=src_ap).then_inc(p.sem["gl"], 16)
                p.cnt["gl"] += 16
            gchain(p.c["wa_sb"][:], wa.ap())
            gchain(p.c["wb_sb"][:], wb.ap())
            gchain(p.c["wr_sb"][:], wr.ap())
            gchain(p.c["rmat_sb"][:], rmat.ap())
            gchain(p.c["mpar_sb"][:], mpar.ap())
            gchain(p.c["pperm_sb"][:], pperm.ap())
            g.wait_ge(p.sem["gl"], p.cnt["gl"])
            id_f, tri, e8, i8 = p.c["id_f"], p.c["tri_sb"], p.c["e8_sb"], p.c["i8_sb"]
            def step(instr):
                p.cnt["gq"] += 1
                instr.then_inc(p.sem["gq"], 1)
                g.wait_ge(p.sem["gq"], p.cnt["gq"])
            step(g.memset(id_f[:], 1.0))
            step(g.affine_select(out=id_f[:], in_=id_f[:], compare_op=ALU.is_ge, fill=0.0,
                                 base=0, pattern=[[-1, 128]], channel_multiplier=1))
            step(g.affine_select(out=id_f[:], in_=id_f[:], compare_op=ALU.is_ge, fill=0.0,
                                 base=0, pattern=[[1, 128]], channel_multiplier=-1))
            step(g.memset(tri[:], 1.0))
            step(g.affine_select(out=tri[:], in_=tri[:], compare_op=ALU.is_ge, fill=0.0,
                                 base=0, pattern=[[1, 128]], channel_multiplier=-1))
            step(g.memset(e8[:], 1.0))
            step(g.affine_select(out=e8[:], in_=e8[:], compare_op=ALU.is_ge, fill=0.0,
                                 base=0, pattern=[[1, HR]], channel_multiplier=-16))
            step(g.affine_select(out=e8[:], in_=e8[:], compare_op=ALU.is_ge, fill=0.0,
                                 base=15, pattern=[[-1, HR]], channel_multiplier=16))
            step(g.memset(i8[:], 1.0))
            step(g.affine_select(out=i8[:], in_=i8[:], compare_op=ALU.is_ge, fill=0.0,
                                 base=0, pattern=[[-1, NEXP]], channel_multiplier=1))
            step(g.affine_select(out=i8[:], in_=i8[:], compare_op=ALU.is_ge, fill=0.0,
                                 base=0, pattern=[[1, NEXP]], channel_multiplier=-1))
            step(g.memset(p.c["epscol"][:], EPS))
            step(g.memset(p.c["onef3"][:], 1.0))
            step(g.memset(p.c["ones_col"][:], 1.0))
            step(g.memset(p.c["ones_row"][:], 1.0))
            step(g.memset(p.c["one1"][:], 1.0))

        @b.vector
        def _(v):
            v.wait_ge(p.sem["gq"], p.cnt["gq"])
            v.tensor_copy(p.c["id_b"][:], p.c["id_f"][:]).then_inc(p.sem["ve"], 1)
            p.cnt["ve"] += 1

    if phases >= 1:
        _p1_lnnat(p, xnat5, lnnat_s)
        if dbg_ln is not None:
            _copy_dram(p, dbg_ln.ap(),
                       lnnat_s.ap().rearrange("a b c -> (a b) c"))
    if phases >= 2:
        _p1_gram_msg(p, lnnat_s, xcv2, x1_s, dbg_prob)
        if dbg_x1 is not None:
            _copy_dram(p, dbg_x1.ap(), x1_s.ap().rearrange("a b c -> (a b) c"))

    if phases >= 3:
        for u in range(2):
            _ffn(p, x1_s.ap()[u], x2_s.ap()[u], hs2_s.ap()[u], S, f"2{u}")
        if dbg_x2 is not None:
            for u in range(2):
                _copy_dram(p, dbg_x2.ap()[u * D:(u + 1) * D, :], x2_s.ap()[u][0:D, :])

    if phases >= 4:
        _p3(p, x2_s, x2pi_s, (cq, ck, ct, cv), wq, wk, wv, q_sc, k_sc, vt_sc, hsel, dbg3)
        if dbg_q is not None:
            _copy_dram(p, dbg_q.ap(), q_sc.ap())
        if dbg_k is not None:
            _copy_dram(p, dbg_k.ap(), k_sc.ap())
        if dbg_vt is not None:
            _copy_dram(p, dbg_vt.ap(), vt_sc.ap().rearrange("a b c -> (a b) c"))

    if phases >= 5:
        with ExitStack() as stA:
            attn_sb = stA.enter_context(nc.sbuf_tensor("attn_sb", [128, NH * TOWN], BF16))
            _p4_attn(p, q_sc, k_sc, vt_sc, attn_sb)
            if dbg_at is not None:
                with MB(p) as b:
                    @b.gpsimd
                    def _(g):
                        g.dma_start(out=dbg_at.ap(), in_=attn_sb[:]).then_inc(p.sem["S2"], 16)
                        p.cnt["S2"] += 16
                        g.wait_ge(p.sem["S2"], p.cnt["S2"])
            if phases >= 6:
                _p5_oproj(p, wo, attn_sb, x2pi_s, x4_s)
        if phases >= 6 and dbg_x4 is not None:
            _copy_dram(p, dbg_x4.ap(), x4_s.ap())

    if phases >= 7:
        _ffn(p, x4_s.ap(), outv.ap(), hs4_s.ap(), TOWN, "4")

    if phases < 7:
        with MB(p) as b:
            @b.gpsimd
            def _(g):
                g.memset(p.c["id_f"][:, 0:1], 0.0)
        with MB(p) as b:
            @b.sync
            def _(sy):
                for i in range(DCH):
                    sy.dma_start(out=outv.ap()[128 * i:128 * (i + 1), 0:128],
                                 in_=p.c["id_f"][:]).then_inc(p.sem["stq"], 16)
                p.cnt["stq"] += 16 * DCH
                sy.wait_ge(p.sem["stq"], p.cnt["stq"])

    p.st.close()
    return p



def _stage_inputs(L0_attn_a_x, L0_attn_b_q_w, L0_attn_b_k_w, L0_attn_b_v_w,
                  L0_attn_b_o_w, L0_ffn_A, L0_ffn_B, L0_ffn_Wr, L0_ffn_br):
    x = np.asarray(L0_attn_a_x, dtype=np.float32)
    wq_p = _prep_wT(L0_attn_b_q_w)
    wk_p = _prep_wT(L0_attn_b_k_w)
    wv_p = _prep_wT(L0_attn_b_v_w)
    wo_p = _prep_wT(L0_attn_b_o_w)
    A = np.asarray(L0_ffn_A, dtype=np.float32)
    wa_p = A.reshape(HR, D).T.reshape(DCH, 128, HR)
    wa_p = np.ascontiguousarray(wa_p.transpose(1, 0, 2)).reshape(128, D)
    Bm = np.asarray(L0_ffn_B, dtype=np.float32)
    wb_p = np.ascontiguousarray(Bm.transpose(0, 2, 1).reshape(HR, D))
    Wr = np.asarray(L0_ffn_Wr, dtype=np.float32)
    wr_p = Wr.T.reshape(DCH, 128, NEXP)
    wr_p = np.ascontiguousarray(wr_p.transpose(1, 0, 2)).reshape(128, NEXP * DCH)
    br_p = np.asarray(L0_ffn_br, dtype=np.float32).reshape(NEXP, 1)

    rmat_p = np.zeros((128, 128), dtype=np.float32)
    for d in range(64):
        rmat_p[64 + d, d] = -1.0
        rmat_p[d, 64 + d] = 1.0

    ct_c, ct_s = _rope_cos_sin(np.arange(TEXT_LO, TEXT_HI))
    cv_c, cv_s = _rope_cos_sin(np.arange(VIS_LO, VIS_HI))
    ltm_p = np.zeros((4, 128, NVIS), dtype=np.float32)
    for ch in range(4):
        for pp in range(128):
            v = ch * 128 + pp
            ltm_p[ch, pp, :v] = 1.0
    ltm_p = np.ascontiguousarray(ltm_p.transpose(1, 0, 2)).reshape(128, 4 * NVIS)

    base = {
        "wq": wq_p, "wk": wk_p, "wv": wv_p, "wo": wo_p,
        "wa": wa_p, "wb": wb_p, "wr": wr_p, "brv": br_p,
        "ct": np.stack([ct_c, ct_s]), "cv": np.stack([cv_c, cv_s]),
        "rmat": rmat_p, "ltm": ltm_p,
    }

    in_maps = []
    for c in range(NCORE):
        P = c // 2
        h = c % 2
        perm = _pi_perm(h)
        xn5 = np.stack([x[P], x[0], x[1], x[2], x[3]])
        xcv = np.stack([np.ascontiguousarray(x[P].T),
                        np.ascontiguousarray(x[0].T)])
        cq_c, cq_s = _rope_cos_sin(perm[:TOWN])
        ck_c, ck_s = _rope_cos_sin(perm)
        mpar_p = np.full((128, 256), float(h), dtype=np.float32)
        pperm_p = np.zeros((16, 16), dtype=ml_dtypes.bfloat16)
        for j in range(16):
            pperm_p[2 * (j % 8) + (h if j < 8 else 1 - h), j] = 1.0
        m = dict(base)
        m.update({
            "xnat5": np.ascontiguousarray(xn5),
            "xcv2": xcv,
            "cq": np.stack([cq_c, cq_s]),
            "ck": np.stack([ck_c, ck_s]),
            "mpar": mpar_p,
            "hsel": np.array([[h, 1 - h]], dtype=np.uint32),
            "pperm": pperm_p,
        })
        in_maps.append(m)
    return in_maps


def kernel(L0_attn_a_x, L0_attn_b_q_w, L0_attn_b_k_w, L0_attn_b_v_w,
           L0_attn_b_o_w, L0_ffn_A, L0_ffn_B, L0_ffn_Wr, L0_ffn_br,
           _debug=(), _phases=99, _return_raw=False):
    in_maps = _stage_inputs(L0_attn_a_x, L0_attn_b_q_w, L0_attn_b_k_w, L0_attn_b_v_w,
                            L0_attn_b_o_w, L0_ffn_A, L0_ffn_B, L0_ffn_Wr, L0_ffn_br)
    p = build(debug=_debug, phases=_phases)
    res = run_bass_kernel_spmd(p.nc, in_maps, core_ids=list(range(NCORE)))
    if _return_raw:
        return res
    y = np.empty((B, S, D), dtype=np.float32)
    for c in range(NCORE):
        P = c // 2
        h = c % 2
        perm = _pi_perm(h)[:TOWN]
        y[P, perm, :] = res.results[c]["outv"].T
    return y


# version 04 (final)
# speedup vs baseline: 1.0145x; 1.0145x over previous
import math
from contextlib import ExitStack

import numpy as np
import ml_dtypes

import concourse.bass as bass
import concourse.mybir as mybir
from concourse.bass_utils import run_bass_kernel_spmd

F32 = mybir.dt.float32
BF16 = mybir.dt.bfloat16
AF = mybir.ActivationFunctionType
ALU = mybir.AluOpType
AX = mybir.AxisListType

D = 4096
S = 2048
B = 4
NCORE = 8
TOWN = 1024
NH = 32
DH = 128
HR = 128
NEXP = 8
VIS_LO, VIS_HI = 1, 513
TEXT_LO, TEXT_HI = 513, 1992
NTEXT = TEXT_HI - TEXT_LO
NVIS = VIS_HI - VIS_LO
KEEP = 256
EPS = 1e-5
ROPE_THETA = 10000.0
GRAM_SCALE = 1.0 / math.sqrt(2048.0)
ATT_SCALE = 1.0 / math.sqrt(128.0)
FFN_FACTOR = 16.0
NEG = -1.0e30
DCH = D // 128
GRAM_LBS = DCH
GRAM_SUB = 3



def _pi_blocks(h):
    return [2 * i + h for i in range(8)] + [2 * i + (1 - h) for i in range(8)]


def _pi_perm(h):
    return np.concatenate([np.arange(128) + 128 * g for g in _pi_blocks(h)])


def _rope_cos_sin(positions):
    inv = 1.0 / (ROPE_THETA ** (np.arange(0, DH, 2, dtype=np.float64) / DH))
    ang = inv[:, None] * np.asarray(positions, dtype=np.float64)[None, :]
    ang = np.concatenate([ang, ang], axis=0)
    return (np.cos(ang).astype(np.float32), np.sin(ang).astype(np.float32))


def _prep_wT(w):
    a = np.ascontiguousarray(np.asarray(w, dtype=np.float32).T)
    a = a.reshape(DCH, 128, DCH, 128)
    a = np.ascontiguousarray(a.transpose(2, 1, 0, 3))
    return np.ascontiguousarray(a.reshape(DCH, 128, D))



class MB:

    def __init__(self, p):
        self.p = p

    def __enter__(self):
        self.blk = self.p.nc.Block()
        self.b = self.blk.__enter__()
        return self.b

    def __exit__(self, *a):
        r = self.blk.__exit__(*a)
        if a[0] is None:
            self.p.nc.all_engine_barrier()
        return r


class Prog:
    def __init__(self, debug=()):
        self.nc = bass.Bass("TRN2", target_bir_lowering=False)
        self.debug = set(debug)
        self.st = ExitStack()
        self.sem = {}
        self.cnt = {}
        self.c = {}

    def semaphore(self, name):
        self.sem[name] = self.st.enter_context(self.nc.semaphore(name))
        self.cnt[name] = 0

    def sbuf(self, name, shape, dtype=F32):
        t = self.st.enter_context(self.nc.sbuf_tensor(name, shape, dtype))
        self.c[name] = t
        return t

    def dbg(self, name, shape, dtype=F32):
        if name in self.debug:
            return self.nc.dram_tensor(name, shape, dtype, kind="ExternalOutput")
        return None

    def edge(self, eng, instr, sem):
        self.cnt[sem] += 1
        instr.then_inc(self.sem[sem], 1)
        eng.wait_ge(self.sem[sem], self.cnt[sem])


def _store(p, sy_or_g, dst_ap, src_ap):
    return sy_or_g.dma_start(out=dst_ap, in_=src_ap)


def _copy_dram(p, dst_ap, src_ap):
    nc = p.nc
    with MB(p) as b:
        @b.sync
        def _(sy):
            sy.dma_start(out=dst_ap, in_=src_ap).then_inc(p.sem["stq"], 16)
            p.cnt["stq"] += 16
            sy.wait_ge(p.sem["stq"], p.cnt["stq"])



def _p1_lnnat(p, xnat5, lnnat_s):
    nc = p.nc
    xa = xnat5.ap()
    la = lnnat_s.ap()
    NT = 5 * (S // 128)
    with ExitStack() as st:
        xin = [st.enter_context(nc.sbuf_tensor(f"p1xin{i}", [128, D], F32)) for i in range(2)]
        lno = [st.enter_context(nc.sbuf_tensor(f"p1lno{i}", [128, D], BF16)) for i in range(2)]
        junkv = st.enter_context(nc.sbuf_tensor("p1junkv", [128, D], BF16))
        junks = st.enter_context(nc.sbuf_tensor("p1junks", [128, D], BF16))
        s1 = st.enter_context(nc.sbuf_tensor("p1s1", [128, NT], F32))
        s2 = st.enter_context(nc.sbuf_tensor("p1s2", [128, NT], F32))
        negm = st.enter_context(nc.sbuf_tensor("p1negm", [128, NT], F32))
        rstd = st.enter_context(nc.sbuf_tensor("p1rstd", [128, NT], F32))
        var = st.enter_context(nc.sbuf_tensor("p1var", [128, NT], F32))
        m2 = st.enter_context(nc.sbuf_tensor("p1m2", [128, NT], F32))

        L = [p.sem["L0"], p.sem["L1"]]
        Sm = [p.sem["S0"], p.sem["S1"]]
        l0 = [p.cnt["L0"], p.cnt["L1"]]
        s0 = [p.cnt["S0"], p.cnt["S1"]]
        ve0 = p.cnt["ve"]
        se0 = p.cnt["se"]

        with MB(p) as b:
            @b.sync
            def _(sy):
                for i in range(NT):
                    s, t = divmod(i, S // 128)
                    if i >= 2:
                        sy.wait_ge(p.sem["ve"], ve0 + 3 * (i - 2) + 3)
                        sy.wait_ge(p.sem["se"], se0 + 2 * (i - 2) + 1)
                    sy.dma_start(out=xin[i % 2][:], in_=xa[s, 128 * t:128 * (t + 1), :]) \
                        .then_inc(L[i % 2], 16)
                    p.cnt[f"L{i % 2}"] += 16

            @b.vector
            def _(v):
                for i in range(NT):
                    x = xin[i % 2]
                    ci = slice(i, i + 1)
                    v.wait_ge(L[i % 2], l0[i % 2] + 16 * (i // 2 + 1))
                    v.tensor_scalar(out=junkv[:], in0=x[:], scalar1=1.0, scalar2=0.0,
                                    op0=ALU.mult, op1=ALU.add, accum_out=s1[:, ci]) \
                        .then_inc(p.sem["ve"], 1)
                    v.wait_ge(p.sem["se"], se0 + 2 * i + 1)
                    v.tensor_sub(var[:, ci], var[:, ci], m2[:, ci]).then_inc(p.sem["ve"], 1)
                    v.wait_ge(p.sem["se"], se0 + 2 * i + 2)
                    if i >= 2:
                        v.wait_ge(Sm[i % 2], s0[i % 2] + 16 * ((i - 2) // 2 + 1))
                    v.tensor_scalar(out=lno[i % 2][:], in0=x[:], scalar1=negm[:, ci],
                                    scalar2=rstd[:, ci], op0=ALU.add, op1=ALU.mult) \
                        .then_inc(p.sem["ve"], 1)
                p.cnt["ve"] += 3 * NT

            @b.scalar
            def _(sc):
                for i in range(NT):
                    x = xin[i % 2]
                    ci = slice(i, i + 1)
                    sc.wait_ge(L[i % 2], l0[i % 2] + 16 * (i // 2 + 1))
                    sc.activation(out=junks[:], in_=x[:], func=AF.Square, accum_out=s2[:, ci])
                    sc.wait_ge(p.sem["ve"], ve0 + 3 * i + 1)
                    p.edge(sc, sc.mul(negm[:, ci], s1[:, ci], -1.0 / D), "sx")
                    sc.activation(out=m2[:, ci], in_=negm[:, ci], func=AF.Square)
                    sc.mul(var[:, ci], s2[:, ci], 1.0 / D).then_inc(p.sem["se"], 1)
                    sc.wait_ge(p.sem["ve"], ve0 + 3 * i + 2)
                    p.edge(sc, sc.activation(out=var[:, ci], in_=var[:, ci], func=AF.Ln,
                                             bias=p.c["epscol"][:, 0:1]), "sx")
                    sc.activation(out=rstd[:, ci], in_=var[:, ci], func=AF.Exp, scale=-0.5) \
                        .then_inc(p.sem["se"], 1)
                p.cnt["se"] += 2 * NT

            @b.gpsimd
            def _(g):
                for i in range(NT):
                    s, t = divmod(i, S // 128)
                    g.wait_ge(p.sem["ve"], ve0 + 3 * i + 3)
                    g.dma_start(out=la[s, 128 * t:128 * (t + 1), :], in_=lno[i % 2][:]) \
                        .then_inc(Sm[i % 2], 16)
                    p.cnt[f"S{i % 2}"] += 16
                g.wait_ge(Sm[0], p.cnt["S0"])
                g.wait_ge(Sm[1], p.cnt["S1"])



def _p1_gram_msg(p, lnnat_s, xcv2, x1_s, dbg_prob):
    nc = p.nc
    la = lnnat_s.ap()
    xca = xcv2.ap()
    x1a = x1_s.ap()
    NTC = S // 128
    PAIRS = [(0, 1), (0, 2), (0, 3), (0, 4), (1, 1), (1, 2), (1, 3), (1, 4)]
    id_b, id_f = p.c["id_b"], p.c["id_f"]

    with ExitStack() as st:
        gsl = [st.enter_context(nc.sbuf_tensor(f"g{s}", [128, S], BF16)) for s in range(5)]
        lnc = [st.enter_context(nc.sbuf_tensor(f"lc{m}", [128, S], BF16)) for m in range(4)]
        lnc0 = st.enter_context(nc.sbuf_tensor("lc_own", [128, S], BF16))
        dall = st.enter_context(nc.sbuf_tensor("dall", [128, 8], F32))
        prob = st.enter_context(nc.sbuf_tensor("prob", [128, 8], F32))
        e4t = st.enter_context(nc.sbuf_tensor("e4t", [128, 8], F32))
        nmx = st.enter_context(nc.sbuf_tensor("nmx", [128, 2], F32))
        sme = st.enter_context(nc.sbuf_tensor("sme", [128, 2], F32))
        rcp = st.enter_context(nc.sbuf_tensor("rcp", [128, 2], F32))
        dmu = [[st.enter_context(nc.sbuf_tensor(f"dm{u}{m}", [128, 128], BF16))
                for m in range(4)] for u in range(2)]
        jnk = st.enter_context(nc.sbuf_tensor("jnkg", [128, 128], F32))
        xcv = [st.enter_context(nc.sbuf_tensor(f"xc{i}", [128, 512], F32)) for i in range(2)]
        x1o = [st.enter_context(nc.sbuf_tensor(f"x1o{i}", [128, 512], F32)) for i in range(2)]
        pg = [st.enter_context(nc.psum_tensor(f"pg{i}", [128, 128], F32)) for i in range(4)]
        pt = [st.enter_context(nc.psum_tensor(f"pt{i}", [128, 128], BF16)) for i in range(2)]
        pm = [st.enter_context(nc.psum_tensor(f"pm{i}", [128, 512], F32)) for i in range(2)]
        dbga = dbg_prob.ap() if dbg_prob is not None else None

        for lb in range(GRAM_LBS):
            te0 = p.cnt["te"]
            ve0 = p.cnt["ve"]
            se0 = p.cnt["se"]
            with MB(p) as b:
                @b.sync
                def _(sy):
                    for s in range(5):
                        sy.dma_start(
                            out=gsl[s][:],
                            in_=bass.AP(la.tensor, la.offset + s * S * D + lb * 128,
                                        [[D, 128], [128 * D, NTC], [1, 128]])) \
                            .then_inc(p.sem[f"L{s}"], 16)
                        p.cnt[f"L{s}"] += 16

                @b.tensor
                def _(te):
                    for s in range(5):
                        te.wait_ge(p.sem[f"L{s}"], p.cnt[f"L{s}"])
                    for j, (a, bb) in enumerate(PAIRS):
                        if j >= 4:
                            te.wait_ge(p.sem["ve"], ve0 + (j - 4) + 1)
                        for tc in range(NTC):
                            mm = te.matmul(pg[j % 4][:], lhsT=gsl[a][:, 128 * tc:128 * (tc + 1)],
                                           rhs=gsl[bb][:, 128 * tc:128 * (tc + 1)],
                                           start=(tc == 0), stop=(tc == NTC - 1))
                        mm.then_inc(p.sem["te"], 1)
                    p.cnt["te"] += 8

                @b.vector
                def _(v):
                    for j in range(8):
                        v.wait_ge(p.sem["te"], te0 + (j + 1))
                        p.edge(v, v.scalar_tensor_tensor(out=jnk[:], in0=pg[j % 4][:], scalar=1.0,
                                                         in1=id_f[:], op0=ALU.mult, op1=ALU.mult,
                                                         accum_out=dall[:, j:j + 1]), "vx")
                        v.nop().then_inc(p.sem["ve"], 1)
                    for u in range(2):
                        cu = slice(4 * u, 4 * u + 4)
                        p.edge(v, v.reduce_max(nmx[:, u:u + 1], dall[:, cu], axis=AX.X), "vx")
                        v.tensor_scalar(out=nmx[:, u:u + 1], in0=nmx[:, u:u + 1],
                                        scalar1=-GRAM_SCALE, scalar2=None, op0=ALU.mult) \
                            .then_inc(p.sem["ve"], 1)
                        v.wait_ge(p.sem["se"], se0 + u + 1)
                        v.reciprocal(rcp[:, u:u + 1], sme[:, u:u + 1]).then_inc(p.sem["ve"], 1)
                    for u in range(2):
                        cu = slice(4 * u, 4 * u + 4)
                        p.edge(v, v.tensor_scalar(out=prob[:, cu], in0=e4t[:, cu],
                                                  scalar1=rcp[:, u:u + 1],
                                                  scalar2=None, op0=ALU.mult), "vx")
                        v.nop().then_inc(p.sem["ve"], 1)
                        for m in range(4):
                            v.tensor_scalar(out=dmu[u][m][:], in0=id_b[:],
                                            scalar1=prob[:, 4 * u + m:4 * u + m + 1],
                                            scalar2=None, op0=ALU.mult).then_inc(p.sem["ve"], 1)
                    p.cnt["ve"] += 22

                @b.scalar
                def _(sc):
                    for u in range(2):
                        cu = slice(4 * u, 4 * u + 4)
                        sc.wait_ge(p.sem["ve"], ve0 + 8 + 2 * u + 1)
                        sc.activation(out=e4t[:, cu], in_=dall[:, cu], func=AF.Exp,
                                      scale=GRAM_SCALE, bias=nmx[:, u:u + 1],
                                      accum_out=sme[:, u:u + 1]).then_inc(p.sem["se"], 1)
                    p.cnt["se"] += 2

                @b.gpsimd
                def _(g):
                    if dbga is not None:
                        g.wait_ge(p.sem["ve"], ve0 + 22)
                        g.dma_start(out=dbga[0, :, 4 * lb:4 * lb + 4], in_=prob[:, 0:4]) \
                            .then_inc(p.sem["S2"], 16)
                        p.cnt["S2"] += 16
                        g.wait_ge(p.sem["S2"], p.cnt["S2"])
                        g.dma_start(out=dbga[1, :, 4 * lb:4 * lb + 4], in_=prob[:, 4:8]) \
                            .then_inc(p.sem["S2"], 16)
                        p.cnt["S2"] += 16
                        g.wait_ge(p.sem["S2"], p.cnt["S2"])

            if GRAM_SUB < 2:
                continue

            te0 = p.cnt["te"]
            ve0 = p.cnt["ve"]
            se0 = p.cnt["se"]
            lx0 = [p.cnt["L4"], p.cnt["L5"]]
            sx0 = [p.cnt["S0"], p.cnt["S1"]]
            with MB(p) as b:
                @b.sync
                def _(sy):
                    for j in range(8):
                        u, tc = divmod(j, 4)
                        if j >= 2:
                            sy.wait_ge(p.sem["ve"], ve0 + (j - 2) + 1)
                        sy.dma_start(out=xcv[j % 2][:],
                                     in_=xca[u, 128 * lb:128 * (lb + 1), 512 * tc:512 * (tc + 1)]) \
                            .then_inc(p.sem[f"L{4 + j % 2}"], 16)
                        p.cnt[f"L{4 + j % 2}"] += 16

                @b.tensor
                def _(te):
                    for m in range(5):
                        for tc in range(NTC):
                            j = m * NTC + tc
                            if j >= 2:
                                te.wait_ge(p.sem["se"], se0 + (j - 1))
                            srcsl = gsl[1 + m] if m < 4 else gsl[0]
                            te.transpose(pt[j % 2][:], srcsl[:, 128 * tc:128 * (tc + 1)],
                                         id_b[:]).then_inc(p.sem["te"], 1)
                    te.wait_ge(p.sem["se"], se0 + 64)
                    for j in range(8):
                        u, tc = divmod(j, 4)
                        if j >= 2:
                            te.wait_ge(p.sem["ve"], ve0 + (j - 2) + 1)
                        for m in range(4):
                            mm = te.matmul(pm[j % 2][:], lhsT=dmu[u][m][:],
                                           rhs=lnc[m][:, 512 * tc:512 * (tc + 1)],
                                           start=(m == 0), stop=(m == 3))
                        mm.then_inc(p.sem["te"], 1)
                    p.cnt["te"] += 88

                @b.scalar
                def _(sc):
                    for m in range(5):
                        for tc in range(NTC):
                            j = m * NTC + tc
                            sc.wait_ge(p.sem["te"], te0 + j + 1)
                            dstsl = lnc[m] if m < 4 else lnc0
                            sc.copy(dstsl[:, 128 * tc:128 * (tc + 1)], pt[j % 2][:]) \
                                .then_inc(p.sem["se"], 1)
                    p.cnt["se"] += 80

                if GRAM_SUB >= 3:
                    @b.vector
                    def _(v):
                        for j in range(8):
                            u, tc = divmod(j, 4)
                            v.wait_ge(p.sem["te"], te0 + 80 + j + 1)
                            v.wait_ge(p.sem["se"], se0 + 80)
                            v.wait_ge(p.sem[f"L{4 + j % 2}"], lx0[j % 2] + 16 * (j // 2 + 1))
                            if j >= 2:
                                v.wait_ge(p.sem[f"S{j % 2}"], sx0[j % 2] + 16 * ((j - 2) // 2 + 1))
                            p.edge(v, v.scalar_tensor_tensor(out=x1o[j % 2][:], in0=pm[j % 2][:],
                                                             scalar=1.0, in1=xcv[j % 2][:],
                                                             op0=ALU.mult, op1=ALU.add), "vx")
                            lnsel = lnc0 if u == 0 else lnc[0]
                            v.tensor_add(x1o[j % 2][:], x1o[j % 2][:],
                                         lnsel[:, 512 * tc:512 * (tc + 1)]) \
                                .then_inc(p.sem["ve"], 1)
                        p.cnt["ve"] += 8

                    @b.gpsimd
                    def _(g):
                        for j in range(8):
                            u, tc = divmod(j, 4)
                            g.wait_ge(p.sem["ve"], ve0 + j + 1)
                            g.dma_start(out=x1a[u, 128 * lb:128 * (lb + 1), 512 * tc:512 * (tc + 1)],
                                        in_=x1o[j % 2][:]).then_inc(p.sem[f"S{j % 2}"], 16)
                            p.cnt[f"S{j % 2}"] += 16
                        g.wait_ge(p.sem["S0"], p.cnt["S0"])
                        g.wait_ge(p.sem["S1"], p.cnt["S1"])
                else:
                    @b.vector
                    def _(v):
                        for j in range(8):
                            v.wait_ge(p.sem["te"], te0 + 80 + j + 1)
                        p.cnt["ve"] += 0



def _ffn(p, src_ap, dst_ap, hs_ap, T, tag, dbg_hs=None):
    nc = p.nc
    NTC = T // 512
    NTT = T // 128
    wa_sb, wb_sb, wr_sb, br_sb = p.c["wa_sb"], p.c["wb_sb"], p.c["wr_sb"], p.c["br_sb"]
    with ExitStack() as st:
        xb = [st.enter_context(nc.sbuf_tensor(f"f{tag}xb{i}", [128, T], BF16)) for i in range(2)]
        sq = [st.enter_context(nc.sbuf_tensor(f"f{tag}sq{i}", [128, T], BF16)) for i in range(2)]
        xin = [st.enter_context(nc.sbuf_tensor(f"f{tag}xi{i}", [128, T], F32)) for i in range(2)]
        s1r = st.enter_context(nc.sbuf_tensor(f"f{tag}s1", [1, T], F32))
        s2r = st.enter_context(nc.sbuf_tensor(f"f{tag}s2", [1, T], F32))
        nmr = st.enter_context(nc.sbuf_tensor(f"f{tag}nm", [1, T], F32))
        rr = st.enter_context(nc.sbuf_tensor(f"f{tag}rr", [1, T], F32))
        m2r = st.enter_context(nc.sbuf_tensor(f"f{tag}m2", [1, T], F32))
        nmb = st.enter_context(nc.sbuf_tensor(f"f{tag}nmb", [128, T], F32))
        hso = [st.enter_context(nc.sbuf_tensor(f"f{tag}hs{i}", [128, T], BF16)) for i in range(2)]
        lowT = st.enter_context(nc.sbuf_tensor(f"f{tag}low", [128, T], BF16))
        rtraw = st.enter_context(nc.sbuf_tensor(f"f{tag}rt", [8, T], F32))
        lgN = st.enter_context(nc.sbuf_tensor(f"f{tag}lgN", [128, 8], F32))
        nm8 = st.enter_context(nc.sbuf_tensor(f"f{tag}nm8", [128, 1], F32))
        sm8 = st.enter_context(nc.sbuf_tensor(f"f{tag}sm8", [128, 1], F32))
        rc8 = st.enter_context(nc.sbuf_tensor(f"f{tag}rc8", [128, 1], F32))
        rcol = st.enter_context(nc.sbuf_tensor(f"f{tag}rcol", [128, 1], F32))
        fac = st.enter_context(nc.sbuf_tensor(f"f{tag}fac", [128, 1], F32))
        e8n = st.enter_context(nc.sbuf_tensor(f"f{tag}e8n", [128, 8], BF16))
        e8T = st.enter_context(nc.sbuf_tensor(f"f{tag}e8T", [8, T], BF16))
        mid = st.enter_context(nc.sbuf_tensor(f"f{tag}mid", [128, T], BF16))
        x2o = [st.enter_context(nc.sbuf_tensor(f"f{tag}xo{i}", [128, 512], F32)) for i in range(2)]
        onef = st.enter_context(nc.sbuf_tensor(f"f{tag}onef", [1, 128], F32))

        l0 = [p.cnt["L0"], p.cnt["L1"]]
        ve0, se0, te0 = p.cnt["ve"], p.cnt["se"], p.cnt["te"]
        with ExitStack() as pst, MB(p) as b:
            ps1 = [pst.enter_context(nc.psum_tensor(f"f{tag}p1{i}", [1, 512], F32)) for i in range(NTC)]
            ps2 = [pst.enter_context(nc.psum_tensor(f"f{tag}p2{i}", [1, 512], F32)) for i in range(NTC)]

            @b.sync
            def _(sy):
                for dc in range(DCH):
                    if dc >= 2:
                        sy.wait_ge(p.sem["te"], te0 + (dc - 2) + 1)
                    sy.dma_start(out=xin[dc % 2][:], in_=src_ap[128 * dc:128 * (dc + 1), :]) \
                        .then_inc(p.sem[f"L{dc % 2}"], 16)
                    p.cnt[f"L{dc % 2}"] += 16

            @b.vector
            def _(v):
                v.memset(onef[:], 1.0).then_inc(p.sem["ve"], 1)
                for dc in range(DCH):
                    v.wait_ge(p.sem[f"L{dc % 2}"], l0[dc % 2] + 16 * (dc // 2 + 1))
                    if dc >= 2:
                        v.wait_ge(p.sem["te"], te0 + (dc - 2) + 1)
                    v.tensor_copy(xb[dc % 2][:], xin[dc % 2][:]).then_inc(p.sem["ve"], 1)
                p.cnt["ve"] += 1 + DCH

            @b.scalar
            def _(sc):
                for dc in range(DCH):
                    sc.wait_ge(p.sem[f"L{dc % 2}"], l0[dc % 2] + 16 * (dc // 2 + 1))
                    if dc >= 2:
                        sc.wait_ge(p.sem["te"], te0 + (dc - 2) + 1)
                    sc.activation(out=sq[dc % 2][:], in_=xin[dc % 2][:], func=AF.Square) \
                        .then_inc(p.sem["se"], 1)
                p.cnt["se"] += DCH

            @b.tensor
            def _(te):
                for dc in range(DCH):
                    te.wait_ge(p.sem["ve"], ve0 + 1 + dc + 1)
                    te.wait_ge(p.sem["se"], se0 + dc + 1)
                    for tc in range(NTC):
                        te.matmul(ps1[tc][:], lhsT=p.c["ones_col"][:],
                                  rhs=xb[dc % 2][:, 512 * tc:512 * (tc + 1)],
                                  start=(dc == 0), stop=(dc == DCH - 1))
                    for tc in range(NTC):
                        mm = te.matmul(ps2[tc][:], lhsT=p.c["ones_col"][:],
                                       rhs=sq[dc % 2][:, 512 * tc:512 * (tc + 1)],
                                       start=(dc == 0), stop=(dc == DCH - 1))
                    mm.then_inc(p.sem["te"], 1)
                p.cnt["te"] += DCH

            @b.scalar
            def _(sc2):
                sc2.wait_ge(p.sem["te"], te0 + DCH)
                for tc in range(NTC):
                    cs = slice(512 * tc, 512 * (tc + 1))
                    p.edge(sc2, sc2.copy(s1r[:, cs], ps1[tc][:]), "sx")
                    p.edge(sc2, sc2.copy(s2r[:, cs], ps2[tc][:]), "sx")
                p.edge(sc2, sc2.mul(nmr[:], s1r[:], -1.0 / D), "sx")
                p.edge(sc2, sc2.activation(out=m2r[:], in_=nmr[:], func=AF.Square), "sx")
                p.edge(sc2, sc2.mul(rr[:], s2r[:], 1.0 / D), "sx")
                sc2.nop().then_inc(p.sem["se"], 1)
                p.cnt["se"] += 1
                sc2.wait_ge(p.sem["ve"], ve0 + 1 + DCH + 1)
                p.edge(sc2, sc2.activation(out=rr[:], in_=rr[:], func=AF.Ln,
                                           bias=p.c["epscol"][0:1, 0:1]), "sx")
                p.edge(sc2, sc2.activation(out=rr[:], in_=rr[:], func=AF.Exp, scale=-0.5), "sx")
                sc2.nop().then_inc(p.sem["se"], 1)
                p.cnt["se"] += 1

            @b.vector
            def _(v2):
                v2.wait_ge(p.sem["se"], se0 + DCH + 1)
                p.edge(v2, v2.tensor_sub(rr[:], rr[:], m2r[:]), "vx")
                v2.nop().then_inc(p.sem["ve"], 1)
                p.cnt["ve"] += 1

        l0 = [p.cnt["L0"], p.cnt["L1"]]
        s0 = [p.cnt["S0"], p.cnt["S1"]]
        ve0, se0, te0 = p.cnt["ve"], p.cnt["se"], p.cnt["te"]
        with ExitStack() as pst, MB(p) as b:
            pb = [pst.enter_context(nc.psum_tensor(f"f{tag}pb{i}", [128, 512], F32)) for i in range(NTC)]

            @b.tensor
            def _(te):
                for tc in range(NTC):
                    mm = te.matmul(pb[tc][:], lhsT=onef[:], rhs=nmr[0:1, 512 * tc:512 * (tc + 1)],
                                   start=True, stop=True)
                mm.then_inc(p.sem["te"], 1)
                p.cnt["te"] += 1

            @b.scalar
            def _(sc):
                sc.wait_ge(p.sem["te"], te0 + 1)
                for tc in range(NTC):
                    p.edge(sc, sc.copy(nmb[:, 512 * tc:512 * (tc + 1)], pb[tc][:]), "sx")
                sc.nop().then_inc(p.sem["se"], 1)
                p.cnt["se"] += 1

            @b.sync
            def _(sy):
                for dc in range(DCH):
                    if dc >= 2:
                        sy.wait_ge(p.sem["ve"], ve0 + (dc - 2) + 1)
                    sy.dma_start(out=xin[dc % 2][:], in_=src_ap[128 * dc:128 * (dc + 1), :]) \
                        .then_inc(p.sem[f"L{dc % 2}"], 16)
                    p.cnt[f"L{dc % 2}"] += 16

            @b.vector
            def _(v):
                v.wait_ge(p.sem["se"], se0 + 1)
                for dc in range(DCH):
                    v.wait_ge(p.sem[f"L{dc % 2}"], l0[dc % 2] + 16 * (dc // 2 + 1))
                    if dc >= 2:
                        v.wait_ge(p.sem[f"S{dc % 2}"], s0[dc % 2] + 16 * ((dc - 2) // 2 + 1))
                    v.tensor_add(hso[dc % 2][:], xin[dc % 2][:], nmb[:]).then_inc(p.sem["ve"], 1)
                p.cnt["ve"] += DCH

            @b.gpsimd
            def _(g):
                for dc in range(DCH):
                    g.wait_ge(p.sem["ve"], ve0 + dc + 1)
                    g.dma_start(out=hs_ap[128 * dc:128 * (dc + 1), :], in_=hso[dc % 2][:]) \
                        .then_inc(p.sem[f"S{dc % 2}"], 16)
                    p.cnt[f"S{dc % 2}"] += 16
                g.wait_ge(p.sem["S0"], p.cnt["S0"])
                g.wait_ge(p.sem["S1"], p.cnt["S1"])

        l0 = [p.cnt[f"L{i}"] for i in range(4)]
        ve0, se0, te0 = p.cnt["ve"], p.cnt["se"], p.cnt["te"]
        with ExitStack() as pst, MB(p) as b:
            pLow = pst.enter_context(nc.psum_tensor(f"f{tag}pL", [128, 512], F32))
            pRt = pst.enter_context(nc.psum_tensor(f"f{tag}pR", [8, 512], F32))
            pr8 = pst.enter_context(nc.psum_tensor(f"f{tag}pr8", [8, 512], F32))

            @b.sync
            def _(sy):
                hbuf = [xb[0], xb[1], sq[0], sq[1]]
                for tc in range(NTC):
                    for dc in range(DCH):
                        j = tc * DCH + dc
                        if j >= 4:
                            tcp, dcp = divmod(j - 4, DCH)
                            sy.wait_ge(p.sem["te"], te0 + tcp * (DCH + 1) + dcp + 1)
                        sy.dma_start(out=hbuf[j % 4][:, 0:512],
                                     in_=hs_ap[128 * dc:128 * (dc + 1), 512 * tc:512 * (tc + 1)]) \
                            .then_inc(p.sem[f"L{j % 4}"], 16)
                        p.cnt[f"L{j % 4}"] += 16

            @b.tensor
            def _(te):
                hbuf = [xb[0], xb[1], sq[0], sq[1]]
                for tc in range(NTC):
                    if tc >= 1:
                        te.wait_ge(p.sem["se"], se0 + 2 * (tc - 1) + 2)
                        te.wait_ge(p.sem["ve"], ve0 + 2 * (tc - 1) + 1)
                    for dc in range(DCH):
                        j = tc * DCH + dc
                        te.wait_ge(p.sem[f"L{j % 4}"], l0[j % 4] + 16 * (j // 4 + 1))
                        rhs = hbuf[j % 4][:, 0:512]
                        te.matmul(pLow[:], lhsT=wa_sb[:, 128 * dc:128 * (dc + 1)], rhs=rhs,
                                  start=(dc == 0), stop=(dc == DCH - 1))
                        mm = te.matmul(pRt[:], lhsT=wr_sb[:, 8 * dc:8 * (dc + 1)], rhs=rhs,
                                       start=(dc == 0), stop=(dc == DCH - 1))
                        mm.then_inc(p.sem["te"], 1)
                    mm = te.matmul(pr8[:], lhsT=onef[0:1, 0:8], rhs=rr[0:1, 512 * tc:512 * (tc + 1)],
                                   start=True, stop=True)
                    mm.then_inc(p.sem["te"], 1)
                p.cnt["te"] += NTC * (DCH + 1)

            @b.scalar
            def _(sc):
                for tc in range(NTC):
                    cs = slice(512 * tc, 512 * (tc + 1))
                    sc.wait_ge(p.sem["te"], te0 + tc * (DCH + 1) + DCH)
                    sc.copy(lowT[:, cs], pLow[:]).then_inc(p.sem["se"], 1)
                    sc.copy(rtraw[:, cs], pRt[:]).then_inc(p.sem["se"], 1)
                p.cnt["se"] += 2 * NTC

            @b.vector
            def _(v):
                for tc in range(NTC):
                    cs = slice(512 * tc, 512 * (tc + 1))
                    v.wait_ge(p.sem["te"], te0 + (tc + 1) * (DCH + 1))
                    v.wait_ge(p.sem["se"], se0 + 2 * tc + 2)
                    p.edge(v, v.tensor_mul(rtraw[:, cs], rtraw[:, cs], pr8[:]), "vx")
                    v.nop().then_inc(p.sem["ve"], 1)
                    v.tensor_scalar(out=rtraw[:, cs], in0=rtraw[:, cs], scalar1=br_sb[:, 0:1],
                                    scalar2=None, op0=ALU.add).then_inc(p.sem["ve"], 1)
                p.cnt["ve"] += 2 * NTC

        ve0, se0, te0 = p.cnt["ve"], p.cnt["se"], p.cnt["te"]
        with ExitStack() as pst, MB(p) as b:
            pLgN = pst.enter_context(nc.psum_tensor(f"f{tag}pN", [128, 8], F32))
            pRc = pst.enter_context(nc.psum_tensor(f"f{tag}pC", [128, 1], F32))
            p8 = pst.enter_context(nc.psum_tensor(f"f{tag}p8", [8, 128], F32))

            @b.tensor
            def _(te):
                for tt in range(NTT):
                    if tt >= 1:
                        te.wait_ge(p.sem["se"], se0 + 4 * tt)
                    mm = te.matmul(pLgN[:], lhsT=rtraw[:, 128 * tt:128 * (tt + 1)], rhs=p.c["i8_sb"][:],
                                   start=True, stop=True)
                    mm.then_inc(p.sem["te"], 1)
                    mm = te.matmul(pRc[:], lhsT=rr[0:1, 128 * tt:128 * (tt + 1)], rhs=onef[0:1, 0:1],
                                   start=True, stop=True)
                    mm.then_inc(p.sem["te"], 1)
                    te.wait_ge(p.sem["ve"], ve0 + 5 * tt + 5)
                    mm = te.matmul(p8[:], lhsT=e8n[:], rhs=p.c["id_b"][:], start=True, stop=True)
                    mm.then_inc(p.sem["te"], 1)
                p.cnt["te"] += 3 * NTT

            @b.scalar
            def _(sc):
                for tt in range(NTT):
                    sc.wait_ge(p.sem["te"], te0 + 3 * tt + 1)
                    p.edge(sc, sc.copy(lgN[:], pLgN[:]), "sx")
                    sc.nop().then_inc(p.sem["se"], 1)
                    sc.wait_ge(p.sem["te"], te0 + 3 * tt + 2)
                    p.edge(sc, sc.copy(rcol[:], pRc[:]), "sx")
                    sc.nop().then_inc(p.sem["se"], 1)
                    sc.wait_ge(p.sem["ve"], ve0 + 5 * tt + 2)
                    sc.wait_ge(p.sem["te"], te0 + 3 * tt)
                    p.edge(sc, sc.activation(out=e8n[:], in_=lgN[:], func=AF.Exp,
                                             bias=nm8[:, 0:1], accum_out=sm8[:, 0:1]), "sx")
                    sc.nop().then_inc(p.sem["se"], 1)
                    sc.wait_ge(p.sem["te"], te0 + 3 * tt + 3)
                    p.edge(sc, sc.copy(e8T[:, 128 * tt:128 * (tt + 1)], p8[:]), "sx")
                    sc.nop().then_inc(p.sem["se"], 1)
                p.cnt["se"] += 4 * NTT

            @b.vector
            def _(v):
                for tt in range(NTT):
                    v.wait_ge(p.sem["se"], se0 + 4 * tt + 1)
                    p.edge(v, v.reduce_max(nm8[:, 0:1], lgN[:], axis=AX.X), "vx")
                    p.edge(v, v.tensor_scalar(out=nm8[:, 0:1], in0=nm8[:, 0:1], scalar1=-1.0,
                                              scalar2=None, op0=ALU.mult), "vx")
                    v.nop().then_inc(p.sem["ve"], 1)
                    v.nop().then_inc(p.sem["ve"], 1)
                    v.wait_ge(p.sem["se"], se0 + 4 * tt + 3)
                    p.edge(v, v.reciprocal(rc8[:, 0:1], sm8[:, 0:1]), "vx")
                    v.nop().then_inc(p.sem["ve"], 1)
                    p.edge(v, v.tensor_scalar(out=fac[:, 0:1], in0=rc8[:, 0:1], scalar1=rcol[:, 0:1],
                                              scalar2=FFN_FACTOR, op0=ALU.mult, op1=ALU.mult), "vx")
                    v.nop().then_inc(p.sem["ve"], 1)
                    p.edge(v, v.tensor_scalar(out=e8n[:], in0=e8n[:], scalar1=fac[:, 0:1],
                                              scalar2=None, op0=ALU.mult), "vx")
                    v.nop().then_inc(p.sem["ve"], 1)
                p.cnt["ve"] += 5 * NTT

        ve0, se0, te0 = p.cnt["ve"], p.cnt["se"], p.cnt["te"]
        with ExitStack() as pst, MB(p) as b:
            pRep = [pst.enter_context(nc.psum_tensor(f"f{tag}pE{i}", [128, 512], F32)) for i in range(2)]

            @b.tensor
            def _(te):
                for tc in range(NTC):
                    if tc >= 2:
                        te.wait_ge(p.sem["ve"], ve0 + (tc - 2) + 1)
                    mm = te.matmul(pRep[tc % 2][:], lhsT=p.c["e8_sb"][:],
                                   rhs=e8T[:, 512 * tc:512 * (tc + 1)], start=True, stop=True)
                    mm.then_inc(p.sem["te"], 1)
                p.cnt["te"] += NTC

            @b.vector
            def _(v):
                for tc in range(NTC):
                    cs = slice(512 * tc, 512 * (tc + 1))
                    v.wait_ge(p.sem["te"], te0 + tc + 1)
                    v.tensor_mul(mid[:, cs], lowT[:, cs], pRep[tc % 2][:]).then_inc(p.sem["ve"], 1)
                p.cnt["ve"] += NTC

        l0 = [p.cnt["L0"], p.cnt["L1"]]
        s0 = [p.cnt["S0"], p.cnt["S1"]]
        ve0, se0, te0 = p.cnt["ve"], p.cnt["se"], p.cnt["te"]
        with ExitStack() as pst, MB(p) as b:
            pOut = [pst.enter_context(nc.psum_tensor(f"f{tag}pO{i}", [128, 512], F32)) for i in range(2)]

            @b.sync
            def _(sy):
                for j in range(DCH * NTC):
                    oblk, tc = divmod(j, NTC)
                    if j >= 2:
                        sy.wait_ge(p.sem["ve"], ve0 + (j - 2) + 1)
                    sy.dma_start(out=xin[j % 2][:, 0:512],
                                 in_=src_ap[128 * oblk:128 * (oblk + 1), 512 * tc:512 * (tc + 1)]) \
                        .then_inc(p.sem[f"L{j % 2}"], 16)
                    p.cnt[f"L{j % 2}"] += 16

            @b.tensor
            def _(te):
                for j in range(DCH * NTC):
                    oblk, tc = divmod(j, NTC)
                    if j >= 2:
                        te.wait_ge(p.sem["ve"], ve0 + (j - 2) + 1)
                    mm = te.matmul(pOut[j % 2][:], lhsT=wb_sb[:, 128 * oblk:128 * (oblk + 1)],
                                   rhs=mid[:, 512 * tc:512 * (tc + 1)], start=True, stop=True)
                    mm.then_inc(p.sem["te"], 1)
                p.cnt["te"] += DCH * NTC

            @b.vector
            def _(v):
                for j in range(DCH * NTC):
                    v.wait_ge(p.sem["te"], te0 + j + 1)
                    v.wait_ge(p.sem[f"L{j % 2}"], l0[j % 2] + 16 * (j // 2 + 1))
                    if j >= 2:
                        v.wait_ge(p.sem[f"S{j % 2}"], s0[j % 2] + 16 * ((j - 2) // 2 + 1))
                    v.scalar_tensor_tensor(out=x2o[j % 2][:], in0=pOut[j % 2][:], scalar=1.0,
                                           in1=xin[j % 2][:, 0:512], op0=ALU.mult, op1=ALU.add) \
                        .then_inc(p.sem["ve"], 1)
                p.cnt["ve"] += DCH * NTC

            @b.gpsimd
            def _(g):
                for j in range(DCH * NTC):
                    oblk, tc = divmod(j, NTC)
                    g.wait_ge(p.sem["ve"], ve0 + j + 1)
                    g.dma_start(out=dst_ap[128 * oblk:128 * (oblk + 1), 512 * tc:512 * (tc + 1)],
                                in_=x2o[j % 2][:]).then_inc(p.sem[f"S{j % 2}"], 16)
                    p.cnt[f"S{j % 2}"] += 16
                g.wait_ge(p.sem["S0"], p.cnt["S0"])
                g.wait_ge(p.sem["S1"], p.cnt["S1"])



def _ln_stats_rows(p, src_ap, T, nmr, rr, m2r, onef, tag):
    nc = p.nc
    NTC = T // 512
    with ExitStack() as st:
        xin = [st.enter_context(nc.sbuf_tensor(f"s{tag}xi{i}", [128, T], F32)) for i in range(2)]
        xb = [st.enter_context(nc.sbuf_tensor(f"s{tag}xb{i}", [128, T], BF16)) for i in range(2)]
        sq = [st.enter_context(nc.sbuf_tensor(f"s{tag}sq{i}", [128, T], BF16)) for i in range(2)]
        s1r = st.enter_context(nc.sbuf_tensor(f"s{tag}s1", [1, T], F32))
        s2r = st.enter_context(nc.sbuf_tensor(f"s{tag}s2", [1, T], F32))
        l0 = [p.cnt["L0"], p.cnt["L1"]]
        ve0, se0, te0 = p.cnt["ve"], p.cnt["se"], p.cnt["te"]
        with ExitStack() as pst, MB(p) as b:
            ps1 = [pst.enter_context(nc.psum_tensor(f"s{tag}p1{i}", [1, 512], F32)) for i in range(NTC)]
            ps2 = [pst.enter_context(nc.psum_tensor(f"s{tag}p2{i}", [1, 512], F32)) for i in range(NTC)]

            @b.sync
            def _(sy):
                for dc in range(DCH):
                    if dc >= 2:
                        sy.wait_ge(p.sem["te"], te0 + (dc - 2) + 1)
                    sy.dma_start(out=xin[dc % 2][:], in_=src_ap[128 * dc:128 * (dc + 1), :]) \
                        .then_inc(p.sem[f"L{dc % 2}"], 16)
                    p.cnt[f"L{dc % 2}"] += 16

            @b.vector
            def _(v):
                for dc in range(DCH):
                    v.wait_ge(p.sem[f"L{dc % 2}"], l0[dc % 2] + 16 * (dc // 2 + 1))
                    if dc >= 2:
                        v.wait_ge(p.sem["te"], te0 + (dc - 2) + 1)
                    v.tensor_copy(xb[dc % 2][:], xin[dc % 2][:]).then_inc(p.sem["ve"], 1)
                p.cnt["ve"] += DCH
                v.wait_ge(p.sem["se"], se0 + DCH + 1)
                p.edge(v, v.tensor_sub(rr[:], rr[:], m2r[:]), "vx")
                v.nop().then_inc(p.sem["ve"], 1)
                p.cnt["ve"] += 1

            @b.scalar
            def _(sc):
                for dc in range(DCH):
                    sc.wait_ge(p.sem[f"L{dc % 2}"], l0[dc % 2] + 16 * (dc // 2 + 1))
                    if dc >= 2:
                        sc.wait_ge(p.sem["te"], te0 + (dc - 2) + 1)
                    sc.activation(out=sq[dc % 2][:], in_=xin[dc % 2][:], func=AF.Square) \
                        .then_inc(p.sem["se"], 1)
                p.cnt["se"] += DCH
                sc.wait_ge(p.sem["te"], te0 + DCH)
                for tc in range(NTC):
                    cs = slice(512 * tc, 512 * (tc + 1))
                    p.edge(sc, sc.copy(s1r[:, cs], ps1[tc][:]), "sx")
                    p.edge(sc, sc.copy(s2r[:, cs], ps2[tc][:]), "sx")
                p.edge(sc, sc.mul(nmr[:], s1r[:], -1.0 / D), "sx")
                p.edge(sc, sc.activation(out=m2r[:], in_=nmr[:], func=AF.Square), "sx")
                p.edge(sc, sc.mul(rr[:], s2r[:], 1.0 / D), "sx")
                sc.nop().then_inc(p.sem["se"], 1)
                p.cnt["se"] += 1
                sc.wait_ge(p.sem["ve"], ve0 + DCH + 1)
                p.edge(sc, sc.activation(out=rr[:], in_=rr[:], func=AF.Ln,
                                         bias=p.c["epscol"][0:1, 0:1]), "sx")
                p.edge(sc, sc.activation(out=rr[:], in_=rr[:], func=AF.Exp, scale=-0.5), "sx")
                sc.nop().then_inc(p.sem["se"], 1)
                p.cnt["se"] += 1

            @b.tensor
            def _(te):
                for dc in range(DCH):
                    te.wait_ge(p.sem["ve"], ve0 + dc + 1)
                    te.wait_ge(p.sem["se"], se0 + dc + 1)
                    for tc in range(NTC):
                        te.matmul(ps1[tc][:], lhsT=p.c["ones_col"][:],
                                  rhs=xb[dc % 2][:, 512 * tc:512 * (tc + 1)],
                                  start=(dc == 0), stop=(dc == DCH - 1))
                    for tc in range(NTC):
                        mm = te.matmul(ps2[tc][:], lhsT=p.c["ones_col"][:],
                                       rhs=sq[dc % 2][:, 512 * tc:512 * (tc + 1)],
                                       start=(dc == 0), stop=(dc == DCH - 1))
                    mm.then_inc(p.sem["te"], 1)
                p.cnt["te"] += DCH


def _bcast_row(p, b_te, b_sc, row_ap, out_sb, T, onef, pbank):
    raise NotImplementedError


def _p3(p, x2_s, x2pi_s, costabs, wq, wk, wv, q_sc, k_sc, vt_sc, hsel, dbg):
    nc = p.nc
    onef = p.c["onef3"]
    with ExitStack() as st0:
        nmz = st0.enter_context(nc.sbuf_tensor("nmz", [1, S], F32))
        rz = st0.enter_context(nc.sbuf_tensor("rz", [1, S], F32))

        with ExitStack() as st1:
            nm3 = st1.enter_context(nc.sbuf_tensor("nm3", [1, S], F32))
            r3 = st1.enter_context(nc.sbuf_tensor("r3", [1, S], F32))
            m2t = st1.enter_context(nc.sbuf_tensor("m2t", [1, S], F32))

            _ln_stats_rows(p, x2_s.ap()[0][0:D, :], S, nm3, r3, m2t, onef, "3p")
            _ln_stats_rows(p, x2_s.ap()[1][0:D, :], S, nmz, rz, m2t, onef, "3z")

            with MB(p) as b:
                @b.gpsimd
                def _(g):
                    x2a = x2_s.ap()
                    g.dma_start(out=x2a[0][D:D + 1, :], in_=nm3[:]).then_inc(p.sem["S2"], 16)
                    p.cnt["S2"] += 16
                    g.wait_ge(p.sem["S2"], p.cnt["S2"])
                    g.dma_start(out=x2a[0][D + 1:D + 2, :], in_=r3[:]).then_inc(p.sem["S2"], 16)
                    p.cnt["S2"] += 16
                    g.wait_ge(p.sem["S2"], p.cnt["S2"])
                    rh = g.alloc_register(name="c_rh")
                    p.roffs = []
                    for half in range(2):
                        g.reg_load(rh, p.c["hsel_sb"][0:1, half:half + 1])
                        g.reg_alu(rh, rh, 128, ALU.mult)
                        for i in range(8):
                            r = g.alloc_register(name=f"roff{half}_{i}")
                            g.reg_alu(r, rh, 256 * i, ALU.add)
                            p.roffs.append(g.snap(r, donate=True, min_val=0, max_val=S - 128))
                    g.free_register(rh)
                    for j in range(16):
                        sv = p.roffs[j]
                        src128 = bass.AP(x2a.tensor, sv, [[S, D + 2], [1, 128]])
                        g.dma_start(out=x2pi_s.ap()[:, 128 * j:128 * (j + 1)], in_=src128) \
                            .then_inc(p.sem["S3"], 16)
                        p.cnt["S3"] += 16
                    p.gather_done = p.cnt["S3"]

        _p3_zside(p, x2_s.ap()[1][0:D, :], nmz, rz, wq, wk, costabs, dbg)

    _p3_pside(p, x2pi_s, wq, wk, wv, q_sc, k_sc, vt_sc, costabs, dbg)


def _p3_zside(p, x2z_ap, nmz, rz, wq, wk, costabs, dbg):
    nc = p.nc
    onef = p.c["onef3"]
    cq_t, ck_t, ct_t, cv_t = costabs
    TW = [512, 512, NTEXT - 1024]
    TOFF = [0, 512, 1024]
    with ExitStack() as st:
        nmzb = st.enter_context(nc.sbuf_tensor("nmzb", [128, S], F32))
        rztb = st.enter_context(nc.sbuf_tensor("rztb", [128, NTEXT], BF16))
        cvr = st.enter_context(nc.sbuf_tensor("cvr", [128, NVIS], BF16))
        svr = st.enter_context(nc.sbuf_tensor("svr", [128, NVIS], BF16))
        qsum = st.enter_context(nc.sbuf_tensor("qsum", [128, DCH], F32))
        qsb = st.enter_context(nc.sbuf_tensor("qsb", [128, DCH], BF16))
        imp = st.enter_context(nc.sbuf_tensor("impr", [1, NVIS], F32))

        te0, se0, ve0 = p.cnt["te"], p.cnt["se"], p.cnt["ve"]
        gl0 = p.cnt["gl"]
        with ExitStack() as st2, MB(p) as b:
            rzvb = st2.enter_context(nc.sbuf_tensor("rzvb", [128, NVIS], F32))
            cv_sb = st2.enter_context(nc.sbuf_tensor("cv_sb", [128, 2 * NVIS], BF16))
            pbn = [st2.enter_context(nc.psum_tensor(f"zpb{i}", [128, 512], F32)) for i in range(4)]
            pbt = [st2.enter_context(nc.psum_tensor(f"zpt{i}", [128, 512], F32)) for i in range(3)]
            pbv = st2.enter_context(nc.psum_tensor("zpv", [128, 512], F32))

            @b.gpsimd
            def _(g):
                def gchain(dst_ap, src_ap):
                    g.dma_start(out=dst_ap, in_=src_ap).then_inc(p.sem["gl"], 16)
                    p.cnt["gl"] += 16
                gchain(cv_sb[:, 0:NVIS], cv_t.ap()[0])
                gchain(cv_sb[:, NVIS:2 * NVIS], cv_t.ap()[1])
                g.wait_ge(p.sem["gl"], p.cnt["gl"])

            @b.tensor
            def _(te):
                for tc in range(4):
                    te.matmul(pbn[tc][:], lhsT=onef[:], rhs=nmz[0:1, 512 * tc:512 * (tc + 1)],
                              start=True, stop=True).then_inc(p.sem["te"], 1)
                for tz in range(3):
                    te.matmul(pbt[tz][:, 0:TW[tz]], lhsT=onef[:],
                              rhs=rz[0:1, TEXT_LO + 512 * tz:TEXT_LO + 512 * tz + TW[tz]],
                              start=True, stop=True).then_inc(p.sem["te"], 1)
                te.matmul(pbv[:], lhsT=onef[:], rhs=rz[0:1, VIS_LO:VIS_HI],
                          start=True, stop=True).then_inc(p.sem["te"], 1)
                p.cnt["te"] += 8

            @b.scalar
            def _(sc):
                for tc in range(4):
                    sc.wait_ge(p.sem["te"], te0 + tc + 1)
                    sc.copy(nmzb[:, 512 * tc:512 * (tc + 1)], pbn[tc][:]).then_inc(p.sem["se"], 1)
                for tz in range(3):
                    sc.wait_ge(p.sem["te"], te0 + 4 + tz + 1)
                    sc.copy(rztb[:, 512 * tz:512 * tz + TW[tz]], pbt[tz][:, 0:TW[tz]]) \
                        .then_inc(p.sem["se"], 1)
                sc.wait_ge(p.sem["te"], te0 + 8)
                sc.copy(rzvb[:], pbv[:]).then_inc(p.sem["se"], 1)
                p.cnt["se"] += 8

            @b.vector
            def _(v):
                v.wait_ge(p.sem["se"], se0 + 8)
                v.wait_ge(p.sem["gl"], gl0 + 16 * 2)
                p.edge(v, v.tensor_mul(cvr[:], cv_sb[:, 0:NVIS], rzvb[:]), "vx")
                p.edge(v, v.tensor_mul(svr[:], cv_sb[:, NVIS:2 * NVIS], rzvb[:]), "vx")
                v.nop().then_inc(p.sem["ve"], 1)
                p.cnt["ve"] += 1

        with ExitStack() as st3:
            hsZt = st3.enter_context(nc.sbuf_tensor("hsZt", [128, DCH * NTEXT], BF16))
            l0 = [p.cnt["L0"], p.cnt["L1"]]
            ve0 = p.cnt["ve"]
            with ExitStack() as stx, MB(p) as b:
                xin = [stx.enter_context(nc.sbuf_tensor(f"zxi{i}", [128, NTEXT], F32))
                       for i in range(2)]

                @b.sync
                def _(sy):
                    for dc in range(DCH):
                        if dc >= 2:
                            sy.wait_ge(p.sem["ve"], ve0 + (dc - 2) + 1)
                        sy.dma_start(out=xin[dc % 2][:],
                                     in_=x2z_ap[128 * dc:128 * (dc + 1), TEXT_LO:TEXT_HI]) \
                            .then_inc(p.sem[f"L{dc % 2}"], 16)
                        p.cnt[f"L{dc % 2}"] += 16

                @b.vector
                def _(v):
                    for dc in range(DCH):
                        v.wait_ge(p.sem[f"L{dc % 2}"], l0[dc % 2] + 16 * (dc // 2 + 1))
                        v.tensor_add(hsZt[:, NTEXT * dc:NTEXT * (dc + 1)],
                                     xin[dc % 2][:], nmzb[:, TEXT_LO:TEXT_HI]) \
                            .then_inc(p.sem["ve"], 1)
                    p.cnt["ve"] += DCH

            gl0 = [p.cnt["L6"], p.cnt["L7"]]
            te0, se0, ve0 = p.cnt["te"], p.cnt["se"], p.cnt["ve"]
            with ExitStack() as pst, MB(p) as b:
                ct_sb = pst.enter_context(nc.sbuf_tensor("ct_sb", [128, 2 * NTEXT], BF16))
                wqb = [pst.enter_context(nc.sbuf_tensor(f"zwq{i}", [128, D], BF16))
                       for i in range(2)]
                qtx = [pst.enter_context(nc.sbuf_tensor(f"qtx{i}", [128, 512], BF16))
                       for i in range(2)]
                q1 = pst.enter_context(nc.sbuf_tensor("q1z", [128, 512], BF16))
                q2 = pst.enter_context(nc.sbuf_tensor("q2z", [128, 512], BF16))
                jk = pst.enter_context(nc.sbuf_tensor("jkz", [128, 512], BF16))
                qs3 = pst.enter_context(nc.sbuf_tensor("qs3", [128, 4], F32))
                pQ = [pst.enter_context(nc.psum_tensor(f"zpQ{i}", [128, 512], F32)) for i in range(2)]
                pR = [pst.enter_context(nc.psum_tensor(f"zpR{i}", [128, 512], F32)) for i in range(2)]

                @b.gpsimd
                def _(g):
                    def gchain(dst_ap, src_ap):
                        g.dma_start(out=dst_ap, in_=src_ap).then_inc(p.sem["gl"], 16)
                        p.cnt["gl"] += 16
                    gchain(ct_sb[:, 0:NTEXT], ct_t.ap()[0])
                    gchain(ct_sb[:, NTEXT:2 * NTEXT], ct_t.ap()[1])
                    for ob in range(DCH):
                        if ob >= 2:
                            g.wait_ge(p.sem["te"], te0 + 6 * ob - 7)
                        g.dma_start(out=wqb[ob % 2][:], in_=wq.ap()[ob]) \
                            .then_inc(p.sem[f"L{6 + ob % 2}"], 16)
                        p.cnt[f"L{6 + ob % 2}"] += 16
                    g.wait_ge(p.sem["gl"], p.cnt["gl"])

                @b.tensor
                def _(te):
                    for ob in range(DCH):
                        te.wait_ge(p.sem[f"L{6 + ob % 2}"], gl0[ob % 2] + 16 * (ob // 2 + 1))
                        for tz in range(3):
                            j = 3 * ob + tz
                            if j >= 2:
                                te.wait_ge(p.sem["se"], se0 + (j - 1))
                            for dc in range(DCH):
                                mm = te.matmul(pQ[j % 2][:, 0:TW[tz]],
                                               lhsT=wqb[ob % 2][:, 128 * dc:128 * (dc + 1)],
                                               rhs=hsZt[:, NTEXT * dc + TOFF[tz]:
                                                        NTEXT * dc + TOFF[tz] + TW[tz]],
                                               start=(dc == 0), stop=(dc == DCH - 1))
                            mm.then_inc(p.sem["te"], 1)
                            te.wait_ge(p.sem["se"], se0 + j + 1)
                            if j >= 2:
                                te.wait_ge(p.sem["ve"], ve0 + 4 * ((j - 2) // 3) + ((j - 2) % 3) + 1)
                            mm = te.matmul(pR[j % 2][:, 0:TW[tz]], lhsT=p.c["rmat_sb"][:],
                                           rhs=qtx[j % 2][:, 0:TW[tz]], start=True, stop=True)
                            mm.then_inc(p.sem["te"], 1)
                    p.cnt["te"] += 6 * DCH

                @b.scalar
                def _(sc):
                    for ob in range(DCH):
                        for tz in range(3):
                            j = 3 * ob + tz
                            sc.wait_ge(p.sem["te"], te0 + 2 * j + 1)
                            if j >= 2:
                                sc.wait_ge(p.sem["te"], te0 + 2 * (j - 2) + 2)
                                sc.wait_ge(p.sem["ve"],
                                           ve0 + 4 * ((j - 2) // 3) + ((j - 2) % 3) + 1)
                            sc.copy(qtx[j % 2][:, 0:TW[tz]], pQ[j % 2][:, 0:TW[tz]]) \
                                .then_inc(p.sem["se"], 1)
                    p.cnt["se"] += 3 * DCH

                @b.vector
                def _(v):
                    gwait = p.cnt["gl"]
                    for ob in range(DCH):
                        for tz in range(3):
                            j = 3 * ob + tz
                            w = TW[tz]
                            cs = slice(TOFF[tz], TOFF[tz] + w)
                            v.wait_ge(p.sem["se"], se0 + j + 1)
                            if j == 0:
                                v.wait_ge(p.sem["gl"], gwait)
                            p.edge(v, v.tensor_mul(q1[:, 0:w], qtx[j % 2][:, 0:w],
                                                   ct_sb[:, cs]), "vx")
                            v.wait_ge(p.sem["te"], te0 + 2 * j + 2)
                            p.edge(v, v.tensor_mul(q2[:, 0:w], pR[j % 2][:, 0:w],
                                                   ct_sb[:, NTEXT + TOFF[tz]:NTEXT + TOFF[tz] + w]),
                                   "vx")
                            p.edge(v, v.tensor_add(q1[:, 0:w], q1[:, 0:w], q2[:, 0:w]), "vx")
                            p.edge(v, v.scalar_tensor_tensor(out=jk[:, 0:w], in0=q1[:, 0:w],
                                                             scalar=1.0, in1=rztb[:, cs],
                                                             op0=ALU.mult, op1=ALU.mult,
                                                             accum_out=qs3[:, tz:tz + 1]), "vx")
                            v.nop().then_inc(p.sem["ve"], 1)
                        p.edge(v, v.tensor_add(qs3[:, 3:4], qs3[:, 0:1], qs3[:, 1:2]), "vx")
                        p.edge(v, v.tensor_add(qsum[:, ob:ob + 1], qs3[:, 3:4], qs3[:, 2:3]), "vx")
                        v.nop().then_inc(p.sem["ve"], 1)
                    p.cnt["ve"] += 4 * DCH

        with ExitStack() as st4:
            hsZv = st4.enter_context(nc.sbuf_tensor("hsZv", [128, DCH * NVIS], BF16))
            l0 = [p.cnt["L0"], p.cnt["L1"]]
            ve0 = p.cnt["ve"]
            with ExitStack() as stx, MB(p) as b:
                xin = [stx.enter_context(nc.sbuf_tensor(f"zxv{i}", [128, NVIS], F32))
                       for i in range(2)]

                @b.sync
                def _(sy):
                    for dc in range(DCH):
                        if dc >= 2:
                            sy.wait_ge(p.sem["ve"], ve0 + (dc - 2) + 1)
                        sy.dma_start(out=xin[dc % 2][:],
                                     in_=x2z_ap[128 * dc:128 * (dc + 1), VIS_LO:VIS_HI]) \
                            .then_inc(p.sem[f"L{dc % 2}"], 16)
                        p.cnt[f"L{dc % 2}"] += 16

                @b.vector
                def _(v):
                    for dc in range(DCH):
                        v.wait_ge(p.sem[f"L{dc % 2}"], l0[dc % 2] + 16 * (dc // 2 + 1))
                        v.tensor_add(hsZv[:, NVIS * dc:NVIS * (dc + 1)],
                                     xin[dc % 2][:], nmzb[:, VIS_LO:VIS_HI]) \
                            .then_inc(p.sem["ve"], 1)
                    p.cnt["ve"] += DCH

            gl0 = [p.cnt["L6"], p.cnt["L7"]]
            te0, se0, ve0 = p.cnt["te"], p.cnt["se"], p.cnt["ve"]
            with ExitStack() as pst, MB(p) as b:
                wqb = [pst.enter_context(nc.sbuf_tensor(f"zwk{i}", [128, D], BF16))
                       for i in range(2)]
                kvr = pst.enter_context(nc.sbuf_tensor("kvr", [128, NVIS], BF16))
                kvo = [pst.enter_context(nc.sbuf_tensor(f"kvo{i}", [128, NVIS], BF16))
                       for i in range(2)]
                q2v = pst.enter_context(nc.sbuf_tensor("q2v", [128, NVIS], BF16))
                pK = [pst.enter_context(nc.psum_tensor(f"zpK{i}", [128, 512], F32)) for i in range(2)]
                pR = [pst.enter_context(nc.psum_tensor(f"zpR2{i}", [128, 512], F32)) for i in range(2)]
                pI = pst.enter_context(nc.psum_tensor("zpI", [1, 512], F32))

                @b.gpsimd
                def _(g):
                    for ob in range(DCH):
                        if ob >= 2:
                            g.wait_ge(p.sem["te"], te0 + 3 * (ob - 2) + 1)
                        g.dma_start(out=wqb[ob % 2][:], in_=wk.ap()[ob]) \
                            .then_inc(p.sem[f"L{6 + ob % 2}"], 16)
                        p.cnt[f"L{6 + ob % 2}"] += 16

                @b.tensor
                def _(te):
                    for ob in range(DCH):
                        te.wait_ge(p.sem[f"L{6 + ob % 2}"], gl0[ob % 2] + 16 * (ob // 2 + 1))
                        if ob >= 2:
                            te.wait_ge(p.sem["se"], se0 + (ob - 2) + 1)
                        for dc in range(DCH):
                            mm = te.matmul(pK[ob % 2][:], lhsT=wqb[ob % 2][:, 128 * dc:128 * (dc + 1)],
                                           rhs=hsZv[:, NVIS * dc:NVIS * (dc + 1)],
                                           start=(dc == 0), stop=(dc == DCH - 1))
                        mm.then_inc(p.sem["te"], 1)
                        te.wait_ge(p.sem["se"], se0 + ob + 1)
                        if ob >= 2:
                            te.wait_ge(p.sem["ve"], ve0 + 1 + (ob - 2) + 1)
                        mm = te.matmul(pR[ob % 2][:], lhsT=p.c["rmat_sb"][:], rhs=kvr[:],
                                       start=True, stop=True)
                        mm.then_inc(p.sem["te"], 1)
                        te.wait_ge(p.sem["ve"], ve0 + 1 + ob + 1)
                        mm = te.matmul(pI[:], lhsT=qsb[:, ob:ob + 1], rhs=kvo[ob % 2][:],
                                       start=(ob == 0), stop=(ob == DCH - 1))
                        mm.then_inc(p.sem["te"], 1)
                    p.cnt["te"] += 3 * DCH

                @b.scalar
                def _(sc):
                    for ob in range(DCH):
                        sc.wait_ge(p.sem["te"], te0 + 3 * ob + 1)
                        if ob >= 1:
                            sc.wait_ge(p.sem["ve"], ve0 + 1 + (ob - 1) + 1)
                        sc.copy(kvr[:], pK[ob % 2][:]).then_inc(p.sem["se"], 1)
                    sc.wait_ge(p.sem["te"], te0 + 3 * DCH)
                    sc.copy(imp[:], pI[:]).then_inc(p.sem["se"], 1)
                    p.cnt["se"] += DCH + 1

                @b.vector
                def _(v):
                    v.tensor_copy(qsb[:], qsum[:]).then_inc(p.sem["ve"], 1)
                    for ob in range(DCH):
                        v.wait_ge(p.sem["se"], se0 + ob + 1)
                        if ob >= 2:
                            v.wait_ge(p.sem["te"], te0 + 3 * (ob - 2) + 3)
                        p.edge(v, v.tensor_mul(kvo[ob % 2][:], kvr[:], cvr[:]), "vx")
                        v.wait_ge(p.sem["te"], te0 + 3 * ob + 2)
                        p.edge(v, v.tensor_mul(q2v[:], pR[ob % 2][:], svr[:]), "vx")
                        p.edge(v, v.tensor_add(kvo[ob % 2][:], kvo[ob % 2][:], q2v[:]), "vx")
                        v.nop().then_inc(p.sem["ve"], 1)
                    p.cnt["ve"] += DCH + 1

        te0, se0, ve0 = p.cnt["te"], p.cnt["se"], p.cnt["ve"]
        gq0 = p.cnt["gq"]
        with ExitStack() as pst, MB(p) as b:
            impb = pst.enter_context(nc.sbuf_tensor("impb", [128, NVIS], F32))
            impc = pst.enter_context(nc.sbuf_tensor("impc", [128, 4], F32))
            jkf = pst.enter_context(nc.sbuf_tensor("jkf", [128, NVIS], F32))
            cg = pst.enter_context(nc.sbuf_tensor("cgz", [128, 4], F32))
            ce = pst.enter_context(nc.sbuf_tensor("cez", [128, 4], F32))
            kbcol = pst.enter_context(nc.sbuf_tensor("kbcolz", [128, 4], F32))
            kbg = pst.enter_context(nc.sbuf_tensor("kbgz", [1, NVIS], F32))
            kbrow16 = pst.enter_context(nc.sbuf_tensor("kbrow16", [16, 128], BF16))
            pB = pst.enter_context(nc.psum_tensor("zpB", [128, 512], F32))
            pC = pst.enter_context(nc.psum_tensor("zpC", [128, 4], F32))
            pRo = pst.enter_context(nc.psum_tensor("zpRo", [1, 512], F32))
            pkc = pst.enter_context(nc.psum_tensor("zpkc", [128, 16], F32))

            @b.tensor
            def _(te):
                te.matmul(pB[:], lhsT=onef[:], rhs=imp[:], start=True, stop=True) \
                    .then_inc(p.sem["te"], 1)
                for c4 in range(4):
                    mm = te.matmul(pC[:, c4:c4 + 1], lhsT=imp[0:1, 128 * c4:128 * (c4 + 1)],
                                   rhs=onef[0:1, 0:1], start=True, stop=True)
                mm.then_inc(p.sem["te"], 1)
                te.wait_ge(p.sem["ve"], ve0 + 1)
                for c4 in range(4):
                    mm = te.matmul(pRo[:, 128 * c4:128 * (c4 + 1)], lhsT=kbcol[:, c4:c4 + 1],
                                   rhs=p.c["id_f"][:], start=True, stop=True)
                mm.then_inc(p.sem["te"], 1)
                te.wait_ge(p.sem["gq"], gq0 + 2)
                te.matmul(pkc[:], lhsT=kbrow16[:], rhs=p.c["pperm_sb"][:],
                          start=True, stop=True).then_inc(p.sem["te"], 1)
                p.cnt["te"] += 4

            @b.scalar
            def _(sc):
                sc.wait_ge(p.sem["te"], te0 + 1)
                sc.copy(impb[:], pB[:]).then_inc(p.sem["se"], 1)
                sc.wait_ge(p.sem["te"], te0 + 2)
                sc.copy(impc[:], pC[:]).then_inc(p.sem["se"], 1)
                sc.wait_ge(p.sem["te"], te0 + 3)
                sc.copy(kbg[:], pRo[:]).then_inc(p.sem["se"], 1)
                sc.wait_ge(p.sem["te"], te0 + 4)
                sc.copy(p.c["kbc_sb"][:], pkc[:]).then_inc(p.sem["se"], 1)
                p.cnt["se"] += 4

            @b.vector
            def _(v):
                v.wait_ge(p.sem["se"], se0 + 2)
                for c4 in range(4):
                    p.edge(v, v.tensor_scalar(out=jkf[:], in0=impb[:],
                                              scalar1=impc[:, c4:c4 + 1], scalar2=0.0,
                                              op0=ALU.is_gt, op1=ALU.add,
                                              accum_out=cg[:, c4:c4 + 1]), "vx")
                    p.edge(v, v.scalar_tensor_tensor(out=jkf[:], in0=impb[:],
                                                     scalar=impc[:, c4:c4 + 1],
                                                     in1=p.c["ltm_sb"][:, NVIS * c4:NVIS * (c4 + 1)],
                                                     op0=ALU.is_equal, op1=ALU.mult,
                                                     accum_out=ce[:, c4:c4 + 1]), "vx")
                p.edge(v, v.tensor_add(cg[:], cg[:], ce[:]), "vx")
                p.edge(v, v.tensor_scalar(out=cg[:], in0=cg[:], scalar1=float(KEEP) - 0.5,
                                          scalar2=None, op0=ALU.is_lt), "vx")
                p.edge(v, v.tensor_scalar(out=kbcol[:], in0=cg[:], scalar1=1.0, scalar2=-NEG,
                                          op0=ALU.subtract, op1=ALU.mult), "vx")
                v.nop().then_inc(p.sem["ve"], 1)
                p.cnt["ve"] += 1

            @b.gpsimd
            def _(g):
                g.memset(kbrow16[:], 0.0).then_inc(p.sem["gq"], 1)
                p.cnt["gq"] += 1
                g.wait_ge(p.sem["gq"], p.cnt["gq"])
                g.wait_ge(p.sem["se"], se0 + 3)
                for (dst, srcs) in (
                        (kbrow16[0:1, 1:128], kbg[0:1, 0:127]),
                        (kbrow16[1:2, 0:128], kbg[0:1, 127:255]),
                        (kbrow16[2:3, 0:128], kbg[0:1, 255:383]),
                        (kbrow16[3:4, 0:128], kbg[0:1, 383:511]),
                        (kbrow16[4:5, 0:1], kbg[0:1, 511:512])):
                    g.dma_start(out=dst, in_=srcs).then_inc(p.sem["S2"], 16)
                    p.cnt["S2"] += 16
                    g.wait_ge(p.sem["S2"], p.cnt["S2"])
                g.nop().then_inc(p.sem["gq"], 1)
                p.cnt["gq"] += 1
                if dbg.get("dbg_imp") is not None:
                    g.dma_start(out=dbg["dbg_imp"].ap(), in_=imp[:]).then_inc(p.sem["S2"], 16)
                    p.cnt["S2"] += 16
                    g.wait_ge(p.sem["S2"], p.cnt["S2"])
                if dbg.get("dbg_kb") is not None:
                    g.wait_ge(p.sem["se"], se0 + 4)
                    g.dma_start(out=dbg["dbg_kb"].ap(), in_=p.c["kbc_sb"][:]) \
                        .then_inc(p.sem["S2"], 16)
                    p.cnt["S2"] += 16
                    g.wait_ge(p.sem["S2"], p.cnt["S2"])
                if dbg.get("dbg_qt") is not None:
                    g.dma_start(out=dbg["dbg_qt"].ap(), in_=qsum[:]).then_inc(p.sem["S2"], 16)
                    p.cnt["S2"] += 16
                    g.wait_ge(p.sem["S2"], p.cnt["S2"])


def _p3_pside(p, x2pi_s, wq, wk, wv, q_sc, k_sc, vt_sc, costabs, dbg):
    nc = p.nc
    onef = p.c["onef3"]
    cq_t, ck_t, ct_t, cv_t = costabs
    x2pi_ap = x2pi_s.ap()
    with ExitStack() as st:
        rpb = st.enter_context(nc.sbuf_tensor("rpb", [128, S], F32))
        ck_sb = st.enter_context(nc.sbuf_tensor("ck_sb", [128, 2 * S], BF16))
        cq_sb = st.enter_context(nc.sbuf_tensor("cq_sb", [128, 2 * TOWN], BF16))
        hs3 = st.enter_context(nc.sbuf_tensor("hs3", [128, DCH * S], BF16))

        with ExitStack() as stn:
            nmpb = stn.enter_context(nc.sbuf_tensor("nmpb", [128, S], F32))

            te0, se0 = p.cnt["te"], p.cnt["se"]
            gl0 = p.cnt["gl"]
            with ExitStack() as st1, MB(p) as b:
                nm3pi = st1.enter_context(nc.sbuf_tensor("nm3pi", [1, S], F32))
                r3pi = st1.enter_context(nc.sbuf_tensor("r3pi", [1, S], F32))
                pbn = [st1.enter_context(nc.psum_tensor(f"ppb{i}", [128, 512], F32))
                       for i in range(4)]
                pbr = [st1.enter_context(nc.psum_tensor(f"ppr{i}", [128, 512], F32))
                       for i in range(4)]

                @b.gpsimd
                def _(g):
                    g.wait_ge(p.sem["S3"], p.gather_done)
                    def gchain(dst_ap, src_ap):
                        g.dma_start(out=dst_ap, in_=src_ap).then_inc(p.sem["gl"], 16)
                        p.cnt["gl"] += 16
                    gchain(nm3pi[:], x2pi_ap[D:D + 1, :])
                    gchain(r3pi[:], x2pi_ap[D + 1:D + 2, :])
                    gchain(ck_sb[:, 0:S], ck_t.ap()[0])
                    gchain(ck_sb[:, S:2 * S], ck_t.ap()[1])
                    gchain(cq_sb[:, 0:TOWN], cq_t.ap()[0])
                    gchain(cq_sb[:, TOWN:2 * TOWN], cq_t.ap()[1])
                    g.wait_ge(p.sem["gl"], p.cnt["gl"])

                @b.tensor
                def _(te):
                    te.wait_ge(p.sem["gl"], gl0 + 96)
                    for tc in range(4):
                        te.matmul(pbn[tc][:], lhsT=onef[:], rhs=nm3pi[0:1, 512 * tc:512 * (tc + 1)],
                                  start=True, stop=True).then_inc(p.sem["te"], 1)
                        te.matmul(pbr[tc][:], lhsT=onef[:], rhs=r3pi[0:1, 512 * tc:512 * (tc + 1)],
                                  start=True, stop=True).then_inc(p.sem["te"], 1)
                    p.cnt["te"] += 8

                @b.scalar
                def _(sc):
                    for tc in range(4):
                        cs = slice(512 * tc, 512 * (tc + 1))
                        sc.wait_ge(p.sem["te"], te0 + 2 * tc + 1)
                        sc.copy(nmpb[:, cs], pbn[tc][:]).then_inc(p.sem["se"], 1)
                        sc.wait_ge(p.sem["te"], te0 + 2 * tc + 2)
                        sc.copy(rpb[:, cs], pbr[tc][:]).then_inc(p.sem["se"], 1)
                    p.cnt["se"] += 8

            with ExitStack() as st2, MB(p) as b:
                rpbb = st2.enter_context(nc.sbuf_tensor("rpbb", [128, S], BF16))

                @b.vector
                def _(v):
                    p.edge(v, v.tensor_copy(rpbb[:], rpb[:]), "vx")
                    for tc in range(4):
                        cs = slice(512 * tc, 512 * (tc + 1))
                        cs2 = slice(S + 512 * tc, S + 512 * (tc + 1))
                        p.edge(v, v.tensor_mul(ck_sb[:, cs], ck_sb[:, cs], rpbb[:, cs]), "vx")
                        p.edge(v, v.tensor_mul(ck_sb[:, cs2], ck_sb[:, cs2], rpbb[:, cs]), "vx")
                    for tc in range(2):
                        cs = slice(512 * tc, 512 * (tc + 1))
                        cs2 = slice(TOWN + 512 * tc, TOWN + 512 * (tc + 1))
                        p.edge(v, v.tensor_mul(cq_sb[:, cs], cq_sb[:, cs], rpbb[:, cs]), "vx")
                        p.edge(v, v.tensor_mul(cq_sb[:, cs2], cq_sb[:, cs2], rpbb[:, cs]), "vx")
                    v.nop().then_inc(p.sem["ve"], 1)
                    p.cnt["ve"] += 1

            l0 = [p.cnt["L0"], p.cnt["L1"]]
            ve0 = p.cnt["ve"]
            with ExitStack() as st3, MB(p) as b:
                xin = [st3.enter_context(nc.sbuf_tensor(f"pxi{i}", [128, S], F32))
                       for i in range(2)]

                @b.sync
                def _(sy):
                    sy.wait_ge(p.sem["S3"], p.gather_done)
                    for dc in range(DCH):
                        if dc >= 2:
                            sy.wait_ge(p.sem["ve"], ve0 + (dc - 2) + 1)
                        sy.dma_start(out=xin[dc % 2][:], in_=x2pi_ap[128 * dc:128 * (dc + 1), :]) \
                            .then_inc(p.sem[f"L{dc % 2}"], 16)
                        p.cnt[f"L{dc % 2}"] += 16

                @b.vector
                def _(v):
                    for dc in range(DCH):
                        v.wait_ge(p.sem[f"L{dc % 2}"], l0[dc % 2] + 16 * (dc // 2 + 1))
                        v.tensor_add(hs3[:, S * dc:S * (dc + 1)], xin[dc % 2][:], nmpb[:]) \
                            .then_inc(p.sem["ve"], 1)
                    p.cnt["ve"] += DCH

        with ExitStack() as st4:
            wwb = [st4.enter_context(nc.sbuf_tensor(f"pww{i}", [128, D], BF16)) for i in range(2)]
            with ExitStack() as st5:
                ktmp = [st5.enter_context(nc.sbuf_tensor(f"pkt{i}", [128, 512], BF16))
                        for i in range(2)]
                e2b = st5.enter_context(nc.sbuf_tensor("pe2", [128, 512], BF16))
                kto = [st5.enter_context(nc.sbuf_tensor(f"pko{i}", [128, 512], BF16))
                       for i in range(2)]
                _proj_rope(p, wk, hs3, ck_sb, S, k_sc, S, "k",
                           kbuf=wwb, ktmp=ktmp, e2b=e2b, kto=kto)
                _proj_rope(p, wq, hs3, cq_sb, TOWN, q_sc, TOWN, "q",
                           kbuf=wwb, ktmp=ktmp, e2b=e2b, kto=kto)
            with ExitStack() as st6:
                vto = st6.enter_context(nc.sbuf_tensor("pvto", [128, 512], BF16))
                vtt = st6.enter_context(nc.sbuf_tensor("pvtt", [128, S], BF16))
                _proj_v(p, wv, hs3, rpb, vt_sc, wwb, vto, vtt)


def _proj_rope(p, wmat, hs3, tabs, sinoff, dst_sc, T, tag, kbuf, ktmp, e2b, kto):
    nc = p.nc
    NTC = T // 512
    gl0 = [p.cnt["L6"], p.cnt["L7"]]
    te0, se0, ve0 = p.cnt["te"], p.cnt["se"], p.cnt["ve"]
    s0 = [p.cnt["S0"], p.cnt["S1"]]
    with ExitStack() as pst, MB(p) as b:
        pK = [pst.enter_context(nc.psum_tensor(f"{tag}pK{i}", [128, 512], F32)) for i in range(2)]
        pR = [pst.enter_context(nc.psum_tensor(f"{tag}pR{i}", [128, 512], F32)) for i in range(2)]

        @b.gpsimd
        def _(g):
            def stores(ob):
                for tc in range(NTC):
                    j = NTC * ob + tc
                    g.wait_ge(p.sem["ve"], ve0 + j + 1)
                    g.dma_start(out=dst_sc.ap()[128 * ob:128 * (ob + 1), 512 * tc:512 * (tc + 1)],
                                in_=kto[j % 2][:]).then_inc(p.sem[f"S{j % 2}"], 16)
                    p.cnt[f"S{j % 2}"] += 16
            for ob in range(DCH):
                if ob >= 2:
                    g.wait_ge(p.sem["te"], te0 + 2 * NTC * (ob - 1))
                g.dma_start(out=kbuf[ob % 2][:], in_=wmat.ap()[ob]) \
                    .then_inc(p.sem[f"L{6 + ob % 2}"], 16)
                p.cnt[f"L{6 + ob % 2}"] += 16
                if ob >= 1:
                    stores(ob - 1)
            stores(DCH - 1)
            g.wait_ge(p.sem["S0"], p.cnt["S0"])
            g.wait_ge(p.sem["S1"], p.cnt["S1"])

        @b.tensor
        def _(te):
            for ob in range(DCH):
                te.wait_ge(p.sem[f"L{6 + ob % 2}"], gl0[ob % 2] + 16 * (ob // 2 + 1))
                for tc in range(NTC):
                    j = NTC * ob + tc
                    if j >= 2:
                        te.wait_ge(p.sem["se"], se0 + (j - 1))
                    for dc in range(DCH):
                        mm = te.matmul(pK[j % 2][:],
                                       lhsT=kbuf[ob % 2][:, 128 * dc:128 * (dc + 1)],
                                       rhs=hs3[:, S * dc + 512 * tc:S * dc + 512 * (tc + 1)],
                                       start=(dc == 0), stop=(dc == DCH - 1))
                    mm.then_inc(p.sem["te"], 1)
                    te.wait_ge(p.sem["se"], se0 + j + 1)
                    if j >= 2:
                        te.wait_ge(p.sem["ve"], ve0 + (j - 2) + 1)
                    mm = te.matmul(pR[j % 2][:], lhsT=p.c["rmat_sb"][:],
                                   rhs=ktmp[j % 2][:], start=True, stop=True)
                    mm.then_inc(p.sem["te"], 1)
            p.cnt["te"] += 2 * NTC * DCH

        @b.scalar
        def _(sc):
            for ob in range(DCH):
                for tc in range(NTC):
                    j = NTC * ob + tc
                    sc.wait_ge(p.sem["te"], te0 + 2 * j + 1)
                    if j >= 2:
                        sc.wait_ge(p.sem["te"], te0 + 2 * j)
                        sc.wait_ge(p.sem["ve"], ve0 + (j - 2) + 1)
                    sc.copy(ktmp[j % 2][:], pK[j % 2][:]).then_inc(p.sem["se"], 1)
            p.cnt["se"] += NTC * DCH

        @b.vector
        def _(v):
            for ob in range(DCH):
                for tc in range(NTC):
                    j = NTC * ob + tc
                    cs = slice(512 * tc, 512 * (tc + 1))
                    cs2 = slice(sinoff + 512 * tc, sinoff + 512 * (tc + 1))
                    v.wait_ge(p.sem["se"], se0 + j + 1)
                    if j >= 2:
                        v.wait_ge(p.sem[f"S{j % 2}"], s0[j % 2] + 16 * ((j - 2) // 2 + 1))
                    p.edge(v, v.tensor_mul(kto[j % 2][:], ktmp[j % 2][:], tabs[:, cs]), "vx")
                    v.wait_ge(p.sem["te"], te0 + 2 * j + 2)
                    p.edge(v, v.tensor_mul(e2b[:], pR[j % 2][:], tabs[:, cs2]), "vx")
                    p.edge(v, v.tensor_add(kto[j % 2][:], kto[j % 2][:], e2b[:]), "vx")
                    v.nop().then_inc(p.sem["ve"], 1)
            p.cnt["ve"] += NTC * DCH


def _proj_v(p, wv, hs3, rpb, vt_sc, kbuf, vto, vtt):
    nc = p.nc
    NTC = S // 512
    gl0 = [p.cnt["L6"], p.cnt["L7"]]
    te0, se0, ve0 = p.cnt["te"], p.cnt["se"], p.cnt["ve"]
    s0 = p.cnt["S0"]
    with ExitStack() as pst, MB(p) as b:
        pV = [pst.enter_context(nc.psum_tensor(f"vpV{i}", [128, 512], F32)) for i in range(2)]
        pT = [pst.enter_context(nc.psum_tensor(f"vpT{i}", [128, 128], BF16)) for i in range(2)]

        @b.gpsimd
        def _(g):
            def store(ob):
                g.wait_ge(p.sem["se"], se0 + 16 * (ob + 1))
                g.dma_start(out=vt_sc.ap()[ob], in_=vtt[:]).then_inc(p.sem["S0"], 16)
                p.cnt["S0"] += 16
                g.wait_ge(p.sem["S0"], p.cnt["S0"])
            for ob in range(DCH):
                if ob >= 2:
                    g.wait_ge(p.sem["te"], te0 + 20 * (ob - 1))
                g.dma_start(out=kbuf[ob % 2][:], in_=wv.ap()[ob]) \
                    .then_inc(p.sem[f"L{6 + ob % 2}"], 16)
                p.cnt[f"L{6 + ob % 2}"] += 16
                if ob >= 1:
                    store(ob - 1)
            store(DCH - 1)

        @b.tensor
        def _(te):
            for ob in range(DCH):
                te.wait_ge(p.sem[f"L{6 + ob % 2}"], gl0[ob % 2] + 16 * (ob // 2 + 1))
                for tc in range(NTC):
                    j = NTC * ob + tc
                    if j >= 2:
                        te.wait_ge(p.sem["ve"], ve0 + (j - 2) + 1)
                    for dc in range(DCH):
                        mm = te.matmul(pV[j % 2][:],
                                       lhsT=kbuf[ob % 2][:, 128 * dc:128 * (dc + 1)],
                                       rhs=hs3[:, S * dc + 512 * tc:S * dc + 512 * (tc + 1)],
                                       start=(dc == 0), stop=(dc == DCH - 1))
                    mm.then_inc(p.sem["te"], 1)
                    te.wait_ge(p.sem["ve"], ve0 + j + 1)
                    for ts4 in range(4):
                        jj = 4 * j + ts4
                        if jj >= 2:
                            te.wait_ge(p.sem["se"], se0 + (jj - 1))
                        te.transpose(pT[jj % 2][:], vto[:, 128 * ts4:128 * (ts4 + 1)], p.c["id_b"][:]) \
                            .then_inc(p.sem["te"], 1)
            p.cnt["te"] += (NTC * 5) * DCH

        @b.vector
        def _(v):
            for ob in range(DCH):
                for tc in range(NTC):
                    j = NTC * ob + tc
                    v.wait_ge(p.sem["te"], te0 + 20 * ob + 5 * tc + 1)
                    if j >= 1:
                        v.wait_ge(p.sem["se"], se0 + 16 * ob + 4 * tc)
                    v.tensor_mul(vto[:], pV[j % 2][:], rpb[:, 512 * tc:512 * (tc + 1)]) \
                        .then_inc(p.sem["ve"], 1)
            p.cnt["ve"] += NTC * DCH

        @b.scalar
        def _(sc):
            for ob in range(DCH):
                if ob >= 1:
                    sc.wait_ge(p.sem["S0"], s0 + 16 * ob)
                for tc in range(NTC):
                    j = NTC * ob + tc
                    for ts4 in range(4):
                        jj = 4 * j + ts4
                        sc.wait_ge(p.sem["te"], te0 + 20 * ob + 5 * tc + 1 + ts4 + 1)
                        sc.copy(vtt[:, 128 * (4 * tc + ts4):128 * (4 * tc + ts4 + 1)], pT[jj % 2][:]) \
                            .then_inc(p.sem["se"], 1)
            p.cnt["se"] += 4 * NTC * DCH




def _p4_attn(p, q_sc, k_sc, vt_sc, attn_sb):
    nc = p.nc
    chunks = []
    for hd in range(NH):
        for qc in range(4):
            jl = list(range(0, 2 * qc + 2)) + list(range(8, 8 + 2 * qc + 2))
            chunks.append({"hd": hd, "qc": qc, "jl": jl, "n": len(jl)})

    te0, se0, ve0 = p.cnt["te"], p.cnt["se"], p.cnt["ve"]
    lk = [p.cnt["L0"], p.cnt["L1"]]
    lq = p.cnt["L4"]
    lv = [p.cnt["S0"], p.cnt["S1"]]
    te, se, ve = te0, se0, ve0
    gk = 0
    gv = 0
    gs = 0
    seE_g = {}
    tePV_g = {}
    for t, r in enumerate(chunks):
        n = r["n"]
        r["t"] = t
        lq += 16
        r["lq"] = lq
        r["lk"] = []
        r["lv"] = []
        for i in range(n // 2):
            lk[gk % 2] += 16
            r["lk"].append((gk % 2, lk[gk % 2]))
            gk += 1
        for i in range(n // 2):
            lv[gv % 2] += 16
            r["lv"].append((gv % 2, lv[gv % 2]))
            gv += 1
        r["gs0"] = gs
        r["teS"] = []
        for i in range(n):
            te += 1
            r["teS"].append(te)
        r["seE"] = []
        for i in range(n):
            se += 1
            r["seE"].append(se)
            seE_g[gs + i] = se
        r["veM"] = {}
        for j in r["jl"]:
            if j in (2 * r["qc"], 2 * r["qc"] + 1, 8 + 2 * r["qc"], 9 + 2 * r["qc"]):
                ve += 1
                r["veM"][j] = ve
        r["tePV"] = []
        for i in range(n):
            te += 1
            r["tePV"].append(te)
            tePV_g[gs + i] = te
        gs += n
        se += 1
        r["seD"] = se
        te += 1
        r["teB"] = te
        se += 1
        r["seB"] = se
        ve += 1
        r["veF"] = ve
    p.cnt["te"] = te
    p.cnt["se"] = se
    p.cnt["ve"] = ve
    for s, v in (("L0", lk[0]), ("L1", lk[1]), ("L4", lq), ("S0", lv[0]), ("S1", lv[1])):
        p.cnt[s] = v

    qa, ka, va = q_sc.ap(), k_sc.ap(), vt_sc.ap()
    with ExitStack() as st, MB(p) as b:
        qtl = st.enter_context(nc.sbuf_tensor("a4q", [128, 256], BF16))
        ktl = [st.enter_context(nc.sbuf_tensor(f"a4k{i}", [128, 256], BF16)) for i in range(2)]
        vtl = [st.enter_context(nc.sbuf_tensor(f"a4v{i}", [128, 256], BF16)) for i in range(2)]
        ptl = st.enter_context(nc.sbuf_tensor("a4p", [128, 16 * 256], BF16))
        dnr = st.enter_context(nc.sbuf_tensor("a4d", [1, 256], F32))
        denb = [st.enter_context(nc.sbuf_tensor(f"a4db{i}", [128, 256], F32)) for i in range(2)]
        pS = [st.enter_context(nc.psum_tensor(f"a4pS{i}", [128, 512], F32)) for i in range(2)]
        pD = [st.enter_context(nc.psum_tensor(f"a4pD{i}", [1, 512], F32)) for i in range(2)]
        pO = [st.enter_context(nc.psum_tensor(f"a4pO{i}", [128, 512], F32)) for i in range(2)]
        pB = [st.enter_context(nc.psum_tensor(f"a4pB{i}", [128, 512], F32)) for i in range(2)]

        @b.sync
        def _(sy):
            for r in chunks:
                t, hd, qc, jl, n = r["t"], r["hd"], r["qc"], r["jl"], r["n"]
                if t >= 1:
                    sy.wait_ge(p.sem["te"], chunks[t - 1]["teS"][-1])
                sy.dma_start(out=qtl[:],
                             in_=qa[128 * hd:128 * (hd + 1), 256 * qc:256 * (qc + 1)]) \
                    .then_inc(p.sem["L4"], 16)
                for pl in range(n // 2):
                    bk, val = r["lk"][pl]
                    gki = r["gs0"] // 2 + pl
                    if gki >= 2:
                        sy.wait_ge(p.sem["te"], _teS_abs(chunks, 2 * (gki - 2) + 1))
                    j0 = jl[2 * pl]
                    sy.dma_start(out=ktl[bk][:],
                                 in_=ka[128 * hd:128 * (hd + 1), 128 * j0:128 * j0 + 256]) \
                        .then_inc(p.sem[f"L{bk}"], 16)

        @b.gpsimd
        def _(g):
            for r in chunks:
                t, hd, qc, jl, n = r["t"], r["hd"], r["qc"], r["jl"], r["n"]
                for pl in range(n // 2):
                    bk, val = r["lv"][pl]
                    gvi = r["gs0"] // 2 + pl
                    if gvi >= 2:
                        g.wait_ge(p.sem["te"], tePV_g[2 * (gvi - 2) + 1])
                    j0 = jl[2 * pl]
                    g.dma_start(out=vtl[bk][:],
                                in_=va[hd][:, 128 * j0:128 * j0 + 256]) \
                        .then_inc(p.sem[f"S{bk}"], 16)

        @b.tensor
        def _(te_):
            for r in chunks:
                t, hd, qc, jl, n = r["t"], r["hd"], r["qc"], r["jl"], r["n"]
                gs0 = r["gs0"]
                for i in range(n):
                    if i == 0:
                        te_.wait_ge(p.sem["L4"], r["lq"])
                    bk, val = r["lk"][i // 2]
                    te_.wait_ge(p.sem[f"L{bk}"], val)
                    gsi = gs0 + i
                    if gsi >= 2:
                        te_.wait_ge(p.sem["se"], seE_g[gsi - 2])
                    te_.matmul(pS[gsi % 2][:, 0:256],
                               lhsT=ktl[(gsi // 2) % 2][:, 128 * (i % 2):128 * (i % 2) + 128],
                               rhs=qtl[:], start=True, stop=True).then_inc(p.sem["te"], 1)
                for i in range(n):
                    j = jl[i]
                    if j in r["veM"]:
                        te_.wait_ge(p.sem["ve"], r["veM"][j])
                    else:
                        te_.wait_ge(p.sem["se"], r["seE"][i])
                    if i == 0 and t >= 2:
                        te_.wait_ge(p.sem["se"], chunks[t - 2]["seD"])
                        te_.wait_ge(p.sem["ve"], chunks[t - 2]["veF"])
                    te_.matmul(pD[t % 2][0:1, 0:256], lhsT=p.c["ones_col"][:],
                               rhs=ptl[:, 256 * i:256 * (i + 1)],
                               start=(i == 0), stop=(i == n - 1))
                    bk, val = r["lv"][i // 2]
                    te_.wait_ge(p.sem[f"S{bk}"], val)
                    te_.matmul(pO[t % 2][:, 0:256],
                               lhsT=vtl[(r["gs0"] // 2 + i // 2) % 2][:, 128 * (i % 2):128 * (i % 2) + 128],
                               rhs=ptl[:, 256 * i:256 * (i + 1)],
                               start=(i == 0), stop=(i == n - 1)).then_inc(p.sem["te"], 1)
                te_.wait_ge(p.sem["se"], r["seD"])
                if t >= 2:
                    te_.wait_ge(p.sem["se"], chunks[t - 2]["seB"])
                te_.matmul(pB[t % 2][:, 0:256], lhsT=p.c["onef3"][:], rhs=dnr[:],
                           start=True, stop=True).then_inc(p.sem["te"], 1)

        @b.scalar
        def _(sc):
            for r in chunks:
                t, qc, jl, n = r["t"], r["qc"], r["jl"], r["n"]
                for i in range(n):
                    j = jl[i]
                    sc.wait_ge(p.sem["te"], r["teS"][i])
                    sc.activation(out=ptl[:, 256 * i:256 * (i + 1)],
                                  in_=pS[(r["gs0"] + i) % 2][:, 0:256], func=AF.Exp,
                                  scale=ATT_SCALE, bias=p.c["kbc_sb"][:, j:j + 1]) \
                        .then_inc(p.sem["se"], 1)
                sc.wait_ge(p.sem["te"], r["tePV"][-1])
                if t >= 1:
                    sc.wait_ge(p.sem["te"], chunks[t - 1]["teB"])
                p.edge(sc, sc.copy(dnr[:], pD[t % 2][0:1, 0:256]), "sx")
                p.edge(sc, sc.activation(out=dnr[:], in_=dnr[:], func=AF.Ln), "sx")
                sc.activation(out=dnr[:], in_=dnr[:], func=AF.Exp, scale=-1.0) \
                    .then_inc(p.sem["se"], 1)
                sc.wait_ge(p.sem["te"], r["teB"])
                if t >= 2:
                    sc.wait_ge(p.sem["ve"], chunks[t - 2]["veF"])
                sc.copy(denb[t % 2][:], pB[t % 2][:, 0:256]).then_inc(p.sem["se"], 1)

        @b.vector
        def _(v):
            tri = p.c["tri_sb"]
            mp = p.c["mpar_sb"]
            for r in chunks:
                t, hd, qc, jl, n = r["t"], r["hd"], r["qc"], r["jl"], r["n"]
                for i in range(n):
                    j = jl[i]
                    if j not in r["veM"]:
                        continue
                    base = 256 * i
                    h1 = slice(base, base + 128)
                    h2 = slice(base + 128, base + 256)
                    v.wait_ge(p.sem["se"], r["seE"][i])
                    if j == 2 * qc:
                        p.edge(v, v.tensor_mul(ptl[:, h1], ptl[:, h1], tri[:]), "vx")
                    elif j == 2 * qc + 1:
                        p.edge(v, v.memset(ptl[:, h1], 0.0), "vx")
                        p.edge(v, v.tensor_mul(ptl[:, h2], ptl[:, h2], tri[:]), "vx")
                    elif j == 8 + 2 * qc:
                        p.edge(v, v.tensor_mul(ptl[:, h1], ptl[:, h1], mp[:, 0:128]), "vx")
                    else:
                        p.edge(v, v.memset(ptl[:, h1], 0.0), "vx")
                        p.edge(v, v.tensor_mul(ptl[:, h2], ptl[:, h2], mp[:, 0:128]), "vx")
                    v.nop().then_inc(p.sem["ve"], 1)
                v.wait_ge(p.sem["se"], r["seB"])
                v.tensor_mul(attn_sb[:, hd * TOWN + 256 * qc:hd * TOWN + 256 * (qc + 1)],
                             pO[t % 2][:, 0:256], denb[t % 2][:]).then_inc(p.sem["ve"], 1)


def _teS_abs(chunks, gsi):
    for r in chunks:
        if r["gs0"] <= gsi < r["gs0"] + r["n"]:
            return r["teS"][gsi - r["gs0"]] - 0
    raise KeyError(gsi)



def _p5_oproj(p, wo, attn_sb, x2pi_s, x4_s):
    nc = p.nc
    gl0 = [p.cnt["L6"], p.cnt["L7"]]
    l0 = [p.cnt["L0"], p.cnt["L1"]]
    s0 = [p.cnt["S0"], p.cnt["S1"]]
    te0, ve0 = p.cnt["te"], p.cnt["ve"]
    x2a = x2pi_s.ap()
    with ExitStack() as st, MB(p) as b:
        owb = [st.enter_context(nc.sbuf_tensor(f"o5w{i}", [128, D], BF16)) for i in range(2)]
        xres = [st.enter_context(nc.sbuf_tensor(f"o5x{i}", [128, 512], F32)) for i in range(2)]
        x4o = [st.enter_context(nc.sbuf_tensor(f"o5o{i}", [128, 512], F32)) for i in range(2)]
        pOut = [st.enter_context(nc.psum_tensor(f"o5p{i}", [128, 512], F32)) for i in range(2)]

        @b.gpsimd
        def _(g):
            def stores(jj):
                ob, tc = divmod(jj, 2)
                g.wait_ge(p.sem["ve"], ve0 + jj + 1)
                g.dma_start(out=x4_s.ap()[128 * ob:128 * (ob + 1), 512 * tc:512 * (tc + 1)],
                            in_=x4o[jj % 2][:]).then_inc(p.sem[f"S{jj % 2}"], 16)
                p.cnt[f"S{jj % 2}"] += 16
            for ob in range(DCH):
                if ob >= 2:
                    g.wait_ge(p.sem["te"], te0 + 2 * ob - 2)
                g.dma_start(out=owb[ob % 2][:], in_=wo.ap()[ob]) \
                    .then_inc(p.sem[f"L{6 + ob % 2}"], 16)
                p.cnt[f"L{6 + ob % 2}"] += 16
                if ob >= 1:
                    stores(2 * (ob - 1))
                    stores(2 * (ob - 1) + 1)
            stores(2 * (DCH - 1))
            stores(2 * (DCH - 1) + 1)
            g.wait_ge(p.sem["S0"], p.cnt["S0"])
            g.wait_ge(p.sem["S1"], p.cnt["S1"])

        @b.sync
        def _(sy):
            for jj in range(2 * DCH):
                ob, tc = divmod(jj, 2)
                if jj >= 2:
                    sy.wait_ge(p.sem["ve"], ve0 + (jj - 2) + 1)
                sy.dma_start(out=xres[jj % 2][:],
                             in_=x2a[128 * ob:128 * (ob + 1), 512 * tc:512 * (tc + 1)]) \
                    .then_inc(p.sem[f"L{jj % 2}"], 16)
                p.cnt[f"L{jj % 2}"] += 16

        @b.tensor
        def _(te):
            for jj in range(2 * DCH):
                ob, tc = divmod(jj, 2)
                if tc == 0:
                    te.wait_ge(p.sem[f"L{6 + ob % 2}"], gl0[ob % 2] + 16 * (ob // 2 + 1))
                if jj >= 2:
                    te.wait_ge(p.sem["ve"], ve0 + (jj - 2) + 1)
                for dc in range(DCH):
                    mm = te.matmul(pOut[jj % 2][:],
                                   lhsT=owb[ob % 2][:, 128 * dc:128 * (dc + 1)],
                                   rhs=attn_sb[:, dc * TOWN + 512 * tc:dc * TOWN + 512 * (tc + 1)],
                                   start=(dc == 0), stop=(dc == DCH - 1))
                mm.then_inc(p.sem["te"], 1)
            p.cnt["te"] += 2 * DCH

        @b.vector
        def _(v):
            for jj in range(2 * DCH):
                v.wait_ge(p.sem["te"], te0 + jj + 1)
                v.wait_ge(p.sem[f"L{jj % 2}"], l0[jj % 2] + 16 * (jj // 2 + 1))
                if jj >= 2:
                    v.wait_ge(p.sem[f"S{jj % 2}"], s0[jj % 2] + 16 * ((jj - 2) // 2 + 1))
                v.scalar_tensor_tensor(out=x4o[jj % 2][:], in0=pOut[jj % 2][:], scalar=1.0,
                                       in1=xres[jj % 2][:], op0=ALU.mult, op1=ALU.add) \
                    .then_inc(p.sem["ve"], 1)
            p.cnt["ve"] += 2 * DCH




def build(debug=(), phases=99):
    p = Prog(debug=debug)
    nc = p.nc

    xnat5 = nc.dram_tensor("xnat5", [5, S, D], F32, kind="ExternalInput")
    xcv2 = nc.dram_tensor("xcv2", [2, D, S], F32, kind="ExternalInput")
    wq = nc.dram_tensor("wq", [DCH, 128, D], F32, kind="ExternalInput")
    wk = nc.dram_tensor("wk", [DCH, 128, D], F32, kind="ExternalInput")
    wv = nc.dram_tensor("wv", [DCH, 128, D], F32, kind="ExternalInput")
    wo = nc.dram_tensor("wo", [DCH, 128, D], F32, kind="ExternalInput")
    wa = nc.dram_tensor("wa", [128, D], F32, kind="ExternalInput")
    wb = nc.dram_tensor("wb", [HR, D], F32, kind="ExternalInput")
    wr = nc.dram_tensor("wr", [128, NEXP * DCH], F32, kind="ExternalInput")
    brv = nc.dram_tensor("brv", [NEXP, 1], F32, kind="ExternalInput")
    cq = nc.dram_tensor("cq", [2, 128, TOWN], F32, kind="ExternalInput")
    ck = nc.dram_tensor("ck", [2, 128, S], F32, kind="ExternalInput")
    ct = nc.dram_tensor("ct", [2, 128, NTEXT], F32, kind="ExternalInput")
    cv = nc.dram_tensor("cv", [2, 128, NVIS], F32, kind="ExternalInput")
    rmat = nc.dram_tensor("rmat", [128, 128], F32, kind="ExternalInput")
    mpar = nc.dram_tensor("mpar", [128, 256], F32, kind="ExternalInput")
    ltm = nc.dram_tensor("ltm", [128, 4 * NVIS], F32, kind="ExternalInput")
    hsel = nc.dram_tensor("hsel", [1, 2], mybir.dt.uint32, kind="ExternalInput")
    pperm = nc.dram_tensor("pperm", [16, 16], BF16, kind="ExternalInput")

    outv = nc.dram_tensor("outv", [D, TOWN], F32, kind="ExternalOutput")

    lnnat_s = nc.dram_tensor("lnnat_s", [5, S, D], BF16)
    x1_s = nc.dram_tensor("x1_s", [2, D, S], F32)
    hs2_s = nc.dram_tensor("hs2_s", [2, D, S], BF16)
    x2_s = nc.dram_tensor("x2_s", [2, D + 2, S], F32)
    x2pi_s = nc.dram_tensor("x2pi_s", [D + 2, S], F32)
    q_sc = nc.dram_tensor("q_sc", [D, TOWN], BF16)
    k_sc = nc.dram_tensor("k_sc", [D, S], BF16)
    vt_sc = nc.dram_tensor("vt_sc", [NH, 128, (S // 128) * DH], BF16)
    x4_s = nc.dram_tensor("x4_s", [D, TOWN], F32)
    hs4_s = nc.dram_tensor("hs4_s", [D, TOWN], BF16)

    p.sbuf("wa_sb", [128, D], BF16)
    p.sbuf("wb_sb", [128, D], BF16)
    p.sbuf("wr_sb", [128, NEXP * DCH], BF16)
    p.sbuf("br_sb", [NEXP, 1], F32)
    p.sbuf("e8_sb", [NEXP, HR], BF16)
    p.sbuf("i8_sb", [NEXP, NEXP], F32)
    p.sbuf("id_f", [128, 128], F32)
    p.sbuf("id_b", [128, 128], BF16)
    p.sbuf("ones_col", [128, 1], BF16)
    p.sbuf("ones_row", [1, 128], BF16)
    p.sbuf("one1", [1, 1], BF16)
    p.sbuf("rmat_sb", [128, 128], BF16)
    p.sbuf("mpar_sb", [128, 256], BF16)
    p.sbuf("tri_sb", [128, 128], BF16)
    p.sbuf("ltm_sb", [128, 4 * NVIS], F32)
    p.sbuf("epscol", [128, 1], F32)
    p.sbuf("onef3", [1, 128], F32)
    p.sbuf("hsel_sb", [1, 2], mybir.dt.uint32)
    p.sbuf("kbc_sb", [128, 16], F32)
    p.sbuf("pperm_sb", [16, 16], BF16)

    for s in ("ld", "gl", "te", "ve", "se", "stq", "gq", "sx", "vx",
              "L0", "L1", "L2", "L3", "L4", "L5", "L6", "L7",
              "S0", "S1", "S2", "S3"):
        p.semaphore(s)

    dbg_x1 = p.dbg("dbg_x1", [2 * D, S], F32)
    dbg_x2 = p.dbg("dbg_x2", [2 * D, S], F32)
    dbg3 = {
        "dbg_qt": p.dbg("dbg_qt", [128, DCH], F32),
        "dbg_imp": p.dbg("dbg_imp", [1, NVIS], F32),
        "dbg_kb": p.dbg("dbg_kb", [128, 16], F32),
    }
    dbg_q = p.dbg("dbg_q", [D, TOWN], BF16)
    dbg_k = p.dbg("dbg_k", [D, S], BF16)
    dbg_vt = p.dbg("dbg_vt", [NH * 128, (S // 128) * DH], BF16)
    dbg_prob = p.dbg("dbg_prob", [2, 128, 128], F32)
    dbg_at = p.dbg("dbg_at", [128, NH * TOWN], BF16)
    dbg_x4 = p.dbg("dbg_x4", [D, TOWN], F32)
    dbg_ln = p.dbg("dbg_ln", [5 * S, D], BF16)

    with MB(p) as b:
        @b.sync
        def _(sy):
            def chain(dst_ap, src_ap):
                sy.dma_start(out=dst_ap, in_=src_ap).then_inc(p.sem["ld"], 16)
                p.cnt["ld"] += 16
            chain(p.c["br_sb"][:], brv.ap())
            chain(p.c["ltm_sb"][:], ltm.ap())
            chain(p.c["hsel_sb"][:], hsel.ap())
            sy.wait_ge(p.sem["ld"], p.cnt["ld"])

        @b.gpsimd
        def _(g):
            def gchain(dst_ap, src_ap):
                g.dma_start(out=dst_ap, in_=src_ap).then_inc(p.sem["gl"], 16)
                p.cnt["gl"] += 16
            gchain(p.c["wa_sb"][:], wa.ap())
            gchain(p.c["wb_sb"][:], wb.ap())
            gchain(p.c["wr_sb"][:], wr.ap())
            gchain(p.c["rmat_sb"][:], rmat.ap())
            gchain(p.c["mpar_sb"][:], mpar.ap())
            gchain(p.c["pperm_sb"][:], pperm.ap())
            g.wait_ge(p.sem["gl"], p.cnt["gl"])
            id_f, tri, e8, i8 = p.c["id_f"], p.c["tri_sb"], p.c["e8_sb"], p.c["i8_sb"]
            def step(instr):
                p.cnt["gq"] += 1
                instr.then_inc(p.sem["gq"], 1)
                g.wait_ge(p.sem["gq"], p.cnt["gq"])
            step(g.memset(id_f[:], 1.0))
            step(g.affine_select(out=id_f[:], in_=id_f[:], compare_op=ALU.is_ge, fill=0.0,
                                 base=0, pattern=[[-1, 128]], channel_multiplier=1))
            step(g.affine_select(out=id_f[:], in_=id_f[:], compare_op=ALU.is_ge, fill=0.0,
                                 base=0, pattern=[[1, 128]], channel_multiplier=-1))
            step(g.memset(tri[:], 1.0))
            step(g.affine_select(out=tri[:], in_=tri[:], compare_op=ALU.is_ge, fill=0.0,
                                 base=0, pattern=[[1, 128]], channel_multiplier=-1))
            step(g.memset(e8[:], 1.0))
            step(g.affine_select(out=e8[:], in_=e8[:], compare_op=ALU.is_ge, fill=0.0,
                                 base=0, pattern=[[1, HR]], channel_multiplier=-16))
            step(g.affine_select(out=e8[:], in_=e8[:], compare_op=ALU.is_ge, fill=0.0,
                                 base=15, pattern=[[-1, HR]], channel_multiplier=16))
            step(g.memset(i8[:], 1.0))
            step(g.affine_select(out=i8[:], in_=i8[:], compare_op=ALU.is_ge, fill=0.0,
                                 base=0, pattern=[[-1, NEXP]], channel_multiplier=1))
            step(g.affine_select(out=i8[:], in_=i8[:], compare_op=ALU.is_ge, fill=0.0,
                                 base=0, pattern=[[1, NEXP]], channel_multiplier=-1))
            step(g.memset(p.c["epscol"][:], EPS))
            step(g.memset(p.c["onef3"][:], 1.0))
            step(g.memset(p.c["ones_col"][:], 1.0))
            step(g.memset(p.c["ones_row"][:], 1.0))
            step(g.memset(p.c["one1"][:], 1.0))

        @b.vector
        def _(v):
            v.wait_ge(p.sem["gq"], p.cnt["gq"])
            v.tensor_copy(p.c["id_b"][:], p.c["id_f"][:]).then_inc(p.sem["ve"], 1)
            p.cnt["ve"] += 1

    if phases >= 1:
        _p1_lnnat(p, xnat5, lnnat_s)
        if dbg_ln is not None:
            _copy_dram(p, dbg_ln.ap(),
                       lnnat_s.ap().rearrange("a b c -> (a b) c"))
    if phases >= 2:
        _p1_gram_msg(p, lnnat_s, xcv2, x1_s, dbg_prob)
        if dbg_x1 is not None:
            _copy_dram(p, dbg_x1.ap(), x1_s.ap().rearrange("a b c -> (a b) c"))

    if phases >= 3:
        for u in range(2):
            _ffn(p, x1_s.ap()[u], x2_s.ap()[u], hs2_s.ap()[u], S, f"2{u}")
        if dbg_x2 is not None:
            for u in range(2):
                _copy_dram(p, dbg_x2.ap()[u * D:(u + 1) * D, :], x2_s.ap()[u][0:D, :])

    if phases >= 4:
        _p3(p, x2_s, x2pi_s, (cq, ck, ct, cv), wq, wk, wv, q_sc, k_sc, vt_sc, hsel, dbg3)
        if dbg_q is not None:
            _copy_dram(p, dbg_q.ap(), q_sc.ap())
        if dbg_k is not None:
            _copy_dram(p, dbg_k.ap(), k_sc.ap())
        if dbg_vt is not None:
            _copy_dram(p, dbg_vt.ap(), vt_sc.ap().rearrange("a b c -> (a b) c"))

    if phases >= 5:
        with ExitStack() as stA:
            attn_sb = stA.enter_context(nc.sbuf_tensor("attn_sb", [128, NH * TOWN], BF16))
            _p4_attn(p, q_sc, k_sc, vt_sc, attn_sb)
            if dbg_at is not None:
                with MB(p) as b:
                    @b.gpsimd
                    def _(g):
                        g.dma_start(out=dbg_at.ap(), in_=attn_sb[:]).then_inc(p.sem["S2"], 16)
                        p.cnt["S2"] += 16
                        g.wait_ge(p.sem["S2"], p.cnt["S2"])
            if phases >= 6:
                _p5_oproj(p, wo, attn_sb, x2pi_s, x4_s)
        if phases >= 6 and dbg_x4 is not None:
            _copy_dram(p, dbg_x4.ap(), x4_s.ap())

    if phases >= 7:
        _ffn(p, x4_s.ap(), outv.ap(), hs4_s.ap(), TOWN, "4")

    if phases < 7:
        with MB(p) as b:
            @b.gpsimd
            def _(g):
                g.memset(p.c["id_f"][:, 0:1], 0.0)
        with MB(p) as b:
            @b.sync
            def _(sy):
                for i in range(DCH):
                    sy.dma_start(out=outv.ap()[128 * i:128 * (i + 1), 0:128],
                                 in_=p.c["id_f"][:]).then_inc(p.sem["stq"], 16)
                p.cnt["stq"] += 16 * DCH
                sy.wait_ge(p.sem["stq"], p.cnt["stq"])

    p.st.close()
    return p



def _stage_inputs(L0_attn_a_x, L0_attn_b_q_w, L0_attn_b_k_w, L0_attn_b_v_w,
                  L0_attn_b_o_w, L0_ffn_A, L0_ffn_B, L0_ffn_Wr, L0_ffn_br):
    x = np.asarray(L0_attn_a_x, dtype=np.float32)
    wq_p = _prep_wT(L0_attn_b_q_w)
    wk_p = _prep_wT(L0_attn_b_k_w)
    wv_p = _prep_wT(L0_attn_b_v_w)
    wo_p = _prep_wT(L0_attn_b_o_w)
    A = np.asarray(L0_ffn_A, dtype=np.float32)
    wa_p = A.reshape(HR, D).T.reshape(DCH, 128, HR)
    wa_p = np.ascontiguousarray(wa_p.transpose(1, 0, 2)).reshape(128, D)
    Bm = np.asarray(L0_ffn_B, dtype=np.float32)
    wb_p = np.ascontiguousarray(Bm.transpose(0, 2, 1).reshape(HR, D))
    Wr = np.asarray(L0_ffn_Wr, dtype=np.float32)
    wr_p = Wr.T.reshape(DCH, 128, NEXP)
    wr_p = np.ascontiguousarray(wr_p.transpose(1, 0, 2)).reshape(128, NEXP * DCH)
    br_p = np.asarray(L0_ffn_br, dtype=np.float32).reshape(NEXP, 1)

    rmat_p = np.zeros((128, 128), dtype=np.float32)
    for d in range(64):
        rmat_p[64 + d, d] = -1.0
        rmat_p[d, 64 + d] = 1.0

    ct_c, ct_s = _rope_cos_sin(np.arange(TEXT_LO, TEXT_HI))
    cv_c, cv_s = _rope_cos_sin(np.arange(VIS_LO, VIS_HI))
    ltm_p = np.zeros((4, 128, NVIS), dtype=np.float32)
    for ch in range(4):
        for pp in range(128):
            v = ch * 128 + pp
            ltm_p[ch, pp, :v] = 1.0
    ltm_p = np.ascontiguousarray(ltm_p.transpose(1, 0, 2)).reshape(128, 4 * NVIS)

    base = {
        "wq": wq_p, "wk": wk_p, "wv": wv_p, "wo": wo_p,
        "wa": wa_p, "wb": wb_p, "wr": wr_p, "brv": br_p,
        "ct": np.stack([ct_c, ct_s]), "cv": np.stack([cv_c, cv_s]),
        "rmat": rmat_p, "ltm": ltm_p,
    }

    in_maps = []
    for c in range(NCORE):
        P = c // 2
        h = c % 2
        perm = _pi_perm(h)
        xn5 = np.stack([x[P], x[0], x[1], x[2], x[3]])
        xcv = np.stack([np.ascontiguousarray(x[P].T),
                        np.ascontiguousarray(x[0].T)])
        cq_c, cq_s = _rope_cos_sin(perm[:TOWN])
        ck_c, ck_s = _rope_cos_sin(perm)
        mpar_p = np.full((128, 256), float(h), dtype=np.float32)
        pperm_p = np.zeros((16, 16), dtype=ml_dtypes.bfloat16)
        for j in range(16):
            pperm_p[2 * (j % 8) + (h if j < 8 else 1 - h), j] = 1.0
        m = dict(base)
        m.update({
            "xnat5": np.ascontiguousarray(xn5),
            "xcv2": xcv,
            "cq": np.stack([cq_c, cq_s]),
            "ck": np.stack([ck_c, ck_s]),
            "mpar": mpar_p,
            "hsel": np.array([[h, 1 - h]], dtype=np.uint32),
            "pperm": pperm_p,
        })
        in_maps.append(m)
    return in_maps


def kernel(L0_attn_a_x, L0_attn_b_q_w, L0_attn_b_k_w, L0_attn_b_v_w,
           L0_attn_b_o_w, L0_ffn_A, L0_ffn_B, L0_ffn_Wr, L0_ffn_br,
           _debug=(), _phases=99, _return_raw=False):
    in_maps = _stage_inputs(L0_attn_a_x, L0_attn_b_q_w, L0_attn_b_k_w, L0_attn_b_v_w,
                            L0_attn_b_o_w, L0_ffn_A, L0_ffn_B, L0_ffn_Wr, L0_ffn_br)
    p = build(debug=_debug, phases=_phases)
    res = run_bass_kernel_spmd(p.nc, in_maps, core_ids=list(range(NCORE)))
    if _return_raw:
        return res
    y = np.empty((B, S, D), dtype=np.float32)
    for c in range(NCORE):
        P = c // 2
        h = c % 2
        perm = _pi_perm(h)[:TOWN]
        y[P, perm, :] = res.results[c]["outv"].T
    return y
